# version 01
# baseline (speedup 1.0000x reference)
import contextlib
import numpy as np
import ml_dtypes
import concourse.bass as bass
import concourse.mybir as mybir
from concourse.bass_utils import run_bass_kernel_spmd

F32 = mybir.dt.float32
BF16 = mybir.dt.bfloat16
I8 = mybir.dt.int8
AF = mybir.ActivationFunctionType
ALU = mybir.AluOpType
AX = mybir.AxisListType

P = 128
D = 2048
DFF = 8192
T = 1024
NCH = D // P
FCH = DFF // P
NCORES = 8
EPS = 1e-5
OQKV = 3 * D // P
NSUP = 32
OSUP = DFF // NSUP
TC = 256
NTC = T // TC
INVD = 1.0 / D
CDOWN = float(1.0 / (np.sqrt(2.0) * 64.0 * 128.0 * 128.0))

ENGINES = ("sp", "act", "dve", "pool", "pe")
SAME_ENGINE_EDGES = True


def o_kind(k):
    return k


class Sched:

    def __init__(self, serialize=False):
        self.frozen = False
        self.ops = []
        self.last_w = {}
        self.readers = {}
        self.serialize = serialize
        self.stream_last = {}

    def op(self, stream, kind, emit_fn, reads=(), writes=()):
        if self.frozen:
            return -1
        oid = len(self.ops)
        deps = set()
        if self.serialize:
            for s, last in self.stream_last.items():
                if s != stream:
                    deps.add(last)
        for t in reads:
            if t in self.last_w:
                deps.add(self.last_w[t])
        for t in writes:
            if t in self.last_w:
                deps.add(self.last_w[t])
            for r in self.readers.get(t, ()):
                deps.add(r)
        if not SAME_ENGINE_EDGES:
            deps = {dd for dd in deps if self.ops[dd][0] != stream}
        else:
            deps = {dd for dd in deps
                    if self.ops[dd][0] != stream or self.ops[dd][1] != o_kind(kind)
                    or True}
        self.ops.append([stream, kind, emit_fn, deps, list(reads), list(writes)])
        for t in reads:
            self.readers.setdefault(t, []).append(oid)
        for t in writes:
            self.last_w[t] = oid
            self.readers[t] = []
        self.stream_last[stream] = oid
        return oid

    def c(self, stream, emit_fn, reads=(), writes=()):
        return self.op(stream, "c", emit_fn, list(reads), list(writes))

    def d(self, stream, emit_fn, reads=(), writes=()):
        return self.op(stream, "d", emit_fn, list(reads), list(writes))

    def fence(self, reads, writes):
        self.c("dve", lambda e: e.engine_nop(), reads=reads, writes=writes)

    NSLOT = 8

    def lower(self):
        n = len(self.ops)
        marked = [False] * n
        for o in self.ops:
            for dd in o[3]:
                marked[dd] = True
        cnt = {}
        comp = [0] * n
        self.slot = [None] * n
        self.use = [0] * n
        dma_idx = {}
        for i, o in enumerate(self.ops):
            if o[1] == "c":
                key = (o[0], "c")
                if marked[i]:
                    cnt[key] = cnt.get(key, 0) + 1
                comp[i] = cnt.get(key, 0)
            elif o[1] == "d":
                k = dma_idx.get(o[0], 0)
                dma_idx[o[0]] = k + 1
                self.slot[i] = k % self.NSLOT
                self.use[i] = k // self.NSLOT
            else:
                k = dma_idx.get("cc", 0)
                dma_idx["cc"] = k + 1
                self.slot[i] = "cc"
                self.use[i] = k
        waits = [[] for _ in range(n)]
        clock = {}
        for i, o in enumerate(self.ops):
            need = {}
            for dd in sorted(o[3]):
                dop = self.ops[dd]
                if dop[1] == "c":
                    key = (dop[0], "c")
                    need[key] = max(need.get(key, 0), comp[dd])
                elif dop[1] == "d":
                    key = (dop[0], "s", self.slot[dd])
                    need[key] = max(need.get(key, 0), 16 * (self.use[dd] + 1))
                else:
                    key = ("cc",)
                    need[key] = max(need.get(key, 0), self.use[dd] + 1)
            for key, v in sorted(need.items(), key=str):
                ck = (o[0], key)
                if clock.get(ck, 0) >= v:
                    continue
                clock[ck] = v
                waits[i].append((key, v))
        return marked, waits

    def emit(self, block, sems):
        marked, waits = self.lower()
        per_stream = {}
        for i, o in enumerate(self.ops):
            per_stream.setdefault(o[0], []).append(i)
        builders = {"sp": block.sync, "act": block.scalar, "dve": block.vector,
                    "pool": block.gpsimd, "pe": block.tensor}
        for stream, ids in per_stream.items():
            def body(eng, ids=ids, stream=stream):
                for i in ids:
                    o = self.ops[i]
                    if o[1] == "d":
                        skey = (stream, "s", self.slot[i])
                        if self.use[i] > 0:
                            eng.wait_ge(sems[skey], 16 * self.use[i])
                    for (key, v) in waits[i]:
                        eng.wait_ge(sems[key], v)
                    ins = o[2](eng)
                    if o[1] == "c":
                        if marked[i]:
                            ins.then_inc(sems[(stream, "c")], 1)
                    elif o[1] == "d":
                        ins.then_inc(sems[(stream, "s", self.slot[i])], 16)
                    else:
                        ins.then_inc(sems[("cc",)], 1)
            builders[stream](body)


class _StopBuild(Exception):
    pass


def build_program(dbg=False, serialize=False):
    import os as _os0
    KSTOP = int(_os0.environ.get("KSTOP", "99"))
    NOMODF = bool(_os0.environ.get("KNOMODF"))
    NBUF = int(_os0.environ.get("KNBUF", "2"))
    Sched.NSLOT = int(_os0.environ.get("KNSLOT", "8"))
    nc = bass.Bass("TRN2", target_bir_lowering=False)

    def ein(name, shape, dt=F32):
        return nc.dram_tensor(name, shape, dt, kind="ExternalInput").ap()

    xT = ein("xT", [NCH, P, T])
    w_in = ein("w_in", [OQKV, NCH, P, P])
    b_in_c = ein("b_in_c", [P, OQKV])
    w_out = ein("w_out", [NCH, NCH, P, P])
    b_out_c = ein("b_out_c", [P, NCH])
    gam_c = ein("gam_c", [P, NCH])
    bet_c = ein("bet_c", [P, NCH])
    wg_t = ein("wg_t", [NSUP, NCH, P, OSUP])
    wu_t = ein("wu_t", [NSUP, NCH, P, OSUP])
    wd_t = ein("wd_t", [NCH, 2, 32, P, P])
    wmod_sh = ein("wmod_sh", [12, NCH, P, P])
    cT = ein("cT", [NCH, P, 2])
    costab = ein("costab", [8, P, T], BF16)
    sintab = ein("sintab", [8, P, T], BF16)
    h128 = ein("h128", [P, P], BF16)
    h64 = ein("h64", [64, 64], BF16)
    ident = ein("ident", [P, P])
    onescol = ein("onescol", [P, 1], BF16)
    bchot = ein("bchot", [8, 8, P], BF16)
    constv = ein("constv", [P, 4])
    bmask = ein("bmask", [P, 2])
    amask = ein("amask", [P, 8])
    msh = ein("msh", [3, NCH, P, T])

    outT = nc.dram_tensor("outT", [NCH, P, T], F32, kind="ExternalOutput").ap()
    dbg_o = {}
    dbg_tags = []
    if dbg:
        for nm, shp in [("d_h", [NCH, P, T]), ("d_xn", [NCH, P, T]),
                        ("d_kv", [NCH, P, T]), ("d_acol", [P, 16]),
                        ("d_x1", [NCH, P, T]), ("d_q1", [NCH, P, T]),
                        ("d_mod", [P, 96]), ("d_mrow", [1, 16]),
                        ("d_q2", [P, 4 * T]), ("d_inter", [P, T]),
                        ("d_m1c", [P, 16]), ("d_s1b", [P, T]),
                        ("d_modsh", [P, 24]), ("d_silu", [P, NCH * 2]),
                        ("d_agp", [P, 20]), ("d_ct", [P, NCH * 2]),
                        ("d_siluf", [P, NCH * 2]), ("d_wmb", [P, NCH * P])]:
            dbg_o[nm] = nc.dram_tensor(nm, shp, F32, kind="ExternalOutput").ap()
            dbg_tags.append(nm)

    qd = nc.dram_tensor("qd", [NCH, P, T], BF16).ap()
    x1d = nc.dram_tensor("x1d", [NCH, P, T], F32).ap()
    pdt = nc.dram_tensor("pdt", [P, FCH, NTC, TC], BF16).ap()
    rowd = nc.dram_tensor("rowd", [16, T], F32).ap()
    ag1i = nc.dram_tensor("ag1i", [P, 24], F32).ap()
    ag1o = nc.dram_tensor("ag1o", [NCORES * P, 24], F32).ap()
    ag2i = nc.dram_tensor("ag2i", [P, 20], F32).ap()
    ag2o = nc.dram_tensor("ag2o", [NCORES * P, 20], F32).ap()
    agfi = nc.dram_tensor("agfi", [P, 4], F32).ap()
    agf1o = nc.dram_tensor("agf1o", [NCORES * P, 4], F32).ap()
    agf2o = nc.dram_tensor("agf2o", [NCORES * P, 4], F32).ap()

    S = Sched(serialize=serialize)

    with contextlib.ExitStack() as ST:
        sems = {}
        for e in ENGINES:
            sems[(e, "c")] = ST.enter_context(nc.semaphore(f"s_{e}_c"))
        for e in ("sp", "pool", "act"):
            for sl in range(Sched.NSLOT):
                sems[(e, "s", sl)] = ST.enter_context(nc.semaphore(f"s_{e}_{sl}"))
        sems[("cc",)] = ST.enter_context(nc.semaphore("s_cc"))
        block = ST.enter_context(nc.Block())

        def sb(stack, name, shape, dt=F32):
            return stack.enter_context(nc.sbuf_tensor(name, shape, dt))

        t_binc = sb(ST, "t_binc", [P, OQKV])
        t_boutc = sb(ST, "t_boutc", [P, NCH])
        t_gam = sb(ST, "t_gam", [P, NCH])
        t_bet = sb(ST, "t_bet", [P, NCH])
        t_h128 = sb(ST, "t_h128", [P, P], BF16)
        t_h64 = sb(ST, "t_h64", [64, 64], BF16)
        t_ident = sb(ST, "t_ident", [P, P])
        t_ones = sb(ST, "t_ones", [P, 1], BF16)
        t_bchot = sb(ST, "t_bchot", [8, 8 * P], BF16)
        t_const = sb(ST, "t_const", [P, 4])
        t_bmask = sb(ST, "t_bmask", [P, 2])
        t_amask = sb(ST, "t_amask", [P, 8])
        t_mod = sb(ST, "t_mod", [P, 96])
        t_a1 = sb(ST, "t_a1", [P, NCH])
        t_a3 = sb(ST, "t_a3", [P, NCH])
        t_acol = sb(ST, "t_acol", [P, NCH])
        t_agp = sb(ST, "t_agp", [P, 20])
        t_mrow = sb(ST, "t_mrow", [1, 16])
        t_scol = sb(ST, "t_scol", [P, 8])
        t_colst = sb(ST, "t_colst", [P, 16])
        t_bchh = sb(ST, "t_bchh", [8, P], BF16)
        t_bchl = sb(ST, "t_bchl", [8, P], BF16)
        t_gmx = sb(ST, "t_gmx", [P, 176])
        t_rs = sb(ST, "t_rs", [P, T], F32)
        t_wb = sb(ST, "t_wb", [P, T])
        t_mb = sb(ST, "t_mb", [P, T])
        t_rb = sb(ST, "t_rb", [P, T])
        t_s1b = sb(ST, "t_s1b", [P, T])
        q1i = sb(ST, "q1i", [P, T], I8)

        nc.const_aps.register(F32, EPS, t_const[:, 0:1])
        nc.const_aps.register(F32, 1.0, t_const[:, 1:2])
        nc.const_aps.register(F32, 0.0, t_const[:, 2:3])

        ps_a = ST.enter_context(nc.psum_tensor("ps_a", [P, T], F32))
        ps_b = ST.enter_context(nc.psum_tensor("ps_b", [P, T], F32))
        ps_p = ST.enter_context(nc.psum_tensor("ps_p", [P, T], F32))
        ps_tr = ST.enter_context(nc.psum_tensor("ps_tr", [P, P], F32))

        def dma(stream, dst, src, reads, writes):
            S.d(stream, lambda e, dst=dst, src=src: e.dma_start(out=dst, in_=src),
                reads=list(reads), writes=list(writes))

        t_rs2 = sb(ST, "t_rs2", [P, T], F32)
        t_min = sb(ST, "t_min", [P, T], F32)
        t_xbf = sb(ST, "t_xbf", [P, T], BF16)
        t_sqs = sb(ST, "t_sqs", [P, T], BF16)
        ystack = contextlib.ExitStack()
        xnb = sb(ystack, "xnb", [P, NCH * T], BF16)
        xnv = xnb[:].rearrange("p (c t) -> p c t", t=T)
        xstack = contextlib.ExitStack()
        xbuf = sb(xstack, "xbuf", [P, NCH * T], F32)
        xv = xbuf[:].rearrange("p (c t) -> p c t", t=T)

        dma("sp", xbuf[:].rearrange("p (c t) -> p c t", t=T),
            xT.rearrange("c p t -> p c t"), [], ["x"])
        for src, dst, tag in [(b_in_c, t_binc, "binc"), (b_out_c, t_boutc, "boutc"),
                              (gam_c, t_gam, "gam"), (bet_c, t_bet, "bet"),
                              (ident, t_ident, "ident"), (constv, t_const, "const"),
                              (bmask, t_bmask, "bmask"), (amask, t_amask, "amask")]:
            dma("sp", dst[:], src, [], [tag])
        for src, dst, tag in [(h128, t_h128, "h128"), (h64, t_h64, "h64"),
                              (onescol, t_ones, "ones")]:
            dma("pool", dst[:], src, [], [tag])
        dma("pool", t_bchot[:].rearrange("k (j p) -> k j p", p=P),
            bchot.rearrange("j k p -> k j p"), [], ["bchot"])

        mstack = contextlib.ExitStack()
        t_ct = sb(mstack, "t_ct", [P, NCH * 2])
        t_sbf = sb(mstack, "t_sbf", [P, NCH * 2], BF16)
        t_slb = sb(mstack, "t_slb", [P, NCH * 2], BF16)
        t_siluf = sb(mstack, "t_siluf", [P, NCH * 2])
        t_slo = sb(mstack, "t_slo", [P, NCH * 2])
        wmb = [sb(mstack, f"wmb{i}", [P, NCH * P], BF16) for i in range(NBUF)]
        wmf = sb(mstack, "wmf", [P, NCH * P], F32)
        wml = sb(mstack, "wml", [P, NCH * P], BF16)
        t_modsh = sb(mstack, "t_modsh", [P, 24])

        dma("sp", t_ct[:].rearrange("p (c b) -> p c b", b=2),
            cT.rearrange("c p b -> p c b"), [], ["ct"])
        S.c("act", lambda e: e.activation(out=t_siluf[:], in_=t_ct[:], func=AF.Sigmoid),
            reads=["ct", "const"], writes=["siluf"])
        S.c("dve", lambda e: e.tensor_tensor(out=t_siluf[:], in0=t_siluf[:],
                                             in1=t_ct[:], op=ALU.mult),
            reads=["siluf", "ct"], writes=["siluf"])
        S.c("dve", lambda e: e.tensor_copy(out=t_sbf[:], in_=t_siluf[:]),
            reads=["siluf"], writes=["sbf"])
        S.c("dve", lambda e: e.tensor_tensor(out=t_slo[:], in0=t_siluf[:],
                                             in1=t_sbf[:], op=ALU.subtract),
            reads=["siluf", "sbf"], writes=["slo"])
        S.c("dve", lambda e: e.tensor_copy(out=t_slb[:], in_=t_slo[:]),
            reads=["slo"], writes=["slb"])

        MODF = () if NOMODF else tuple(range(12))
        for i in range(12):
            bi = i % NBUF
            dma("pool", wmb[bi][:].rearrange("p (c m) -> p c m", m=P),
                wmod_sh[i].rearrange("c p m -> p c m"), [], [f"wmb{bi}"])
            extra = []
            if i in MODF:
                dma("pool", wmf[:].rearrange("p (c m) -> p c m", m=P),
                    wmod_sh[i].rearrange("c p m -> p c m"), [], ["wmf"])
                S.c("dve", lambda e, bi=bi: e.tensor_tensor(
                    out=wml[:], in0=wmf[:], in1=wmb[bi][:], op=ALU.subtract),
                    reads=["wmf", f"wmb{bi}"], writes=["wml"])
                extra = ["wml"]

            def mod_mm(e, i=i, bi=bi):
                for c in range(NCH):
                    e.matmul(ps_tr[:, 0:2], wmb[bi][:, c * P:(c + 1) * P],
                             t_sbf[:, 2 * c:2 * c + 2], start=(c == 0),
                             stop=False)
                    ins = e.matmul(ps_tr[:, 2:4], wmb[bi][:, c * P:(c + 1) * P],
                                   t_slb[:, 2 * c:2 * c + 2], start=False,
                                   stop=False)
                if i in MODF:
                    for c in range(NCH):
                        ins = e.matmul(ps_tr[:, 4:6], wml[:, c * P:(c + 1) * P],
                                       t_sbf[:, 2 * c:2 * c + 2], start=False,
                                       stop=(c == NCH - 1))
                else:
                    ins = e.matmul(ps_tr[:, 2:4], wmb[bi][:, 0:P],
                                   t_slb[:, 0:2], start=False, stop=True) if False else ins
                return ins
            S.c("pe", mod_mm, reads=[f"wmb{bi}", "sbf", "slb"] + extra, writes=["ps_tr"])

            S.c("dve", lambda e: e.tensor_copy(out=t_rs[:, 0:6], in_=ps_tr[:, 0:6]),
                reads=["ps_tr"], writes=["t_rs", "ps_tr"])

            def mod_ev(e, i=i):
                a0 = t_modsh[:, i:i + 1]
                a1 = t_modsh[:, 12 + i:13 + i]
                e.tensor_add(out=a0, in0=t_rs[:, 0:1], in1=t_rs[:, 2:3])
                ins = e.tensor_add(out=a1, in0=t_rs[:, 1:2], in1=t_rs[:, 3:4])
                if i in MODF:
                    e.tensor_add(out=a0, in0=a0, in1=t_rs[:, 4:5])
                    return e.tensor_add(out=a1, in0=a1, in1=t_rs[:, 5:6])
                return ins
            S.c("dve", mod_ev, reads=["t_rs"], writes=["modsh"])

        if dbg:
            dma("sp", dbg_o["d_ct"], t_ct[:], ["ct", "siluf"], ["d_ct"])
            dma("sp", dbg_o["d_siluf"], t_siluf[:], ["siluf", "silu"], ["d_siluf"])
            dma("pool", dbg_o["d_wmb"], wmb[1][:], ["wmb1", "modsh"], ["d_wmb"])
            dma("sp", dbg_o["d_modsh"], t_modsh[:], ["modsh"], ["d_modsh"])
            dma("pool", dbg_o["d_silu"], t_sbf[:], ["sbf", "slb"], ["d_silu"])
        dma("pool", ag1i[:, 0:24], t_modsh[:], ["modsh"], ["ag1i"])
        import os as _os
        if _os.environ.get("KNOCC1"):
            for _r in range(NCORES):
                dma("pool", ag1o[_r * P:(_r + 1) * P, :], ag1i[:, :],
                    ["ag1i"], ["ag1o"])
        else:
            S.op("pool", "cc", lambda e: e.collective_compute(
                "AllGather", ALU.bypass, replica_groups=[list(range(NCORES))],
                ins=[ag1i.opt()], outs=[ag1o.opt()]), reads=["ag1i"], writes=["ag1o"])
            S.op("pool", "cc", lambda e: e.collective_compute(
                "AllGather", ALU.bypass, replica_groups=[list(range(NCORES))],
                ins=[agfi.opt()], outs=[agf1o.opt()]),
                reads=["ag1o"], writes=["ag1f"])
        src_m = ag1o.rearrange("(r p) x -> p r x", p=P)
        dma("pool", t_rs[:, 0:96].rearrange("p (r c) -> p r c", c=12),
            src_m[:, :, 0:12], ["ag1o", "ag1f"], ["mg0", "t_rs"])
        dma("pool", t_mb[:, 0:96].rearrange("p (r c) -> p r c", c=12),
            src_m[:, :, 12:24], ["ag1o", "ag1f"], ["mbv"])

        S.c("dve", lambda e: e.tensor_scalar(out=t_mod[:], in0=t_rs[:, 0:96],
                                             scalar1=t_bmask[:, 0:1], scalar2=None,
                                             op0=ALU.mult),
            reads=["mg0", "bmask"], writes=["mod"])
        S.c("dve", lambda e: e.tensor_scalar(out=t_rs[:, 96:192], in0=t_mb[:, 0:96],
                                             scalar1=t_bmask[:, 1:2], scalar2=None,
                                             op0=ALU.mult),
            reads=["mbv", "bmask"], writes=["t_rs", "mbv"])
        S.c("dve", lambda e: e.tensor_tensor(out=t_mod[:], in0=t_mod[:],
                                             in1=t_rs[:, 96:192], op=ALU.add),
            reads=["mod", "t_rs"], writes=["mod"])
        for base in (32, 80):
            gaap = t_mod[:, base:base + 16]
            S.c("dve", lambda e, g=gaap: e.tensor_scalar(
                out=t_rs[:, 0:16], in0=g, scalar1=0.0, scalar2=None, op0=ALU.is_ge),
                reads=["mod"], writes=["t_rs"])
            S.c("dve", lambda e: e.tensor_scalar(
                out=t_rs[:, 16:32], in0=t_rs[:, 0:16], scalar1=2.0, scalar2=-1.0,
                op0=ALU.mult, op1=ALU.add), reads=["t_rs"], writes=["t_rs"])
            S.c("dve", lambda e, g=gaap: e.tensor_tensor(
                out=t_rs[:, 32:48], in0=g, in1=t_rs[:, 16:32], op=ALU.mult),
                reads=["mod", "t_rs"], writes=["t_rs"])
            S.c("dve", lambda e: e.tensor_scalar(
                out=t_rs[:, 48:64], in0=t_rs[:, 32:48], scalar1=0.25, scalar2=None,
                op0=ALU.max), reads=["t_rs"], writes=["t_rs"])
            S.c("dve", lambda e, g=gaap: e.tensor_tensor(
                out=g, in0=t_rs[:, 48:64], in1=t_rs[:, 16:32], op=ALU.mult),
                reads=["t_rs"], writes=["mod"])
        S.c("dve", lambda e: e.tensor_scalar(out=t_a1[:], in0=t_mod[:, 16:32],
                                             scalar1=1.0, scalar2=None, op0=ALU.add),
            reads=["mod"], writes=["a1"])
        S.c("dve", lambda e: e.tensor_scalar(out=t_a3[:], in0=t_mod[:, 64:80],
                                             scalar1=1.0, scalar2=None, op0=ALU.add),
            reads=["mod"], writes=["a3"])
        if dbg:
            dma("sp", dbg_o["d_mod"], t_mod[:], ["mod"], ["d_mod"])
        mstack.close()

        if KSTOP < 1:
            S.frozen = True
        def stats_pass(srcv, src_tag_fn):
            for c in range(NCH):
                S.c("dve", lambda e, c=c: e.tensor_copy(out=t_xbf[:], in_=srcv[:, c, :]),
                    reads=[src_tag_fn(c)], writes=["xbf"])
                S.c("act", lambda e, c=c: e.activation(out=t_sqs[:], in_=srcv[:, c, :],
                                                       func=AF.Square),
                    reads=[src_tag_fn(c)], writes=["sqs"])

                def mm(e, c=c):
                    for hh in range(2):
                        e.matmul(ps_p[0:1, hh * 512:(hh + 1) * 512], t_ones[:],
                                 t_xbf[:, hh * 512:(hh + 1) * 512],
                                 start=(c == 0), stop=False)
                        ins = e.matmul(ps_p[64:65, hh * 512:(hh + 1) * 512], t_ones[:],
                                       t_sqs[:, hh * 512:(hh + 1) * 512],
                                       start=(c == 0), stop=(c == NCH - 1))
                    return ins
                S.c("pe", mm, reads=["xbf", "sqs", "ones"],
                    writes=["ps_rows", "xbf", "sqs"])

        def ln_rows_and_bcast(mtile, mtag, rtile, rtag):
            def rmath1(e):
                e.tensor_scalar(out=mtile[0:1, 0:T], in0=ps_p[0:1, :], scalar1=INVD,
                                scalar2=None, op0=ALU.mult)
                return e.tensor_scalar(out=rtile[0:1, 0:T], in0=ps_p[64:65, :],
                                       scalar1=INVD, scalar2=None, op0=ALU.mult)
            S.c("dve", rmath1, reads=["ps_rows"], writes=[mtag, rtag, "ps_rows"])

            def rmath2(e):
                m = mtile[0:1, 0:T]
                v = rtile[0:1, 0:T]
                e.tensor_tensor(out=t_rs[0:1, 0:T], in0=m, in1=m, op=ALU.mult)
                return e.tensor_tensor(out=v, in0=v, in1=t_rs[0:1, 0:T], op=ALU.subtract)
            S.c("dve", rmath2, reads=[mtag, rtag], writes=[rtag, "t_rs"])
            S.c("act", lambda e: e.activation(out=t_rs[0:1, 0:T],
                                              in_=rtile[0:1, 0:T],
                                              func=AF.Sqrt, bias=EPS),
                reads=[rtag, "const"], writes=["t_rs"])
            S.c("dve", lambda e: e.reciprocal(out=rtile[0:1, 0:T],
                                              in_=t_rs[0:1, 0:T]),
                reads=["t_rs"], writes=[rtag])
            dma("sp", rowd[0:1, :], mtile[0:1, 0:T], [mtag], ["rowd01"])
            dma("sp", rowd[1:2, :], rtile[0:1, 0:T], [rtag], ["rowd01"])
            dma("sp", mtile[:], rowd[0:1, 0:T].partition_broadcast(P),
                ["rowd01"], [mtag])
            dma("sp", rtile[:], rowd[1:2, 0:T].partition_broadcast(P),
                ["rowd01"], [rtag])

        def col_bcast(srccols, nblk, dst_fn, dsttag, rtag):
            S.c("pe", lambda e: e.transpose(ps_tr[0:nblk, :], srccols, t_ident[:]),
                reads=[rtag, "ident"], writes=["ps_tr"])

            S.c("dve", lambda e: e.tensor_copy(out=t_bchh[:], in_=ps_tr[0:8, :]),
                reads=["ps_tr"], writes=["bch"])
            S.c("dve", lambda e: e.tensor_tensor(out=t_rs[0:8, 0:P],
                                                 in0=ps_tr[0:8, :], in1=t_bchh[:],
                                                 op=ALU.subtract),
                reads=["ps_tr", "bch"], writes=["t_rs", "ps_tr"])
            S.c("dve", lambda e: e.tensor_copy(out=t_bchl[:], in_=t_rs[0:8, 0:P]),
                reads=["t_rs"], writes=["bch"])
            for j in range(nblk):
                def bmm(e, j=j):
                    e.matmul(ps_tr[:, 0:P], t_bchot[:, j * P:(j + 1) * P], t_bchh[:],
                             start=True, stop=False)
                    return e.matmul(ps_tr[:, 0:P], t_bchot[:, j * P:(j + 1) * P],
                                    t_bchl[:], start=False, stop=True)
                S.c("pe", bmm, reads=["bch", "bchot"], writes=["ps_tr"])
                S.c("dve", lambda e, j=j: e.tensor_copy(out=dst_fn(j), in_=ps_tr[:, 0:P]),
                    reads=["ps_tr"], writes=[dsttag, "ps_tr"])

        def tree_to_cols(nblk, srcmax, srcmin, rtags):
            for j in range(nblk):
                S.c("pe", lambda e, j=j: e.transpose(
                    ps_tr[:], srcmax[:, j * P:(j + 1) * P], t_ident[:]),
                    reads=rtags + ["ident"], writes=["ps_tr"])
                S.c("dve", lambda e, j=j: e.tensor_reduce(
                    out=t_colst[:, 8 + j:9 + j], in_=ps_tr[:], axis=AX.X, op=ALU.max),
                    reads=["ps_tr"], writes=["colv", "ps_tr"])
            for j in range(nblk):
                S.c("pe", lambda e, j=j: e.transpose(
                    ps_tr[:], srcmin[:, j * P:(j + 1) * P], t_ident[:]),
                    reads=rtags + ["ident"], writes=["ps_tr"])

                S.c("dve", lambda e: e.tensor_reduce(out=t_rs[:, 0:1], in_=ps_tr[:],
                                                     axis=AX.X, op=ALU.min),
                    reads=["ps_tr"], writes=["t_rs", "ps_tr"])

                def mr(e, j=j):
                    e.tensor_scalar(out=t_rs[:, 0:1], in0=t_rs[:, 0:1], scalar1=-1.0,
                                    scalar2=None, op0=ALU.mult)
                    return e.tensor_tensor(out=t_colst[:, 8 + j:9 + j],
                                           in0=t_colst[:, 8 + j:9 + j],
                                           in1=t_rs[:, 0:1], op=ALU.max)
                S.c("dve", mr, reads=["t_rs"], writes=["colv", "t_rs"])

        def ln_apply_ops(srctag, srcslice, acol, bcol, mtile, mtag, rtile, rtag,
                         outslice, outtags, extra_reads=()):
            S.c("dve", lambda e: e.tensor_tensor(out=t_rs[:], in0=srcslice,
                                                 in1=mtile[:], op=ALU.subtract),
                reads=[srctag, mtag] + list(extra_reads), writes=["t_rs"])
            S.c("dve", lambda e: e.scalar_tensor_tensor(
                out=t_rs[:], in0=t_rs[:], scalar=acol, in1=rtile[:],
                op0=ALU.mult, op1=ALU.mult),
                reads=["t_rs", rtag, "mod", "a1", "a3", "gam"], writes=["t_rs"])
            S.c("dve", lambda e: e.tensor_scalar(out=outslice, in0=t_rs[:],
                                                 scalar1=bcol, scalar2=None,
                                                 op0=ALU.add),
                reads=["t_rs", "mod", "bet"], writes=list(outtags))

        stats_pass(xv, lambda c: "x")
        ln_rows_and_bcast(t_mb, "mbv", t_rb, "rbv")
        for c in range(NCH):
            ln_apply_ops("x", xv[:, c, :], t_a1[:, c:c + 1], t_mod[:, c:c + 1],
                         t_mb, "mbv", t_rb, "rbv", t_rs2[:], ["t_rs2"])
            S.c("dve", lambda e: e.tensor_copy(out=t_xbf[:], in_=t_rs2[:]),
                reads=["t_rs2"], writes=["xbf"])
            S.c("act", lambda e: e.activation(out=t_sqs[:], in_=t_rs2[:], func=AF.Square),
                reads=["t_rs2"], writes=["sqs"])

            def hmm(e, c=c):
                for hh in range(2):
                    e.matmul(ps_p[0:1, hh * 512:(hh + 1) * 512], t_ones[:],
                             t_xbf[:, hh * 512:(hh + 1) * 512],
                             start=(c == 0), stop=False)
                    ins = e.matmul(ps_p[64:65, hh * 512:(hh + 1) * 512], t_ones[:],
                                   t_sqs[:, hh * 512:(hh + 1) * 512],
                                   start=(c == 0), stop=(c == NCH - 1))
                return ins
            S.c("pe", hmm, reads=["xbf", "sqs", "ones"],
                writes=["ps_rows", "xbf", "sqs", "t_rs2"])
        ln_rows_and_bcast(t_s1b, "m2b", t_wb, "r2b")
        for c in range(NCH):
            ln_apply_ops("x", xv[:, c, :], t_a1[:, c:c + 1], t_mod[:, c:c + 1],
                         t_mb, "mbv", t_rb, "rbv", t_rs2[:], ["t_rs2"])
            ln_apply_ops("t_rs2", t_rs2[:], t_gam[:, c:c + 1], t_bet[:, c:c + 1],
                         t_s1b, "m2b", t_wb, "r2b", xnv[:, c, :], [f"xn{c}"])
            S.c("dve", lambda e, c=c: e.tensor_scalar(
                out=t_rs[:], in0=t_rs2[:], scalar1=t_mod[:, 32 + c:33 + c],
                scalar2=None, op0=ALU.mult),
                reads=["t_rs2", "mod"], writes=["t_rs"])
            S.c("dve", lambda e, c=c: e.tensor_tensor(
                out=xv[:, c, :], in0=xv[:, c, :], in1=t_rs[:], op=ALU.add),
                reads=["t_rs", "x"], writes=["x"])
        if dbg:
            dma("pool", dbg_o["d_xn"].rearrange("c p t -> p c t"), xnv,
                [f"xn{c}" for c in range(NCH)], ["d_xn"])
            dma("sp", dbg_o["d_h"].rearrange("c p t -> p c t"), xv,
                ["x"], ["d_h"])

        if KSTOP < 2:
            S.frozen = True
        p2a = contextlib.ExitStack()
        krotb = sb(p2a, "krotb", [P, NCH * T], BF16)
        krv = krotb[:].rearrange("p (c t) -> p c t", t=T)
        p2b = contextlib.ExitStack()
        winb = [sb(p2b, f"winb{i}", [P, NCH * P], BF16) for i in range(2)]
        t_cos = [sb(p2b, f"tcos{i}", [P, T], BF16) for i in range(2)]
        t_sin = [sb(p2b, f"tsin{i}", [P, T], BF16) for i in range(2)]
        t_kc = [sb(p2b, f"tkc{i}", [P, T], F32) for i in range(2)]

        for o in range(OQKV):
            bi = o % 2
            dma("pool", winb[bi][:].rearrange("p (c m) -> p c m", m=P),
                w_in[o].rearrange("c p m -> p c m"), [], [f"winb{bi}"])
            pstag = "ps_a" if o % 2 == 0 else "ps_b"
            pb = ps_a if o % 2 == 0 else ps_b

            def qkv_mm(e, bi=bi, pb=pb):
                for c in range(NCH):
                    for hh in range(2):
                        ins = e.matmul(pb[:, hh * 512:(hh + 1) * 512],
                                       winb[bi][:, c * P:(c + 1) * P],
                                       xnv[:, c, hh * 512:(hh + 1) * 512],
                                       start=(c == 0), stop=(c == NCH - 1))
                return ins
            S.c("pe", qkv_mm, reads=[f"winb{bi}"] + [f"xn{c}" for c in range(NCH)],
                writes=[pstag])

            if o < 16:
                S.c("act", lambda e, o=o, pb=pb: e.activation(
                    out=t_xbf[:], in_=pb[:], func=AF.Identity,
                    bias=t_binc[:, o:o + 1], scale=1.0),
                    reads=[pstag, "binc"], writes=[pstag, "xbf"])
                dma("sp", qd[o], t_xbf[:], ["xbf"], [f"qd{o}"])
            else:
                kvi = o % 2
                cch = (o - 16) % NCH
                isK = o < 32
                S.c("act", lambda e, o=o, pb=pb, kvi=kvi: e.activation(
                    out=t_kc[kvi][:], in_=pb[:], func=AF.Identity,
                    bias=t_binc[:, o:o + 1], scale=1.0),
                    reads=[pstag, "binc"], writes=[f"kc{kvi}", pstag])
                S.c("act", lambda e, kvi=kvi: e.activation(
                    out=t_sqs[:], in_=t_kc[kvi][:], func=AF.Square),
                    reads=[f"kc{kvi}"], writes=["sqs"])

                def nmm(e, o=o):
                    pr = 0 if o < 32 else 64
                    for hh in range(2):
                        ins = e.matmul(ps_p[pr:pr + 1, hh * 512:(hh + 1) * 512],
                                       t_ones[:], t_sqs[:, hh * 512:(hh + 1) * 512],
                                       start=(o in (16, 32)), stop=(o == 47))
                    return ins
                S.c("pe", nmm, reads=["sqs", "ones"], writes=["ps_rows", "sqs"])

                if cch < 8:
                    ci = cch % 2
                    dma("sp", t_cos[ci][:], costab[cch], [], [f"cos{ci}"])
                    dma("sp", t_sin[ci][:], sintab[cch], [], [f"sin{ci}"])

                    mask = [ii + 1 if ii % 2 == 0 else ii - 1 for ii in range(32)]
                    S.c("dve", lambda e, kvi=kvi, mask=mask: e.stream_shuffle(
                        out=t_rs[:], in_=t_kc[kvi][:], mask=mask),
                        reads=[f"kc{kvi}"], writes=["t_rs"])
                    S.c("dve", lambda e, ci=ci: e.tensor_tensor(
                        out=t_rs[:], in0=t_rs[:], in1=t_sin[ci][:], op=ALU.mult),
                        reads=["t_rs", f"sin{ci}"], writes=["t_rs"])
                    S.c("dve", lambda e, kvi=kvi, ci=ci: e.tensor_tensor(
                        out=t_rs2[:], in0=t_kc[kvi][:], in1=t_cos[ci][:], op=ALU.mult),
                        reads=[f"kc{kvi}", f"cos{ci}"], writes=["t_rs2"])
                    if isK:
                        S.c("dve", lambda e, cch=cch: e.tensor_tensor(
                            out=krv[:, cch, :], in0=t_rs[:], in1=t_rs2[:], op=ALU.add),
                            reads=["t_rs", "t_rs2"],
                            writes=[f"kr{cch}", f"kc{kvi}"])
                    else:
                        S.c("dve", lambda e: e.tensor_tensor(
                            out=t_rs[:], in0=t_rs[:], in1=t_rs2[:], op=ALU.add),
                            reads=["t_rs", "t_rs2"], writes=["t_rs"])
                        S.c("dve", lambda e, cch=cch: e.tensor_tensor(
                            out=krv[:, cch, :], in0=t_rs[:], in1=krv[:, cch, :],
                            op=ALU.mult),
                            reads=["t_rs", f"kr{cch}"],
                            writes=[f"kr{cch}", f"kc{kvi}"])
                else:
                    if isK:
                        S.c("dve", lambda e, kvi=kvi, cch=cch: e.tensor_copy(
                            out=krv[:, cch, :], in_=t_kc[kvi][:]),
                            reads=[f"kc{kvi}"], writes=[f"kr{cch}", f"kc{kvi}"])
                    else:
                        S.c("dve", lambda e, kvi=kvi, cch=cch: e.tensor_tensor(
                            out=krv[:, cch, :], in0=t_kc[kvi][:], in1=krv[:, cch, :],
                            op=ALU.mult),
                            reads=[f"kc{kvi}", f"kr{cch}"],
                            writes=[f"kr{cch}", f"kc{kvi}"])

        S.c("dve", lambda e: e.tensor_copy(out=t_wb[0:1, 0:T], in_=ps_p[0:1, :]),
            reads=["ps_rows"], writes=["wb", "r2b"])
        S.c("dve", lambda e: e.tensor_tensor(out=t_wb[0:1, 0:T], in0=t_wb[0:1, 0:T],
                                             in1=ps_p[64:65, :], op=ALU.mult),
            reads=["ps_rows", "wb"], writes=["wb", "ps_rows"])
        S.c("act", lambda e: e.activation(out=t_rs[0:1, 0:T],
                                          in_=t_wb[0:1, 0:T], func=AF.Sqrt),
            reads=["wb"], writes=["t_rs"])
        S.c("dve", lambda e: e.reciprocal(out=t_wb[0:1, 0:T],
                                          in_=t_rs[0:1, 0:T]),
            reads=["t_rs"], writes=["wb"])
        dma("sp", rowd[4:5, :], t_wb[0:1, 0:T], ["wb"], ["rowd4"])
        dma("sp", t_wb[:], rowd[4:5, :].partition_broadcast(P),
            ["rowd4"], ["wb"])

        for c in range(NCH):
            def apart(e, c=c):
                e.tensor_tensor(out=t_rs[:], in0=krv[:, c, :], in1=t_wb[:], op=ALU.mult)
                return e.tensor_reduce(out=t_agp[:, c:c + 1], in_=t_rs[:], axis=AX.X,
                                       op=ALU.add)
            S.c("dve", apart, reads=[f"kr{c}", "wb"], writes=["agp", "t_rs"])
        if dbg:
            dma("pool", dbg_o["d_kv"].rearrange("c p t -> p c t"), krv,
                [f"kr{c}" for c in range(NCH)], ["d_kv"])

        S.fence(["sb1", "m2b"], ["mshfence"])
        for widx in range(3):
            for cs in range(NCH):
                dma("sp", t_s1b[:], msh[widx, cs], ["mshfence", "mshabs"],
                    ["sb1", "m2b"])
                S.c("act", lambda e: e.activation(out=t_rs2[:], in_=t_s1b[:], func=AF.Abs),
                    reads=["sb1"], writes=["mshabs", "t_rs2"])

                S.c("dve", lambda e: e.tensor_reduce(out=t_colst[:, 1:2], in_=t_rs2[:],
                                                     axis=AX.X, op=ALU.add),
                    reads=["mshabs"], writes=["colv"])

                def mred(e, widx=widx, cs=cs):
                    if cs == 0:
                        return e.tensor_copy(out=t_agp[:, 16 + widx:17 + widx],
                                             in_=t_colst[:, 1:2])
                    return e.tensor_tensor(out=t_agp[:, 16 + widx:17 + widx],
                                           in0=t_agp[:, 16 + widx:17 + widx],
                                           in1=t_colst[:, 1:2], op=ALU.add)
                S.c("dve", mred, reads=["colv"], writes=["agp"])

        if dbg:
            dma("sp", dbg_o["d_agp"], t_agp[:], ["agp"], ["d_agp"])
        dma("pool", ag2i[:, 0:20], t_agp[:], ["agp"], ["ag2i"])
        import os as _os
        if _os.environ.get("KNOCC2"):
            for _r in range(NCORES):
                dma("pool", ag2o[_r * P:(_r + 1) * P, :], ag2i[:, :],
                    ["ag2i"], ["ag2o"])
        else:
            S.op("pool", "cc", lambda e: e.collective_compute(
                "AllGather", ALU.bypass, replica_groups=[list(range(NCORES))],
                ins=[ag2i.opt()], outs=[ag2o.opt()]), reads=["ag2i"], writes=["ag2o"])
            S.op("pool", "cc", lambda e: e.collective_compute(
                "AllGather", ALU.bypass, replica_groups=[list(range(NCORES))],
                ins=[agfi.opt()], outs=[agf2o.opt()]),
                reads=["ag2o"], writes=["ag2f"])
        dma("pool", t_gmx[:, 0:160].rearrange("p (r x) -> p r x", x=20),
            ag2o.rearrange("(r p) x -> p r x", p=P), ["ag2o", "ag2f"], ["gmx"])

        blocks = t_gmx[:, 0:160].rearrange("p (r x) -> p r x", x=20)
        for r in range(NCORES):
            if r == 0:
                S.c("dve", lambda e, r=r: e.tensor_scalar(
                    out=t_rs[:, 0:20], in0=blocks[:, r, :],
                    scalar1=t_amask[:, r:r + 1], scalar2=None, op0=ALU.mult),
                    reads=["gmx", "amask"], writes=["t_rs"])
                S.c("dve", lambda e, r=r: e.tensor_copy(out=t_rs[:, 32:52],
                                                        in_=blocks[:, r, :]),
                    reads=["gmx"], writes=["t_rs"])
            else:
                S.c("dve", lambda e, r=r: e.tensor_scalar(
                    out=t_rs[:, 64:84], in0=blocks[:, r, :],
                    scalar1=t_amask[:, r:r + 1], scalar2=None, op0=ALU.mult),
                    reads=["gmx", "amask"], writes=["t_rs"])
                S.c("dve", lambda e: e.tensor_tensor(
                    out=t_rs[:, 0:20], in0=t_rs[:, 0:20], in1=t_rs[:, 64:84],
                    op=ALU.add), reads=["t_rs"], writes=["t_rs"])
                S.c("dve", lambda e, r=r: e.tensor_tensor(
                    out=t_rs[:, 32:52], in0=t_rs[:, 32:52], in1=blocks[:, r, :],
                    op=ALU.add), reads=["t_rs", "gmx"], writes=["t_rs"])
        S.c("dve", lambda e: e.tensor_scalar(out=t_acol[:], in0=t_rs[:, 0:16],
                                             scalar1=0.5, scalar2=0.5,
                                             op0=ALU.mult, op1=ALU.add),
            reads=["t_rs"], writes=["acol"])
        S.c("dve", lambda e: e.tensor_copy(out=t_rs2[:, 0:3], in_=t_rs[:, 48:51]),
            reads=["t_rs"], writes=["t_rs2"])
        if dbg:
            dma("sp", dbg_o["d_acol"], t_acol[:], ["acol"], ["d_acol"])

        S.c("pe", lambda e: e.transpose(ps_tr[0:3, :], t_rs2[:, 0:3], t_ident[:]),
            reads=["t_rs2", "ident"], writes=["ps_tr"])

        S.c("dve", lambda e: e.tensor_copy(out=t_rs[0:3, 0:P], in_=ps_tr[0:3, :]),
            reads=["ps_tr"], writes=["t_rs", "ps_tr"])
        S.c("dve", lambda e: e.tensor_reduce(out=t_colst[0:3, 4:5], in_=t_rs[0:3, 0:P],
                                             axis=AX.X, op=ALU.add),
            reads=["t_rs"], writes=["colv"])
        dma("sp", rowd[5:6, 0:3], t_colst[0:3, 4:5], ["colv"], ["rowd5"])
        dma("sp", t_mrow[0:1, 0:3], rowd[5:6, 0:3], ["rowd5"], ["mrow"])

        S.c("dve", lambda e: e.tensor_scalar(out=t_mrow[0:1, 4:7],
                                             in0=t_mrow[0:1, 0:3],
                                             scalar1=1.0 / (DFF * D), scalar2=EPS,
                                             op0=ALU.mult, op1=ALU.max),
            reads=["mrow"], writes=["mrow"])
        S.c("dve", lambda e: e.reciprocal(out=t_mrow[0:1, 8:11],
                                          in_=t_mrow[0:1, 4:7]),
            reads=["mrow"], writes=["mrow"])
        S.c("dve", lambda e: e.tensor_scalar(out=t_mrow[0:1, 11:12],
                                             in0=t_mrow[0:1, 4:5],
                                             scalar1=1.0 / 128.0, scalar2=None,
                                             op0=ALU.mult),
            reads=["mrow"], writes=["mrow"])
        S.c("dve", lambda e: e.tensor_tensor(out=t_mrow[0:1, 12:13],
                                             in0=t_mrow[0:1, 5:6],
                                             in1=t_mrow[0:1, 6:7], op=ALU.mult),
            reads=["mrow"], writes=["mrow"])
        S.c("dve", lambda e: e.tensor_scalar(out=t_mrow[0:1, 12:13],
                                             in0=t_mrow[0:1, 12:13],
                                             scalar1=CDOWN, scalar2=None,
                                             op0=ALU.mult),
            reads=["mrow"], writes=["mrow"])
        dma("sp", rowd[6:7, 0:16], t_mrow[:], ["mrow"], ["rowd6"])
        dma("sp", t_scol[:, 0:5], rowd[6:7, 8:13].partition_broadcast(P),
            ["rowd6"], ["scol"])
        if dbg:
            dma("sp", dbg_o["d_mrow"], t_mrow[:], ["mrow"], ["d_mrow"])

        if KSTOP < 3:
            S.frozen = True
        yv = xnv
        dma("sp", xnb[:].rearrange("p (c t) -> p c t", t=T),
            qd.rearrange("c p t -> p c t"),
            [f"qd{o}" for o in range(16)], ["y"] + [f"xn{c}" for c in range(NCH)])
        for c in range(NCH):
            S.c("dve", lambda e, c=c: e.tensor_scalar(
                out=yv[:, c, :], in0=yv[:, c, :], scalar1=t_acol[:, c:c + 1],
                scalar2=None, op0=ALU.mult), reads=["acol"], writes=["y"])

        for o in range(NCH):
            bi = o % 2
            dma("pool", winb[bi][:].rearrange("p (c m) -> p c m", m=P),
                w_out[o].rearrange("c p m -> p c m"), [], [f"winb{bi}"])
            pstag = "ps_a" if o % 2 == 0 else "ps_b"
            pb = ps_a if o % 2 == 0 else ps_b

            def wo_mm(e, bi=bi, pb=pb):
                for c in range(NCH):
                    for hh in range(2):
                        ins = e.matmul(pb[:, hh * 512:(hh + 1) * 512],
                                       winb[bi][:, c * P:(c + 1) * P],
                                       yv[:, c, hh * 512:(hh + 1) * 512],
                                       start=(c == 0), stop=(c == NCH - 1))
                return ins
            S.c("pe", wo_mm, reads=[f"winb{bi}", "y"], writes=[pstag])

            S.c("dve", lambda e, o=o, pb=pb: e.tensor_scalar(
                out=t_rs[:], in0=pb[:], scalar1=t_boutc[:, o:o + 1],
                scalar2=t_mod[:, 32 + o:33 + o], op0=ALU.add, op1=ALU.mult),
                reads=[pstag, "boutc", "mod"], writes=["t_rs", pstag])
            S.c("dve", lambda e, o=o: e.tensor_tensor(
                out=xv[:, o, :], in0=xv[:, o, :], in1=t_rs[:], op=ALU.add),
                reads=["t_rs", "x"], writes=["x", f"x1c{o}"])
            dma("sp", x1d[o], xv[:, o, :], [f"x1c{o}"], [f"x1d{o}"])
        if dbg:
            dma("sp", dbg_o["d_x1"].rearrange("c p t -> p c t"), xv,
                [f"x1c{o}" for o in range(NCH)], ["d_x1"])

        stats_pass(xv, lambda c: f"x1c{c}")
        ln_rows_and_bcast(t_mb, "mbv", t_rb, "rbv")

        for c in range(NCH):
            ln_apply_ops(f"x1c{c}", xv[:, c, :], t_a3[:, c:c + 1],
                         t_mod[:, 48 + c:49 + c], t_mb, "mbv", t_rb, "rbv",
                         t_rs[:], ["t_rs", "h2c"])
            if c == 0:
                S.c("dve", lambda e: e.tensor_copy(out=t_rs2[:], in_=t_rs[:]),
                    reads=["h2c"], writes=["t_rs2"])
                S.c("dve", lambda e: e.tensor_copy(out=t_min[:], in_=t_rs[:]),
                    reads=["h2c"], writes=["t_min"])
            else:
                S.c("dve", lambda e: e.tensor_tensor(out=t_rs2[:], in0=t_rs2[:],
                                                     in1=t_rs[:], op=ALU.max),
                    reads=["h2c", "t_rs2"], writes=["t_rs2"])
                S.c("dve", lambda e: e.tensor_tensor(
                    out=t_min[:], in0=t_min[:], in1=t_rs[:], op=ALU.min),
                    reads=["h2c", "t_min"], writes=["t_min"])

        tree_to_cols(8, t_rs2, t_min, ["t_rs2", "t_min"])

        S.c("dve", lambda e: e.tensor_scalar(out=t_colst[:, 8:16],
                                             in0=t_colst[:, 8:16], scalar1=EPS,
                                             scalar2=None, op0=ALU.max),
            reads=["colv"], writes=["colv"])
        S.c("dve", lambda e: e.reciprocal(out=t_colst[:, 0:8], in_=t_colst[:, 8:16]),
            reads=["colv"], writes=["colv"])
        S.c("dve", lambda e: e.tensor_scalar(out=t_colst[:, 0:8], in0=t_colst[:, 0:8],
                                             scalar1=128.0, scalar2=None,
                                             op0=ALU.mult),
            reads=["colv"], writes=["colv"])
        S.c("dve", lambda e: e.tensor_copy(out=t_gmx[:, 160:168],
                                           in_=t_colst[:, 8:16]),
            reads=["colv"], writes=["gmx"])
        if dbg:
            dma("sp", dbg_o["d_m1c"], t_colst[:], ["colv"], ["d_m1c"])

        col_bcast(t_colst[:, 0:8], 8, lambda j: t_s1b[:, j * P:(j + 1) * P],
                  "sb1", "colv")
        if dbg:
            dma("sp", dbg_o["d_s1b"], t_s1b[:], ["sb1"], ["d_s1b"])

        def cg_colmath(e):
            return e.tensor_scalar(out=t_colst[:, 0:8], in0=t_colst[:, 8:16],
                                   scalar1=t_scol[:, 3:4], scalar2=None, op0=ALU.mult)
        S.c("dve", cg_colmath, reads=["colv", "scol"], writes=["colv"])
        col_bcast(t_colst[:, 0:8], 8, lambda j: t_wb[:, j * P:(j + 1) * P],
                  "wb", "colv")

        q1v = yv
        for c in range(NCH):
            ln_apply_ops(f"x1c{c}", xv[:, c, :], t_a3[:, c:c + 1],
                         t_mod[:, 48 + c:49 + c], t_mb, "mbv", t_rb, "rbv",
                         t_rs[:], ["t_rs", "h2c"])
            S.c("dve", lambda e: e.tensor_tensor(out=q1i[:], in0=t_rs[:],
                                                 in1=t_s1b[:], op=ALU.mult),
                reads=["h2c", "sb1"], writes=["q1i"])
            S.c("dve", lambda e, c=c: e.tensor_copy(out=q1v[:, c, :], in_=q1i[:]),
                reads=["q1i", "y"], writes=[f"q1{c}", "y"])
        if dbg:
            dma("pool", dbg_o["d_q1"].rearrange("c p t -> p c t"), q1v,
                [f"q1{c}" for c in range(NCH)], ["d_q1"])

        if KSTOP < 4:
            S.frozen = True
        p2b.close()
        p2a.close()
        xstack.close()
        S.fence(["x"] + [f"kr{c}" for c in range(NCH)] +
                ["cos0", "cos1", "sin0", "sin1", "kc0", "kc1", "xbf", "sqs",
                 "winb0", "winb1"], ["p4fence"])
        p4 = contextlib.ExitStack()
        wgf = [sb(p4, f"wgf{i}", [P, NCH * OSUP], F32) for i in range(2)]
        wuf = [sb(p4, f"wuf{i}", [P, NCH * OSUP], F32) for i in range(2)]
        t_d2 = sb(p4, "t_d2", [P, NCH * OSUP], BF16)
        tern_g = sb(p4, "tern_g", [P, NCH * OSUP], BF16)
        tern_u = sb(p4, "tern_u", [P, NCH * OSUP], BF16)
        t_sg = sb(p4, "t_sg", [P, T], F32)
        t_ic = sb(p4, "t_ic", [P, T], BF16)
        t_pc = sb(p4, "t_pc", [P, T], BF16)

        for i in range(NSUP):
            bi = i % 2
            dma("sp", wgf[bi][:].rearrange("p (c m) -> p c m", m=OSUP),
                wg_t[i].rearrange("c p m -> p c m"), ["p4fence"], [f"wgf{bi}"])
            dma("sp", wuf[bi][:].rearrange("p (c m) -> p c m", m=OSUP),
                wu_t[i].rearrange("c p m -> p c m"), ["p4fence"], [f"wuf{bi}"])
            for wf, wtag, tt, ttag, sci in ((wgf[bi], f"wgf{bi}", tern_g, "tg", 0),
                                            (wuf[bi], f"wuf{bi}", tern_u, "tu", 1)):
                S.c("dve", lambda e, wf=wf, tt=tt, sci=sci: e.tensor_scalar(
                    out=tt[:], in0=wf[:], scalar1=t_scol[:, sci:sci + 1], scalar2=0.5,
                    op0=ALU.mult, op1=ALU.is_ge),
                    reads=[wtag, "scol"], writes=[ttag])
                S.c("pool", lambda e, wf=wf, sci=sci: e.tensor_scalar(
                    out=t_d2[:], in0=wf[:], scalar1=t_scol[:, sci:sci + 1],
                    scalar2=-0.5, op0=ALU.mult, op1=ALU.is_lt),
                    reads=[wtag, "scol"], writes=["t_d2"])
                S.c("dve", lambda e, tt=tt: e.tensor_tensor(
                    out=tt[:], in0=tt[:], in1=t_d2[:], op=ALU.subtract),
                    reads=["t_d2", ttag], writes=[ttag, "t_d2", wtag])

            for blk in range(OSUP // P):
                a = i * (OSUP // P) + blk
                gsl = tern_g[:].rearrange("p (c m) -> p c m", m=OSUP)[:, :, blk * P:(blk + 1) * P]
                usl = tern_u[:].rearrange("p (c m) -> p c m", m=OSUP)[:, :, blk * P:(blk + 1) * P]

                def gmm(e, gsl=gsl):
                    for c in range(NCH):
                        for hh in range(2):
                            ins = e.matmul(ps_a[:, hh * 512:(hh + 1) * 512],
                                           gsl[:, c, :],
                                           q1v[:, c, hh * 512:(hh + 1) * 512],
                                           start=(c == 0), stop=(c == NCH - 1))
                    return ins
                S.c("pe", gmm, reads=["tg"] + [f"q1{c}" for c in range(NCH)],
                    writes=["ps_a"])
                S.c("dve", lambda e: e.tensor_tensor(out=t_sg[:], in0=ps_a[:],
                                                     in1=t_wb[:], op=ALU.mult),
                    reads=["ps_a", "wb"], writes=["sg", "ps_a"])
                S.c("act", lambda e: e.activation(out=t_sg[:], in_=t_sg[:],
                                                  func=AF.Silu),
                    reads=["sg"], writes=["sg"])
                if dbg and a == 0:
                    dma("sp", dbg_o["d_inter"], t_sg[:], ["sg"], ["d_inter"])

                def umm(e, usl=usl):
                    for c in range(NCH):
                        for hh in range(2):
                            ins = e.matmul(ps_b[:, hh * 512:(hh + 1) * 512],
                                           usl[:, c, :],
                                           q1v[:, c, hh * 512:(hh + 1) * 512],
                                           start=(c == 0), stop=(c == NCH - 1))
                    return ins
                S.c("pe", umm, reads=["tu"] + [f"q1{c}" for c in range(NCH)],
                    writes=["ps_b"])
                S.c("dve", lambda e: e.tensor_tensor(out=t_ic[:], in0=t_sg[:],
                                                     in1=ps_b[:], op=ALU.mult),
                    reads=["sg", "ps_b"], writes=["ic", "ps_b"])

                def s1mm(e):
                    for hh in range(2):
                        ins = e.matmul(ps_p[:, hh * 512:(hh + 1) * 512], t_h128[:],
                                       t_ic[:, hh * 512:(hh + 1) * 512],
                                       start=True, stop=True)
                    return ins
                S.c("pe", s1mm, reads=["ic", "h128"], writes=["ps_p", "ps_rows"])
                S.c("dve", lambda e: e.tensor_copy(out=t_pc[:], in_=ps_p[:]),
                    reads=["ps_p"], writes=["pc", "ps_p"])
                dma("sp", pdt[:, a, :, :],
                    t_pc[:].rearrange("p (tc t) -> p tc t", t=TC), ["pc"], [f"pd{a}"])

        if KSTOP < 5:
            S.frozen = True
        p4.close()
        ystack.close()
        S.fence(["y", "tg", "tu", "t_d2", "wgf0", "wgf1", "wuf0", "wuf1",
                 "sg", "ic", "pc"] + [f"q1{c}" for c in range(NCH)], ["p5fence"])
        p5a = contextlib.ExitStack()
        q2b = sb(p5a, "q2b", [P, FCH * T], BF16)
        q2v = q2b[:].rearrange("p (k t) -> p k t", t=T)
        p5b = contextlib.ExitStack()
        gtc = sb(p5b, "gtc", [P, FCH * TC], BF16)
        gv = gtc[:].rearrange("p (k t) -> p k t", t=TC)
        prb = [sb(p5b, f"prb{i}", [64, TC], BF16) for i in range(2)]
        q2i8 = sb(p5b, "q2i8", [P, 2 * TC], I8)
        pdall = [f"pd{a}" for a in range(FCH)]

        for tc in range(NTC):
            for jj in range(32):
                for q in range(4):
                    bp = 4 * jj + q
                    bi = bp % 2
                    dma("sp", prb[bi][:], pdt[bp, :, tc, :],
                        pdall + ["p5fence"], [f"prb{bi}"])

                    def s2mm(e, q=q, bi=bi):
                        pbase = 64 * (q % 2)
                        fbase = 256 * (q // 2)
                        return e.matmul(ps_p[pbase:pbase + 64, fbase:fbase + TC],
                                        t_h64[:], prb[bi][:], start=True, stop=True)
                    S.c("pe", s2mm, reads=[f"prb{bi}", "h64"], writes=["ps_p"])

                def s2ev(e, jj=jj):
                    e.copy(out=gv[:, 2 * jj, :], in_=ps_p[:, 0:TC])
                    return e.copy(out=gv[:, 2 * jj + 1, :],
                                  in_=ps_p[:, TC:2 * TC])
                S.c("act", s2ev, reads=["ps_p"], writes=[f"g{jj}", "ps_p"])

            for k in range(FCH):
                jj = k // 2
                if k == 0:
                    S.c("dve", lambda e, k=k, jj=jj: e.tensor_copy(
                        out=t_rs2[:, 0:TC], in_=gv[:, k, :]),
                        reads=[f"g{jj}"], writes=["t_rs2"])
                    S.c("dve", lambda e, k=k, jj=jj: e.tensor_copy(
                        out=t_min[:, 0:TC], in_=gv[:, k, :]),
                        reads=[f"g{jj}"], writes=["t_min"])
                else:
                    S.c("dve", lambda e, k=k, jj=jj: e.tensor_tensor(
                        out=t_rs2[:, 0:TC], in0=t_rs2[:, 0:TC], in1=gv[:, k, :],
                        op=ALU.max), reads=[f"g{jj}", "t_rs2"], writes=["t_rs2"])
                    S.c("dve", lambda e, k=k, jj=jj: e.tensor_tensor(
                        out=t_min[:, 0:TC], in0=t_min[:, 0:TC], in1=gv[:, k, :],
                        op=ALU.min), reads=[f"g{jj}", "t_min"], writes=["t_min"])

            tree_to_cols(2, t_rs2[:, 0:TC], t_min[:, 0:TC], ["t_rs2", "t_min"])

            S.c("dve", lambda e: e.tensor_scalar(out=t_colst[:, 8:10],
                                                 in0=t_colst[:, 8:10], scalar1=EPS,
                                                 scalar2=None, op0=ALU.max),
                reads=["colv"], writes=["colv"])
            S.c("dve", lambda e: e.reciprocal(out=t_colst[:, 0:2],
                                              in_=t_colst[:, 8:10]),
                reads=["colv"], writes=["colv"])
            S.c("dve", lambda e: e.tensor_scalar(out=t_colst[:, 0:2],
                                                 in0=t_colst[:, 0:2], scalar1=128.0,
                                                 scalar2=None, op0=ALU.mult),
                reads=["colv"], writes=["colv"])
            S.c("dve", lambda e, tc=tc: e.tensor_tensor(
                out=t_colst[:, 4:6], in0=t_colst[:, 8:10],
                in1=t_gmx[:, 160 + 2 * tc:162 + 2 * tc], op=ALU.mult),
                reads=["colv", "gmx"], writes=["colv"])
            S.c("dve", lambda e: e.tensor_scalar(out=t_colst[:, 4:6],
                                                 in0=t_colst[:, 4:6],
                                                 scalar1=t_scol[:, 4:5],
                                                 scalar2=None, op0=ALU.mult),
                reads=["colv", "scol"], writes=["colv"])
            S.c("dve", lambda e, tc=tc: e.tensor_copy(
                out=t_gmx[:, 168 + 2 * tc:170 + 2 * tc], in_=t_colst[:, 4:6]),
                reads=["colv"], writes=["gmx"])

            col_bcast(t_colst[:, 0:2], 2,
                      lambda j, tc=tc: t_s1b[:, tc * TC + j * P:tc * TC + (j + 1) * P],
                      "sb1", "colv")

            for jj in range(32):
                def q2a(e, jj=jj, tc=tc):
                    sl = t_s1b[:, tc * TC:(tc + 1) * TC]
                    e.tensor_tensor(out=q2i8[:, 0:TC], in0=gv[:, 2 * jj, :], in1=sl,
                                    op=ALU.mult)
                    return e.tensor_tensor(out=q2i8[:, TC:2 * TC],
                                           in0=gv[:, 2 * jj + 1, :], in1=sl,
                                           op=ALU.mult)
                S.c("dve", q2a, reads=[f"g{jj}", "sb1"], writes=["q2i8"])

                def q2b_(e, jj=jj, tc=tc):
                    e.tensor_copy(out=q2v[:, 2 * jj, tc * TC:(tc + 1) * TC],
                                  in_=q2i8[:, 0:TC])
                    return e.tensor_copy(out=q2v[:, 2 * jj + 1, tc * TC:(tc + 1) * TC],
                                         in_=q2i8[:, TC:2 * TC])
                S.c("dve", q2b_, reads=["q2i8"], writes=[f"q2_{tc}_{jj}", f"g{jj}"])

        if dbg:
            dma("pool", dbg_o["d_q2"], q2b[:, 0:4 * T],
                [f"q2_{tc}_{jj}" for tc in range(NTC) for jj in range(4)], ["d_q2"])

        col_bcast(t_gmx[:, 168:176], 8, lambda j: t_wb[:, j * P:(j + 1) * P],
                  "wb", "gmx")

        if KSTOP < 6:
            S.frozen = True
        S.fence(["t_rs2", "t_min", "sb1"] + [f"g{jj}" for jj in range(32)] +
                [f"s2b{tc}" for tc in range(NTC)], ["p6fence"])
        p5b.close()
        p6 = contextlib.ExitStack()
        wdf = sb(p6, "wdf", [P, 32 * P], F32)
        t_dd = sb(p6, "t_dd", [P, 32 * P], BF16)
        tern_d = sb(p6, "tern_d", [P, 32 * P], BF16)
        t_x1c = sb(p6, "t_x1c", [P, T], F32)
        t_oc = sb(p6, "t_oc", [P, T], F32)
        q2tags = [f"q2_{tc}_{jj}" for tc in range(NTC) for jj in range(32)]

        for o2 in range(NCH):
            for kh in range(2):
                dma("sp", wdf[:].rearrange("p (c m) -> p c m", m=P),
                    wd_t[o2, kh].rearrange("c p m -> p c m"), ["p6fence"], ["wdf"])
                S.c("dve", lambda e: e.tensor_scalar(
                    out=tern_d[:], in0=wdf[:], scalar1=t_scol[:, 2:3], scalar2=0.5,
                    op0=ALU.mult, op1=ALU.is_ge),
                    reads=["wdf", "scol"], writes=["ternd"])
                S.c("pool", lambda e: e.tensor_scalar(
                    out=t_dd[:], in0=wdf[:], scalar1=t_scol[:, 2:3], scalar2=-0.5,
                    op0=ALU.mult, op1=ALU.is_lt),
                    reads=["wdf", "scol"], writes=["tdd"])
                S.c("dve", lambda e: e.tensor_tensor(
                    out=tern_d[:], in0=tern_d[:], in1=t_dd[:], op=ALU.subtract),
                    reads=["tdd", "ternd"], writes=["ternd", "tdd", "wdf"])

                def dmm(e, kh=kh):
                    for ck in range(32):
                        kc = kh * 32 + ck
                        for hh in range(2):
                            ins = e.matmul(ps_a[:, hh * 512:(hh + 1) * 512],
                                           tern_d[:, ck * P:(ck + 1) * P],
                                           q2v[:, kc, hh * 512:(hh + 1) * 512],
                                           start=(kh == 0 and ck == 0),
                                           stop=(kh == 1 and ck == 31))
                    return ins
                S.c("pe", dmm, reads=["ternd"] + q2tags, writes=["ps_a"])

            dma("sp", t_x1c[:], x1d[o2], [f"x1d{o2}", "x1used"], ["x1c"])

            def down_ev(e, o2=o2):
                e.scalar_tensor_tensor(out=t_oc[:], in0=ps_a[:],
                                       scalar=t_mod[:, 80 + o2:81 + o2], in1=t_wb[:],
                                       op0=ALU.mult, op1=ALU.mult)
                return e.tensor_tensor(out=t_oc[:], in0=t_oc[:], in1=t_x1c[:],
                                       op=ALU.add)
            S.c("dve", down_ev, reads=["ps_a", "mod", "wb", "x1c"],
                writes=["toc", "ps_a", "x1used"])
            dma("sp", outT[o2], t_oc[:], ["toc"], [f"out{o2}", "toc"])

        S.c("dve", lambda e: e.engine_nop(),
            reads=[f"out{o2}" for o2 in range(NCH)] + dbg_tags, writes=["fin"])

        S.emit(block, sems)
        p6.close()
        p5a.close()

    return nc


def _sylvester(n):
    h = np.array([[1.0]], dtype=np.float32)
    while h.shape[0] < n:
        h = np.block([[h, h], [h, -h]])
    return h.astype(np.float32)


def _prep_inputs(inputs):
    X = np.asarray(inputs["L0_attn_X"])
    W_in = np.asarray(inputs["L0_attn_W_in"])
    b_in = np.asarray(inputs["L0_attn_b_in"])
    W_out = np.asarray(inputs["L0_attn_W_out"])
    b_out = np.asarray(inputs["L0_attn_b_out"])
    gamma = np.asarray(inputs["L0_attn_gamma"])
    beta = np.asarray(inputs["L0_attn_beta"])
    w_gate = np.asarray(inputs["L0_ffn_w_gate"])
    w_up = np.asarray(inputs["L0_ffn_w_up"])
    w_down = np.asarray(inputs["L0_ffn_w_down"])
    c = np.asarray(inputs["L0_c"])
    wmod = np.asarray(inputs["L0_wmod"])
    f32 = np.float32

    w_in_t = np.ascontiguousarray(
        W_in.reshape(NCH, P, OQKV, P).transpose(2, 0, 1, 3)).astype(f32, copy=False)
    w_out_t = np.ascontiguousarray(
        W_out.reshape(NCH, P, NCH, P).transpose(2, 0, 1, 3)).astype(f32, copy=False)
    wgT = np.ascontiguousarray(w_gate.T)
    wuT = np.ascontiguousarray(w_up.T)
    wg_tile = np.ascontiguousarray(wgT.reshape(NCH, P, NSUP, OSUP).transpose(2, 0, 1, 3))
    wu_tile = np.ascontiguousarray(wuT.reshape(NCH, P, NSUP, OSUP).transpose(2, 0, 1, 3))
    wdT = np.ascontiguousarray(w_down.T)
    perm = np.zeros(DFF, dtype=np.int64)
    for k in range(FCH):
        jj, half = k // 2, k % 2
        for p in range(P):
            bprime = 4 * jj + 2 * half + (1 if p >= 64 else 0)
            aprime = p % 64
            perm[k * P + p] = aprime * P + bprime
    wdT_perm = wdT[perm]
    wd_tile = np.ascontiguousarray(
        wdT_perm.reshape(2, 32, P, NCH, P).transpose(3, 0, 1, 2, 4))

    b_in_c = np.ascontiguousarray(b_in.reshape(OQKV, P).T)
    b_out_cc = np.ascontiguousarray(b_out.reshape(NCH, P).T)
    gam_cc = np.ascontiguousarray(gamma.reshape(NCH, P).T)
    bet_cc = np.ascontiguousarray(beta.reshape(NCH, P).T)
    cTt = np.ascontiguousarray(c.T.reshape(NCH, P, 2))

    h128m = _sylvester(128).astype(ml_dtypes.bfloat16)
    h64m = _sylvester(64).astype(ml_dtypes.bfloat16)
    identm = np.eye(P, dtype=f32)
    onesc = np.ones((P, 1), dtype=ml_dtypes.bfloat16)
    constv = np.tile(np.array([[EPS, 1.0, 0.0, 0.5]], dtype=f32), (P, 1))
    bchot = np.zeros((8, 8, P), dtype=ml_dtypes.bfloat16)
    for j in range(8):
        bchot[j, j, :] = 1.0

    rot_dim = D // 2
    inv_freq = (1.0 / (10000.0 ** (np.arange(0, rot_dim, 2, dtype=f32) / f32(rot_dim))))
    in_maps = []
    for core in range(NCORES):
        b = core // 4
        s0 = (core % 4) * T
        pos = np.arange(s0, s0 + T, dtype=f32)
        fr = pos[None, :] * inv_freq[:, None]
        cosf = np.repeat(np.cos(fr), 2, axis=0)
        sinf = np.repeat(np.sin(fr), 2, axis=0)
        sgn = np.where(np.arange(rot_dim) % 2 == 0, -1.0, 1.0).astype(f32)
        sinf = sinf * sgn[:, None]
        costabc = np.ascontiguousarray(cosf.reshape(8, P, T)).astype(ml_dtypes.bfloat16)
        sintabc = np.ascontiguousarray(sinf.reshape(8, P, T)).astype(ml_dtypes.bfloat16)

        xTc = np.ascontiguousarray(X[b].T[:, s0:s0 + T]).reshape(NCH, P, T).astype(f32)
        wmod_s = np.ascontiguousarray(
            wmod[:, core * 1536:(core + 1) * 1536]
            .reshape(NCH, P, 12, P).transpose(2, 0, 1, 3))
        bm = np.tile(np.array([[1.0, 0.0]] if b == 0 else [[0.0, 1.0]], dtype=f32), (P, 1))
        am = np.zeros((P, 8), dtype=f32)
        am[:, 4 * b:4 * b + 4] = 1.0
        mg = np.ascontiguousarray(
            wgT[:, core * 1024:(core + 1) * 1024].reshape(NCH, P, T))
        mu = np.ascontiguousarray(
            wuT[:, core * 1024:(core + 1) * 1024].reshape(NCH, P, T))
        md = np.ascontiguousarray(
            wdT_perm[core * 1024:(core + 1) * 1024].reshape(NCH, P, T))
        mshard = np.stack([mg, mu, md], axis=0)

        in_maps.append({
            "xT": xTc, "w_in": w_in_t, "b_in_c": b_in_c,
            "w_out": w_out_t, "b_out_c": b_out_cc, "gam_c": gam_cc, "bet_c": bet_cc,
            "wg_t": wg_tile, "wu_t": wu_tile, "wd_t": wd_tile,
            "wmod_sh": wmod_s, "cT": cTt, "costab": costabc, "sintab": sintabc,
            "h128": h128m, "h64": h64m, "ident": identm, "onescol": onesc,
            "bchot": bchot, "constv": constv, "bmask": bm, "amask": am,
            "msh": mshard,
        })
    return in_maps


_prog_cache = {}


def kernel(**inputs):
    in_maps = _prep_inputs(inputs)
    if "main" not in _prog_cache:
        _prog_cache["main"] = build_program(dbg=False, serialize=False)
    nc = _prog_cache["main"]
    res = run_bass_kernel_spmd(nc, in_maps, core_ids=list(range(NCORES)))
    out = np.zeros((2, 4096, D), dtype=np.float32)
    for core in range(NCORES):
        b = core // 4
        s0 = (core % 4) * T
        o = np.asarray(res.results[core]["outT"])
        out[b, s0:s0 + T, :] = o.reshape(D, T).T
    return out


# version 02
# speedup vs baseline: 2.0895x; 2.0895x over previous
import contextlib
import numpy as np
import ml_dtypes
import concourse.bass as bass
import concourse.mybir as mybir
from concourse.bass_utils import run_bass_kernel_spmd

F32 = mybir.dt.float32
BF16 = mybir.dt.bfloat16
I8 = mybir.dt.int8
AF = mybir.ActivationFunctionType
ALU = mybir.AluOpType
AX = mybir.AxisListType

P = 128
D = 2048
DFF = 8192
T = 1024
NCH = D // P
FCH = DFF // P
NCORES = 8
EPS = 1e-5
OQKV = 3 * D // P
NSUP = 32
OSUP = DFF // NSUP
TC = 256
NTC = T // TC
INVD = 1.0 / D
CDOWN = float(1.0 / (np.sqrt(2.0) * 64.0 * 128.0 * 128.0))

ENGINES = ("sp", "act", "dve", "pool", "pe")
SAME_ENGINE_EDGES = True


def o_kind(k):
    return k


class Sched:

    def __init__(self, serialize=False):
        self.frozen = False
        self.ops = []
        self.last_w = {}
        self.readers = {}
        self.serialize = serialize
        self.stream_last = {}

    def op(self, stream, kind, emit_fn, reads=(), writes=()):
        if self.frozen:
            return -1
        oid = len(self.ops)
        deps = set()
        if self.serialize:
            for s, last in self.stream_last.items():
                if s != stream:
                    deps.add(last)
        for t in reads:
            if t in self.last_w:
                deps.add(self.last_w[t])
        for t in writes:
            if t in self.last_w:
                deps.add(self.last_w[t])
            for r in self.readers.get(t, ()):
                deps.add(r)
        if not SAME_ENGINE_EDGES:
            deps = {dd for dd in deps if self.ops[dd][0] != stream}
        else:
            deps = {dd for dd in deps
                    if self.ops[dd][0] != stream or self.ops[dd][1] != o_kind(kind)
                    or True}
        self.ops.append([stream, kind, emit_fn, deps, list(reads), list(writes)])
        for t in reads:
            self.readers.setdefault(t, []).append(oid)
        for t in writes:
            self.last_w[t] = oid
            self.readers[t] = []
        self.stream_last[stream] = oid
        return oid

    def c(self, stream, emit_fn, reads=(), writes=()):
        return self.op(stream, "c", emit_fn, list(reads), list(writes))

    def d(self, stream, emit_fn, reads=(), writes=()):
        return self.op(stream, "d", emit_fn, list(reads), list(writes))

    def fence(self, reads, writes):
        self.c("dve", lambda e: e.engine_nop(), reads=reads, writes=writes)

    NSLOT = 8

    def lower(self):
        n = len(self.ops)
        marked = [False] * n
        for o in self.ops:
            for dd in o[3]:
                marked[dd] = True
        cnt = {}
        comp = [0] * n
        self.slot = [None] * n
        self.use = [0] * n
        dma_idx = {}
        for i, o in enumerate(self.ops):
            if o[1] == "c":
                key = (o[0], "c")
                if marked[i]:
                    cnt[key] = cnt.get(key, 0) + 1
                comp[i] = cnt.get(key, 0)
            elif o[1] == "d":
                k = dma_idx.get(o[0], 0)
                dma_idx[o[0]] = k + 1
                self.slot[i] = k % self.NSLOT
                self.use[i] = k // self.NSLOT
            else:
                k = dma_idx.get("cc", 0)
                dma_idx["cc"] = k + 1
                self.slot[i] = "cc"
                self.use[i] = k
        waits = [[] for _ in range(n)]
        clock = {}
        for i, o in enumerate(self.ops):
            need = {}
            for dd in sorted(o[3]):
                dop = self.ops[dd]
                if dop[1] == "c":
                    key = (dop[0], "c")
                    need[key] = max(need.get(key, 0), comp[dd])
                elif dop[1] == "d":
                    key = (dop[0], "s", self.slot[dd])
                    need[key] = max(need.get(key, 0), 16 * (self.use[dd] + 1))
                else:
                    key = ("cc",)
                    need[key] = max(need.get(key, 0), self.use[dd] + 1)
            for key, v in sorted(need.items(), key=str):
                ck = (o[0], key)
                if clock.get(ck, 0) >= v:
                    continue
                clock[ck] = v
                waits[i].append((key, v))
        return marked, waits

    def emit(self, block, sems):
        marked, waits = self.lower()
        per_stream = {}
        for i, o in enumerate(self.ops):
            per_stream.setdefault(o[0], []).append(i)
        builders = {"sp": block.sync, "act": block.scalar, "dve": block.vector,
                    "pool": block.gpsimd, "pe": block.tensor}
        for stream, ids in per_stream.items():
            def body(eng, ids=ids, stream=stream):
                for i in ids:
                    o = self.ops[i]
                    if o[1] == "d":
                        skey = (stream, "s", self.slot[i])
                        if self.use[i] > 0:
                            eng.wait_ge(sems[skey], 16 * self.use[i])
                    for (key, v) in waits[i]:
                        eng.wait_ge(sems[key], v)
                    ins = o[2](eng)
                    if o[1] == "c":
                        if marked[i]:
                            ins.then_inc(sems[(stream, "c")], 1)
                    elif o[1] == "d":
                        ins.then_inc(sems[(stream, "s", self.slot[i])], 16)
                    else:
                        ins.then_inc(sems[("cc",)], 1)
            builders[stream](body)


class _StopBuild(Exception):
    pass


def build_program(dbg=False, serialize=False):
    import os as _os0
    KSTOP = int(_os0.environ.get("KSTOP", "99"))
    NOMODF = bool(_os0.environ.get("KNOMODF"))
    NBUF = int(_os0.environ.get("KNBUF", "2"))
    Sched.NSLOT = int(_os0.environ.get("KNSLOT", "8"))
    nc = bass.Bass("TRN2", target_bir_lowering=False)

    def ein(name, shape, dt=F32):
        return nc.dram_tensor(name, shape, dt, kind="ExternalInput").ap()

    xT = ein("xT", [NCH, P, T])
    w_in = ein("w_in", [OQKV, NCH, P, P])
    b_in_c = ein("b_in_c", [P, OQKV])
    w_out = ein("w_out", [NCH, NCH, P, P])
    b_out_c = ein("b_out_c", [P, NCH])
    gam_c = ein("gam_c", [P, NCH])
    bet_c = ein("bet_c", [P, NCH])
    wg_t = ein("wg_t", [NSUP, NCH, P, OSUP])
    wu_t = ein("wu_t", [NSUP, NCH, P, OSUP])
    wd_t = ein("wd_t", [NCH, 2, 32, P, P])
    wmod_sh = ein("wmod_sh", [12, NCH, P, P])
    cT = ein("cT", [NCH, P, 2])
    costab = ein("costab", [8, P, T], BF16)
    sintab = ein("sintab", [8, P, T], BF16)
    h128 = ein("h128", [P, P], BF16)
    h64 = ein("h64", [64, 64], BF16)
    ident = ein("ident", [P, P])
    onescol = ein("onescol", [P, 1], BF16)
    bchot = ein("bchot", [8, 8, P], BF16)
    constv = ein("constv", [P, 4])
    bmask = ein("bmask", [P, 2])
    amask = ein("amask", [P, 8])
    msh = ein("msh", [3, NCH, P, T])

    outT = nc.dram_tensor("outT", [NCH, P, T], F32, kind="ExternalOutput").ap()
    dbg_o = {}
    dbg_tags = []
    if dbg:
        for nm, shp in [("d_h", [NCH, P, T]), ("d_xn", [NCH, P, T]),
                        ("d_kv", [NCH, P, T]), ("d_acol", [P, 16]),
                        ("d_x1", [NCH, P, T]), ("d_q1", [NCH, P, T]),
                        ("d_mod", [P, 96]), ("d_mrow", [1, 16]),
                        ("d_q2", [P, 4 * T]), ("d_inter", [P, T]),
                        ("d_m1c", [P, 16]), ("d_s1b", [P, T]),
                        ("d_modsh", [P, 24]), ("d_silu", [P, NCH * 2]),
                        ("d_agp", [P, 20]), ("d_ct", [P, NCH * 2]),
                        ("d_siluf", [P, NCH * 2]), ("d_wmb", [P, NCH * P])]:
            dbg_o[nm] = nc.dram_tensor(nm, shp, F32, kind="ExternalOutput").ap()
            dbg_tags.append(nm)

    qd = nc.dram_tensor("qd", [NCH, P, T], BF16).ap()
    x1d = nc.dram_tensor("x1d", [NCH, P, T], F32).ap()
    pdt = nc.dram_tensor("pdt", [P, FCH, NTC, TC], BF16).ap()
    rowd = nc.dram_tensor("rowd", [16, T], F32).ap()
    ag1i = nc.dram_tensor("ag1i", [P, 24], F32).ap()
    ag1o = nc.dram_tensor("ag1o", [NCORES * P, 24], F32).ap()
    ag2i = nc.dram_tensor("ag2i", [P, 20], F32).ap()
    ag2o = nc.dram_tensor("ag2o", [NCORES * P, 20], F32).ap()
    agfi = nc.dram_tensor("agfi", [P, 4], F32).ap()
    agf1o = nc.dram_tensor("agf1o", [NCORES * P, 4], F32).ap()
    agf2o = nc.dram_tensor("agf2o", [NCORES * P, 4], F32).ap()

    S = Sched(serialize=serialize)

    with contextlib.ExitStack() as ST:
        sems = {}
        for e in ENGINES:
            sems[(e, "c")] = ST.enter_context(nc.semaphore(f"s_{e}_c"))
        for e in ("sp", "pool", "act"):
            for sl in range(Sched.NSLOT):
                sems[(e, "s", sl)] = ST.enter_context(nc.semaphore(f"s_{e}_{sl}"))
        sems[("cc",)] = ST.enter_context(nc.semaphore("s_cc"))
        block = ST.enter_context(nc.Block())

        def sb(stack, name, shape, dt=F32):
            return stack.enter_context(nc.sbuf_tensor(name, shape, dt))

        t_binc = sb(ST, "t_binc", [P, OQKV])
        t_boutc = sb(ST, "t_boutc", [P, NCH])
        t_gam = sb(ST, "t_gam", [P, NCH])
        t_bet = sb(ST, "t_bet", [P, NCH])
        t_h128 = sb(ST, "t_h128", [P, P], BF16)
        t_h64 = sb(ST, "t_h64", [64, 64], BF16)
        t_ident = sb(ST, "t_ident", [P, P])
        t_ones = sb(ST, "t_ones", [P, 1], BF16)
        t_bchot = sb(ST, "t_bchot", [8, 8 * P], BF16)
        t_const = sb(ST, "t_const", [P, 4])
        t_bmask = sb(ST, "t_bmask", [P, 2])
        t_amask = sb(ST, "t_amask", [P, 8])
        t_mod = sb(ST, "t_mod", [P, 96])
        t_a1 = sb(ST, "t_a1", [P, NCH])
        t_a3 = sb(ST, "t_a3", [P, NCH])
        t_acol = sb(ST, "t_acol", [P, NCH])
        t_agp = sb(ST, "t_agp", [P, 20])
        t_mrow = sb(ST, "t_mrow", [1, 16])
        t_scol = sb(ST, "t_scol", [P, 8])
        t_colst = sb(ST, "t_colst", [P, 16])
        t_bchh = sb(ST, "t_bchh", [8, P], BF16)
        t_bchl = sb(ST, "t_bchl", [8, P], BF16)
        t_gmx = sb(ST, "t_gmx", [P, 176])
        t_rs = sb(ST, "t_rs", [P, T], F32)
        t_wb = sb(ST, "t_wb", [P, T])
        t_mb = sb(ST, "t_mb", [P, T])
        t_rb = sb(ST, "t_rb", [P, T])
        t_s1b = sb(ST, "t_s1b", [P, T])
        q1i = sb(ST, "q1i", [P, T], I8)

        nc.const_aps.register(F32, EPS, t_const[:, 0:1])
        nc.const_aps.register(F32, 1.0, t_const[:, 1:2])
        nc.const_aps.register(F32, 0.0, t_const[:, 2:3])

        ps_a = ST.enter_context(nc.psum_tensor("ps_a", [P, T], F32))
        ps_b = ST.enter_context(nc.psum_tensor("ps_b", [P, T], F32))
        ps_p = ST.enter_context(nc.psum_tensor("ps_p", [P, T], F32))
        ps_tr = ST.enter_context(nc.psum_tensor("ps_tr", [P, P], F32))

        def dma(stream, dst, src, reads, writes):
            S.d(stream, lambda e, dst=dst, src=src: e.dma_start(out=dst, in_=src),
                reads=list(reads), writes=list(writes))

        t_rs2 = sb(ST, "t_rs2", [P, T], F32)
        t_min = sb(ST, "t_min", [P, T], F32)
        t_xbf = sb(ST, "t_xbf", [P, T], BF16)
        t_sqs = sb(ST, "t_sqs", [P, T], BF16)
        ystack = contextlib.ExitStack()
        xnb = sb(ystack, "xnb", [P, NCH * T], BF16)
        xnv = xnb[:].rearrange("p (c t) -> p c t", t=T)
        xstack = contextlib.ExitStack()
        xbuf = sb(xstack, "xbuf", [P, NCH * T], F32)
        xv = xbuf[:].rearrange("p (c t) -> p c t", t=T)

        dma("sp", xbuf[:].rearrange("p (c t) -> p c t", t=T),
            xT.rearrange("c p t -> p c t"), [], ["x"])
        for src, dst, tag in [(b_in_c, t_binc, "binc"), (b_out_c, t_boutc, "boutc"),
                              (gam_c, t_gam, "gam"), (bet_c, t_bet, "bet"),
                              (ident, t_ident, "ident"), (constv, t_const, "const"),
                              (bmask, t_bmask, "bmask"), (amask, t_amask, "amask")]:
            dma("sp", dst[:], src, [], [tag])
        for src, dst, tag in [(h128, t_h128, "h128"), (h64, t_h64, "h64"),
                              (onescol, t_ones, "ones")]:
            dma("pool", dst[:], src, [], [tag])
        dma("pool", t_bchot[:].rearrange("k (j p) -> k j p", p=P),
            bchot.rearrange("j k p -> k j p"), [], ["bchot"])

        mstack = contextlib.ExitStack()
        t_ct = sb(mstack, "t_ct", [P, NCH * 2])
        t_sbf = sb(mstack, "t_sbf", [P, NCH * 2], BF16)
        t_slb = sb(mstack, "t_slb", [P, NCH * 2], BF16)
        t_siluf = sb(mstack, "t_siluf", [P, NCH * 2])
        t_slo = sb(mstack, "t_slo", [P, NCH * 2])
        wmb = [sb(mstack, f"wmb{i}", [P, NCH * P], BF16) for i in range(NBUF)]
        wmf = sb(mstack, "wmf", [P, NCH * P], F32)
        wml = sb(mstack, "wml", [P, NCH * P], BF16)
        t_modsh = sb(mstack, "t_modsh", [P, 24])

        dma("sp", t_ct[:].rearrange("p (c b) -> p c b", b=2),
            cT.rearrange("c p b -> p c b"), [], ["ct"])
        S.c("act", lambda e: e.activation(out=t_siluf[:], in_=t_ct[:], func=AF.Sigmoid),
            reads=["ct", "const"], writes=["siluf"])
        S.c("dve", lambda e: e.tensor_tensor(out=t_siluf[:], in0=t_siluf[:],
                                             in1=t_ct[:], op=ALU.mult),
            reads=["siluf", "ct"], writes=["siluf"])
        S.c("dve", lambda e: e.tensor_copy(out=t_sbf[:], in_=t_siluf[:]),
            reads=["siluf"], writes=["sbf"])
        S.c("dve", lambda e: e.tensor_tensor(out=t_slo[:], in0=t_siluf[:],
                                             in1=t_sbf[:], op=ALU.subtract),
            reads=["siluf", "sbf"], writes=["slo"])
        S.c("dve", lambda e: e.tensor_copy(out=t_slb[:], in_=t_slo[:]),
            reads=["slo"], writes=["slb"])

        MODF = () if NOMODF else tuple(range(12))
        for i in range(12):
            bi = i % NBUF
            dma("pool", wmb[bi][:].rearrange("p (c m) -> p c m", m=P),
                wmod_sh[i].rearrange("c p m -> p c m"), [], [f"wmb{bi}"])
            extra = []
            if i in MODF:
                dma("pool", wmf[:].rearrange("p (c m) -> p c m", m=P),
                    wmod_sh[i].rearrange("c p m -> p c m"), [], ["wmf"])
                S.c("dve", lambda e, bi=bi: e.tensor_tensor(
                    out=wml[:], in0=wmf[:], in1=wmb[bi][:], op=ALU.subtract),
                    reads=["wmf", f"wmb{bi}"], writes=["wml"])
                extra = ["wml"]

            def mod_mm(e, i=i, bi=bi):
                for c in range(NCH):
                    e.matmul(ps_tr[:, 0:2], wmb[bi][:, c * P:(c + 1) * P],
                             t_sbf[:, 2 * c:2 * c + 2], start=(c == 0),
                             stop=False)
                    ins = e.matmul(ps_tr[:, 2:4], wmb[bi][:, c * P:(c + 1) * P],
                                   t_slb[:, 2 * c:2 * c + 2], start=False,
                                   stop=False)
                if i in MODF:
                    for c in range(NCH):
                        ins = e.matmul(ps_tr[:, 4:6], wml[:, c * P:(c + 1) * P],
                                       t_sbf[:, 2 * c:2 * c + 2], start=False,
                                       stop=(c == NCH - 1))
                else:
                    ins = e.matmul(ps_tr[:, 2:4], wmb[bi][:, 0:P],
                                   t_slb[:, 0:2], start=False, stop=True) if False else ins
                return ins
            S.c("pe", mod_mm, reads=[f"wmb{bi}", "sbf", "slb"] + extra, writes=["ps_tr"])

            S.c("dve", lambda e: e.tensor_copy(out=t_rs[:, 0:6], in_=ps_tr[:, 0:6]),
                reads=["ps_tr"], writes=["t_rs", "ps_tr"])

            def mod_ev(e, i=i):
                a0 = t_modsh[:, i:i + 1]
                a1 = t_modsh[:, 12 + i:13 + i]
                e.tensor_add(out=a0, in0=t_rs[:, 0:1], in1=t_rs[:, 2:3])
                ins = e.tensor_add(out=a1, in0=t_rs[:, 1:2], in1=t_rs[:, 3:4])
                if i in MODF:
                    e.tensor_add(out=a0, in0=a0, in1=t_rs[:, 4:5])
                    return e.tensor_add(out=a1, in0=a1, in1=t_rs[:, 5:6])
                return ins
            S.c("dve", mod_ev, reads=["t_rs"], writes=["modsh"])

        if dbg:
            dma("sp", dbg_o["d_ct"], t_ct[:], ["ct", "siluf"], ["d_ct"])
            dma("sp", dbg_o["d_siluf"], t_siluf[:], ["siluf", "silu"], ["d_siluf"])
            dma("pool", dbg_o["d_wmb"], wmb[1][:], ["wmb1", "modsh"], ["d_wmb"])
            dma("sp", dbg_o["d_modsh"], t_modsh[:], ["modsh"], ["d_modsh"])
            dma("pool", dbg_o["d_silu"], t_sbf[:], ["sbf", "slb"], ["d_silu"])
        dma("pool", ag1i[:, 0:24], t_modsh[:], ["modsh"], ["ag1i"])
        import os as _os
        if _os.environ.get("KNOCC1"):
            for _r in range(NCORES):
                dma("pool", ag1o[_r * P:(_r + 1) * P, :], ag1i[:, :],
                    ["ag1i"], ["ag1o"])
        else:
            S.op("pool", "cc", lambda e: e.collective_compute(
                "AllGather", ALU.bypass, replica_groups=[list(range(NCORES))],
                ins=[ag1i.opt()], outs=[ag1o.opt()]), reads=["ag1i"], writes=["ag1o"])
            S.op("pool", "cc", lambda e: e.collective_compute(
                "AllGather", ALU.bypass, replica_groups=[list(range(NCORES))],
                ins=[agfi.opt()], outs=[agf1o.opt()]),
                reads=["ag1o"], writes=["ag1f"])
        src_m = ag1o.rearrange("(r p) x -> p r x", p=P)
        dma("pool", t_rs[:, 0:96].rearrange("p (r c) -> p r c", c=12),
            src_m[:, :, 0:12], ["ag1o", "ag1f"], ["mg0", "t_rs"])
        dma("pool", t_mb[:, 0:96].rearrange("p (r c) -> p r c", c=12),
            src_m[:, :, 12:24], ["ag1o", "ag1f"], ["mbv"])

        S.c("dve", lambda e: e.tensor_scalar(out=t_mod[:], in0=t_rs[:, 0:96],
                                             scalar1=t_bmask[:, 0:1], scalar2=None,
                                             op0=ALU.mult),
            reads=["mg0", "bmask"], writes=["mod"])
        S.c("dve", lambda e: e.tensor_scalar(out=t_rs[:, 96:192], in0=t_mb[:, 0:96],
                                             scalar1=t_bmask[:, 1:2], scalar2=None,
                                             op0=ALU.mult),
            reads=["mbv", "bmask"], writes=["t_rs", "mbv"])
        S.c("dve", lambda e: e.tensor_tensor(out=t_mod[:], in0=t_mod[:],
                                             in1=t_rs[:, 96:192], op=ALU.add),
            reads=["mod", "t_rs"], writes=["mod"])
        for base in (32, 80):
            gaap = t_mod[:, base:base + 16]
            S.c("dve", lambda e, g=gaap: e.tensor_scalar(
                out=t_rs[:, 0:16], in0=g, scalar1=0.0, scalar2=None, op0=ALU.is_ge),
                reads=["mod"], writes=["t_rs"])
            S.c("dve", lambda e: e.tensor_scalar(
                out=t_rs[:, 16:32], in0=t_rs[:, 0:16], scalar1=2.0, scalar2=-1.0,
                op0=ALU.mult, op1=ALU.add), reads=["t_rs"], writes=["t_rs"])
            S.c("dve", lambda e, g=gaap: e.tensor_tensor(
                out=t_rs[:, 32:48], in0=g, in1=t_rs[:, 16:32], op=ALU.mult),
                reads=["mod", "t_rs"], writes=["t_rs"])
            S.c("dve", lambda e: e.tensor_scalar(
                out=t_rs[:, 48:64], in0=t_rs[:, 32:48], scalar1=0.25, scalar2=None,
                op0=ALU.max), reads=["t_rs"], writes=["t_rs"])
            S.c("dve", lambda e, g=gaap: e.tensor_tensor(
                out=g, in0=t_rs[:, 48:64], in1=t_rs[:, 16:32], op=ALU.mult),
                reads=["t_rs"], writes=["mod"])
        S.c("dve", lambda e: e.tensor_scalar(out=t_a1[:], in0=t_mod[:, 16:32],
                                             scalar1=1.0, scalar2=None, op0=ALU.add),
            reads=["mod"], writes=["a1"])
        S.c("dve", lambda e: e.tensor_scalar(out=t_a3[:], in0=t_mod[:, 64:80],
                                             scalar1=1.0, scalar2=None, op0=ALU.add),
            reads=["mod"], writes=["a3"])
        if dbg:
            dma("sp", dbg_o["d_mod"], t_mod[:], ["mod"], ["d_mod"])
        mstack.close()

        if KSTOP < 1:
            S.frozen = True
        def stats_pass(srcv, src_tag_fn):
            for c in range(NCH):
                S.c("dve", lambda e, c=c: e.tensor_copy(out=t_xbf[:], in_=srcv[:, c, :]),
                    reads=[src_tag_fn(c)], writes=["xbf"])
                S.c("act", lambda e, c=c: e.activation(out=t_sqs[:], in_=srcv[:, c, :],
                                                       func=AF.Square),
                    reads=[src_tag_fn(c)], writes=["sqs"])

                def mm(e, c=c):
                    for hh in range(2):
                        e.matmul(ps_p[0:1, hh * 512:(hh + 1) * 512], t_ones[:],
                                 t_xbf[:, hh * 512:(hh + 1) * 512],
                                 start=(c == 0), stop=False)
                        ins = e.matmul(ps_p[64:65, hh * 512:(hh + 1) * 512], t_ones[:],
                                       t_sqs[:, hh * 512:(hh + 1) * 512],
                                       start=(c == 0), stop=(c == NCH - 1))
                    return ins
                S.c("pe", mm, reads=["xbf", "sqs", "ones"],
                    writes=["ps_rows", "xbf", "sqs"])

        def ln_rows_and_bcast(mtile, mtag, rtile, rtag):
            def rmath1(e):
                e.tensor_scalar(out=mtile[0:1, 0:T], in0=ps_p[0:1, :], scalar1=INVD,
                                scalar2=None, op0=ALU.mult)
                return e.tensor_scalar(out=rtile[0:1, 0:T], in0=ps_p[64:65, :],
                                       scalar1=INVD, scalar2=None, op0=ALU.mult)
            S.c("dve", rmath1, reads=["ps_rows"], writes=[mtag, rtag, "ps_rows"])

            def rmath2(e):
                m = mtile[0:1, 0:T]
                v = rtile[0:1, 0:T]
                e.tensor_tensor(out=t_rs[0:1, 0:T], in0=m, in1=m, op=ALU.mult)
                return e.tensor_tensor(out=v, in0=v, in1=t_rs[0:1, 0:T], op=ALU.subtract)
            S.c("dve", rmath2, reads=[mtag, rtag], writes=[rtag, "t_rs"])
            S.c("act", lambda e: e.activation(out=t_rs[0:1, 0:T],
                                              in_=rtile[0:1, 0:T],
                                              func=AF.Sqrt, bias=EPS),
                reads=[rtag, "const"], writes=["t_rs"])
            S.c("dve", lambda e: e.reciprocal(out=rtile[0:1, 0:T],
                                              in_=t_rs[0:1, 0:T]),
                reads=["t_rs"], writes=[rtag])
            dma("sp", rowd[0:1, :], mtile[0:1, 0:T], [mtag], ["rowd01"])
            dma("sp", rowd[1:2, :], rtile[0:1, 0:T], [rtag], ["rowd01"])
            dma("sp", mtile[:], rowd[0:1, 0:T].partition_broadcast(P),
                ["rowd01"], [mtag])
            dma("sp", rtile[:], rowd[1:2, 0:T].partition_broadcast(P),
                ["rowd01"], [rtag])

        def col_bcast(srccols, nblk, dst_fn, dsttag, rtag):
            S.c("pe", lambda e: e.transpose(ps_tr[0:nblk, :], srccols, t_ident[:]),
                reads=[rtag, "ident"], writes=["ps_tr"])

            S.c("dve", lambda e: e.tensor_copy(out=t_bchh[:], in_=ps_tr[0:8, :]),
                reads=["ps_tr"], writes=["bch"])
            S.c("dve", lambda e: e.tensor_tensor(out=t_rs[0:8, 0:P],
                                                 in0=ps_tr[0:8, :], in1=t_bchh[:],
                                                 op=ALU.subtract),
                reads=["ps_tr", "bch"], writes=["t_rs", "ps_tr"])
            S.c("dve", lambda e: e.tensor_copy(out=t_bchl[:], in_=t_rs[0:8, 0:P]),
                reads=["t_rs"], writes=["bch"])
            for j in range(nblk):
                def bmm(e, j=j):
                    e.matmul(ps_tr[:, 0:P], t_bchot[:, j * P:(j + 1) * P], t_bchh[:],
                             start=True, stop=False)
                    return e.matmul(ps_tr[:, 0:P], t_bchot[:, j * P:(j + 1) * P],
                                    t_bchl[:], start=False, stop=True)
                S.c("pe", bmm, reads=["bch", "bchot"], writes=["ps_tr"])
                S.c("dve", lambda e, j=j: e.tensor_copy(out=dst_fn(j), in_=ps_tr[:, 0:P]),
                    reads=["ps_tr"], writes=[dsttag, "ps_tr"])

        def tree_to_cols(nblk, srcmax, srcmin, rtags):
            for j in range(nblk):
                S.c("pe", lambda e, j=j: e.transpose(
                    ps_tr[:], srcmax[:, j * P:(j + 1) * P], t_ident[:]),
                    reads=rtags + ["ident"], writes=["ps_tr"])
                S.c("dve", lambda e, j=j: e.tensor_reduce(
                    out=t_colst[:, 8 + j:9 + j], in_=ps_tr[:], axis=AX.X, op=ALU.max),
                    reads=["ps_tr"], writes=["colv", "ps_tr"])
            for j in range(nblk):
                S.c("pe", lambda e, j=j: e.transpose(
                    ps_tr[:], srcmin[:, j * P:(j + 1) * P], t_ident[:]),
                    reads=rtags + ["ident"], writes=["ps_tr"])

                S.c("dve", lambda e: e.tensor_reduce(out=t_rs[:, 0:1], in_=ps_tr[:],
                                                     axis=AX.X, op=ALU.min),
                    reads=["ps_tr"], writes=["t_rs", "ps_tr"])

                def mr(e, j=j):
                    e.tensor_scalar(out=t_rs[:, 0:1], in0=t_rs[:, 0:1], scalar1=-1.0,
                                    scalar2=None, op0=ALU.mult)
                    return e.tensor_tensor(out=t_colst[:, 8 + j:9 + j],
                                           in0=t_colst[:, 8 + j:9 + j],
                                           in1=t_rs[:, 0:1], op=ALU.max)
                S.c("dve", mr, reads=["t_rs"], writes=["colv", "t_rs"])

        def ln_apply_ops(srctag, srcslice, acol, bcol, mtile, mtag, rtile, rtag,
                         outslice, outtags, extra_reads=()):
            S.c("dve", lambda e: e.tensor_tensor(out=t_rs[:], in0=srcslice,
                                                 in1=mtile[:], op=ALU.subtract),
                reads=[srctag, mtag] + list(extra_reads), writes=["t_rs"])
            S.c("dve", lambda e: e.scalar_tensor_tensor(
                out=t_rs[:], in0=t_rs[:], scalar=acol, in1=rtile[:],
                op0=ALU.mult, op1=ALU.mult),
                reads=["t_rs", rtag, "mod", "a1", "a3", "gam"], writes=["t_rs"])
            S.c("dve", lambda e: e.tensor_scalar(out=outslice, in0=t_rs[:],
                                                 scalar1=bcol, scalar2=None,
                                                 op0=ALU.add),
                reads=["t_rs", "mod", "bet"], writes=list(outtags))

        stats_pass(xv, lambda c: "x")
        ln_rows_and_bcast(t_mb, "mbv", t_rb, "rbv")
        for c in range(NCH):
            ln_apply_ops("x", xv[:, c, :], t_a1[:, c:c + 1], t_mod[:, c:c + 1],
                         t_mb, "mbv", t_rb, "rbv", t_rs2[:], ["t_rs2"])
            S.c("dve", lambda e: e.tensor_copy(out=t_xbf[:], in_=t_rs2[:]),
                reads=["t_rs2"], writes=["xbf"])
            S.c("act", lambda e: e.activation(out=t_sqs[:], in_=t_rs2[:], func=AF.Square),
                reads=["t_rs2"], writes=["sqs"])

            def hmm(e, c=c):
                for hh in range(2):
                    e.matmul(ps_p[0:1, hh * 512:(hh + 1) * 512], t_ones[:],
                             t_xbf[:, hh * 512:(hh + 1) * 512],
                             start=(c == 0), stop=False)
                    ins = e.matmul(ps_p[64:65, hh * 512:(hh + 1) * 512], t_ones[:],
                                   t_sqs[:, hh * 512:(hh + 1) * 512],
                                   start=(c == 0), stop=(c == NCH - 1))
                return ins
            S.c("pe", hmm, reads=["xbf", "sqs", "ones"],
                writes=["ps_rows", "xbf", "sqs", "t_rs2"])
        ln_rows_and_bcast(t_s1b, "m2b", t_wb, "r2b")
        for c in range(NCH):
            ln_apply_ops("x", xv[:, c, :], t_a1[:, c:c + 1], t_mod[:, c:c + 1],
                         t_mb, "mbv", t_rb, "rbv", t_rs2[:], ["t_rs2"])
            ln_apply_ops("t_rs2", t_rs2[:], t_gam[:, c:c + 1], t_bet[:, c:c + 1],
                         t_s1b, "m2b", t_wb, "r2b", xnv[:, c, :], [f"xn{c}"])
            S.c("dve", lambda e, c=c: e.tensor_scalar(
                out=t_rs[:], in0=t_rs2[:], scalar1=t_mod[:, 32 + c:33 + c],
                scalar2=None, op0=ALU.mult),
                reads=["t_rs2", "mod"], writes=["t_rs"])
            S.c("dve", lambda e, c=c: e.tensor_tensor(
                out=xv[:, c, :], in0=xv[:, c, :], in1=t_rs[:], op=ALU.add),
                reads=["t_rs", "x"], writes=["x"])
        if dbg:
            dma("pool", dbg_o["d_xn"].rearrange("c p t -> p c t"), xnv,
                [f"xn{c}" for c in range(NCH)], ["d_xn"])
            dma("sp", dbg_o["d_h"].rearrange("c p t -> p c t"), xv,
                ["x"], ["d_h"])

        if KSTOP < 2:
            S.frozen = True
        p2a = contextlib.ExitStack()
        krotb = sb(p2a, "krotb", [P, NCH * T], BF16)
        krv = krotb[:].rearrange("p (c t) -> p c t", t=T)
        p2b = contextlib.ExitStack()
        winb = [sb(p2b, f"winb{i}", [P, NCH * P], BF16) for i in range(2)]
        t_cos = [sb(p2b, f"tcos{i}", [P, T], BF16) for i in range(2)]
        t_sin = [sb(p2b, f"tsin{i}", [P, T], BF16) for i in range(2)]
        t_kc = [sb(p2b, f"tkc{i}", [P, T], F32) for i in range(2)]

        for o in range(OQKV):
            bi = o % 2
            dma("pool", winb[bi][:].rearrange("p (c m) -> p c m", m=P),
                w_in[o].rearrange("c p m -> p c m"), [], [f"winb{bi}"])
            pstag = "ps_a" if o % 2 == 0 else "ps_b"
            pb = ps_a if o % 2 == 0 else ps_b

            def qkv_mm(e, bi=bi, pb=pb):
                for c in range(NCH):
                    for hh in range(2):
                        ins = e.matmul(pb[:, hh * 512:(hh + 1) * 512],
                                       winb[bi][:, c * P:(c + 1) * P],
                                       xnv[:, c, hh * 512:(hh + 1) * 512],
                                       start=(c == 0), stop=(c == NCH - 1))
                return ins
            S.c("pe", qkv_mm, reads=[f"winb{bi}"] + [f"xn{c}" for c in range(NCH)],
                writes=[pstag])

            if o < 16:
                S.c("act", lambda e, o=o, pb=pb: e.activation(
                    out=t_xbf[:], in_=pb[:], func=AF.Identity,
                    bias=t_binc[:, o:o + 1], scale=1.0),
                    reads=[pstag, "binc"], writes=[pstag, "xbf"])
                dma("sp", qd[o], t_xbf[:], ["xbf"], [f"qd{o}"])
            else:
                kvi = o % 2
                cch = (o - 16) % NCH
                isK = o < 32
                S.c("act", lambda e, o=o, pb=pb, kvi=kvi: e.activation(
                    out=t_kc[kvi][:], in_=pb[:], func=AF.Identity,
                    bias=t_binc[:, o:o + 1], scale=1.0),
                    reads=[pstag, "binc"], writes=[f"kc{kvi}", pstag])
                S.c("act", lambda e, kvi=kvi: e.activation(
                    out=t_sqs[:], in_=t_kc[kvi][:], func=AF.Square),
                    reads=[f"kc{kvi}"], writes=["sqs"])

                def nmm(e, o=o):
                    pr = 0 if o < 32 else 64
                    for hh in range(2):
                        ins = e.matmul(ps_p[pr:pr + 1, hh * 512:(hh + 1) * 512],
                                       t_ones[:], t_sqs[:, hh * 512:(hh + 1) * 512],
                                       start=(o in (16, 32)), stop=(o == 47))
                    return ins
                S.c("pe", nmm, reads=["sqs", "ones"], writes=["ps_rows", "sqs"])

                if cch < 8:
                    ci = cch % 2
                    dma("sp", t_cos[ci][:], costab[cch], [], [f"cos{ci}"])
                    dma("sp", t_sin[ci][:], sintab[cch], [], [f"sin{ci}"])

                    mask = [ii + 1 if ii % 2 == 0 else ii - 1 for ii in range(32)]
                    S.c("dve", lambda e, kvi=kvi, mask=mask: e.stream_shuffle(
                        out=t_rs[:], in_=t_kc[kvi][:], mask=mask),
                        reads=[f"kc{kvi}"], writes=["t_rs"])
                    S.c("dve", lambda e, ci=ci: e.tensor_tensor(
                        out=t_rs[:], in0=t_rs[:], in1=t_sin[ci][:], op=ALU.mult),
                        reads=["t_rs", f"sin{ci}"], writes=["t_rs"])
                    S.c("dve", lambda e, kvi=kvi, ci=ci: e.tensor_tensor(
                        out=t_rs2[:], in0=t_kc[kvi][:], in1=t_cos[ci][:], op=ALU.mult),
                        reads=[f"kc{kvi}", f"cos{ci}"], writes=["t_rs2"])
                    if isK:
                        S.c("dve", lambda e, cch=cch: e.tensor_tensor(
                            out=krv[:, cch, :], in0=t_rs[:], in1=t_rs2[:], op=ALU.add),
                            reads=["t_rs", "t_rs2"],
                            writes=[f"kr{cch}", f"kc{kvi}"])
                    else:
                        S.c("dve", lambda e: e.tensor_tensor(
                            out=t_rs[:], in0=t_rs[:], in1=t_rs2[:], op=ALU.add),
                            reads=["t_rs", "t_rs2"], writes=["t_rs"])
                        S.c("dve", lambda e, cch=cch: e.tensor_tensor(
                            out=krv[:, cch, :], in0=t_rs[:], in1=krv[:, cch, :],
                            op=ALU.mult),
                            reads=["t_rs", f"kr{cch}"],
                            writes=[f"kr{cch}", f"kc{kvi}"])
                else:
                    if isK:
                        S.c("dve", lambda e, kvi=kvi, cch=cch: e.tensor_copy(
                            out=krv[:, cch, :], in_=t_kc[kvi][:]),
                            reads=[f"kc{kvi}"], writes=[f"kr{cch}", f"kc{kvi}"])
                    else:
                        S.c("dve", lambda e, kvi=kvi, cch=cch: e.tensor_tensor(
                            out=krv[:, cch, :], in0=t_kc[kvi][:], in1=krv[:, cch, :],
                            op=ALU.mult),
                            reads=[f"kc{kvi}", f"kr{cch}"],
                            writes=[f"kr{cch}", f"kc{kvi}"])

        S.c("dve", lambda e: e.tensor_copy(out=t_wb[0:1, 0:T], in_=ps_p[0:1, :]),
            reads=["ps_rows"], writes=["wb", "r2b"])
        S.c("dve", lambda e: e.tensor_tensor(out=t_wb[0:1, 0:T], in0=t_wb[0:1, 0:T],
                                             in1=ps_p[64:65, :], op=ALU.mult),
            reads=["ps_rows", "wb"], writes=["wb", "ps_rows"])
        S.c("act", lambda e: e.activation(out=t_rs[0:1, 0:T],
                                          in_=t_wb[0:1, 0:T], func=AF.Sqrt),
            reads=["wb"], writes=["t_rs"])
        S.c("dve", lambda e: e.reciprocal(out=t_wb[0:1, 0:T],
                                          in_=t_rs[0:1, 0:T]),
            reads=["t_rs"], writes=["wb"])
        dma("sp", rowd[4:5, :], t_wb[0:1, 0:T], ["wb"], ["rowd4"])
        dma("sp", t_wb[:], rowd[4:5, :].partition_broadcast(P),
            ["rowd4"], ["wb"])

        for c in range(NCH):
            def apart(e, c=c):
                e.tensor_tensor(out=t_rs[:], in0=krv[:, c, :], in1=t_wb[:], op=ALU.mult)
                return e.tensor_reduce(out=t_agp[:, c:c + 1], in_=t_rs[:], axis=AX.X,
                                       op=ALU.add)
            S.c("dve", apart, reads=[f"kr{c}", "wb"], writes=["agp", "t_rs"])
        if dbg:
            dma("pool", dbg_o["d_kv"].rearrange("c p t -> p c t"), krv,
                [f"kr{c}" for c in range(NCH)], ["d_kv"])

        S.fence(["sb1", "m2b"], ["mshfence"])
        for widx in range(3):
            for cs in range(NCH):
                dma("sp", t_s1b[:], msh[widx, cs], ["mshfence", "mshabs"],
                    ["sb1", "m2b"])
                S.c("act", lambda e: e.activation(out=t_rs2[:], in_=t_s1b[:], func=AF.Abs),
                    reads=["sb1"], writes=["mshabs", "t_rs2"])

                S.c("dve", lambda e: e.tensor_reduce(out=t_colst[:, 1:2], in_=t_rs2[:],
                                                     axis=AX.X, op=ALU.add),
                    reads=["mshabs"], writes=["colv"])

                def mred(e, widx=widx, cs=cs):
                    if cs == 0:
                        return e.tensor_copy(out=t_agp[:, 16 + widx:17 + widx],
                                             in_=t_colst[:, 1:2])
                    return e.tensor_tensor(out=t_agp[:, 16 + widx:17 + widx],
                                           in0=t_agp[:, 16 + widx:17 + widx],
                                           in1=t_colst[:, 1:2], op=ALU.add)
                S.c("dve", mred, reads=["colv"], writes=["agp"])

        if dbg:
            dma("sp", dbg_o["d_agp"], t_agp[:], ["agp"], ["d_agp"])
        dma("pool", ag2i[:, 0:20], t_agp[:], ["agp"], ["ag2i"])
        import os as _os
        if _os.environ.get("KNOCC2"):
            for _r in range(NCORES):
                dma("pool", ag2o[_r * P:(_r + 1) * P, :], ag2i[:, :],
                    ["ag2i"], ["ag2o"])
        else:
            S.op("pool", "cc", lambda e: e.collective_compute(
                "AllGather", ALU.bypass, replica_groups=[list(range(NCORES))],
                ins=[ag2i.opt()], outs=[ag2o.opt()]), reads=["ag2i"], writes=["ag2o"])
            S.op("pool", "cc", lambda e: e.collective_compute(
                "AllGather", ALU.bypass, replica_groups=[list(range(NCORES))],
                ins=[agfi.opt()], outs=[agf2o.opt()]),
                reads=["ag2o"], writes=["ag2f"])
        dma("pool", t_gmx[:, 0:160].rearrange("p (r x) -> p r x", x=20),
            ag2o.rearrange("(r p) x -> p r x", p=P), ["ag2o", "ag2f"], ["gmx"])

        blocks = t_gmx[:, 0:160].rearrange("p (r x) -> p r x", x=20)
        for r in range(NCORES):
            if r == 0:
                S.c("dve", lambda e, r=r: e.tensor_scalar(
                    out=t_rs[:, 0:20], in0=blocks[:, r, :],
                    scalar1=t_amask[:, r:r + 1], scalar2=None, op0=ALU.mult),
                    reads=["gmx", "amask"], writes=["t_rs"])
                S.c("dve", lambda e, r=r: e.tensor_copy(out=t_rs[:, 32:52],
                                                        in_=blocks[:, r, :]),
                    reads=["gmx"], writes=["t_rs"])
            else:
                S.c("dve", lambda e, r=r: e.tensor_scalar(
                    out=t_rs[:, 64:84], in0=blocks[:, r, :],
                    scalar1=t_amask[:, r:r + 1], scalar2=None, op0=ALU.mult),
                    reads=["gmx", "amask"], writes=["t_rs"])
                S.c("dve", lambda e: e.tensor_tensor(
                    out=t_rs[:, 0:20], in0=t_rs[:, 0:20], in1=t_rs[:, 64:84],
                    op=ALU.add), reads=["t_rs"], writes=["t_rs"])
                S.c("dve", lambda e, r=r: e.tensor_tensor(
                    out=t_rs[:, 32:52], in0=t_rs[:, 32:52], in1=blocks[:, r, :],
                    op=ALU.add), reads=["t_rs", "gmx"], writes=["t_rs"])
        S.c("dve", lambda e: e.tensor_scalar(out=t_acol[:], in0=t_rs[:, 0:16],
                                             scalar1=0.5, scalar2=0.5,
                                             op0=ALU.mult, op1=ALU.add),
            reads=["t_rs"], writes=["acol"])
        S.c("dve", lambda e: e.tensor_copy(out=t_rs2[:, 0:3], in_=t_rs[:, 48:51]),
            reads=["t_rs"], writes=["t_rs2"])
        if dbg:
            dma("sp", dbg_o["d_acol"], t_acol[:], ["acol"], ["d_acol"])

        S.c("pe", lambda e: e.transpose(ps_tr[0:3, :], t_rs2[:, 0:3], t_ident[:]),
            reads=["t_rs2", "ident"], writes=["ps_tr"])

        S.c("dve", lambda e: e.tensor_copy(out=t_rs[0:3, 0:P], in_=ps_tr[0:3, :]),
            reads=["ps_tr"], writes=["t_rs", "ps_tr"])
        S.c("dve", lambda e: e.tensor_reduce(out=t_colst[0:3, 4:5], in_=t_rs[0:3, 0:P],
                                             axis=AX.X, op=ALU.add),
            reads=["t_rs"], writes=["colv"])
        dma("sp", rowd[5:6, 0:3], t_colst[0:3, 4:5], ["colv"], ["rowd5"])
        dma("sp", t_mrow[0:1, 0:3], rowd[5:6, 0:3], ["rowd5"], ["mrow"])

        S.c("dve", lambda e: e.tensor_scalar(out=t_mrow[0:1, 4:7],
                                             in0=t_mrow[0:1, 0:3],
                                             scalar1=1.0 / (DFF * D), scalar2=EPS,
                                             op0=ALU.mult, op1=ALU.max),
            reads=["mrow"], writes=["mrow"])
        S.c("dve", lambda e: e.reciprocal(out=t_mrow[0:1, 8:11],
                                          in_=t_mrow[0:1, 4:7]),
            reads=["mrow"], writes=["mrow"])
        S.c("dve", lambda e: e.tensor_scalar(out=t_mrow[0:1, 11:12],
                                             in0=t_mrow[0:1, 4:5],
                                             scalar1=1.0 / 128.0, scalar2=None,
                                             op0=ALU.mult),
            reads=["mrow"], writes=["mrow"])
        S.c("dve", lambda e: e.tensor_tensor(out=t_mrow[0:1, 12:13],
                                             in0=t_mrow[0:1, 5:6],
                                             in1=t_mrow[0:1, 6:7], op=ALU.mult),
            reads=["mrow"], writes=["mrow"])
        S.c("dve", lambda e: e.tensor_scalar(out=t_mrow[0:1, 12:13],
                                             in0=t_mrow[0:1, 12:13],
                                             scalar1=CDOWN, scalar2=None,
                                             op0=ALU.mult),
            reads=["mrow"], writes=["mrow"])
        dma("sp", rowd[6:7, 0:16], t_mrow[:], ["mrow"], ["rowd6"])
        dma("sp", t_scol[:, 0:5], rowd[6:7, 8:13].partition_broadcast(P),
            ["rowd6"], ["scol"])
        if dbg:
            dma("sp", dbg_o["d_mrow"], t_mrow[:], ["mrow"], ["d_mrow"])

        if KSTOP < 3:
            S.frozen = True
        yv = xnv
        dma("sp", xnb[:].rearrange("p (c t) -> p c t", t=T),
            qd.rearrange("c p t -> p c t"),
            [f"qd{o}" for o in range(16)], ["y"] + [f"xn{c}" for c in range(NCH)])
        for c in range(NCH):
            S.c("dve", lambda e, c=c: e.tensor_scalar(
                out=yv[:, c, :], in0=yv[:, c, :], scalar1=t_acol[:, c:c + 1],
                scalar2=None, op0=ALU.mult), reads=["acol"], writes=["y"])

        for o in range(NCH):
            bi = o % 2
            dma("pool", winb[bi][:].rearrange("p (c m) -> p c m", m=P),
                w_out[o].rearrange("c p m -> p c m"), [], [f"winb{bi}"])
            pstag = "ps_a" if o % 2 == 0 else "ps_b"
            pb = ps_a if o % 2 == 0 else ps_b

            def wo_mm(e, bi=bi, pb=pb):
                for c in range(NCH):
                    for hh in range(2):
                        ins = e.matmul(pb[:, hh * 512:(hh + 1) * 512],
                                       winb[bi][:, c * P:(c + 1) * P],
                                       yv[:, c, hh * 512:(hh + 1) * 512],
                                       start=(c == 0), stop=(c == NCH - 1))
                return ins
            S.c("pe", wo_mm, reads=[f"winb{bi}", "y"], writes=[pstag])

            S.c("dve", lambda e, o=o, pb=pb: e.tensor_scalar(
                out=t_rs[:], in0=pb[:], scalar1=t_boutc[:, o:o + 1],
                scalar2=t_mod[:, 32 + o:33 + o], op0=ALU.add, op1=ALU.mult),
                reads=[pstag, "boutc", "mod"], writes=["t_rs", pstag])
            S.c("dve", lambda e, o=o: e.tensor_tensor(
                out=xv[:, o, :], in0=xv[:, o, :], in1=t_rs[:], op=ALU.add),
                reads=["t_rs", "x"], writes=["x", f"x1c{o}"])
            dma("sp", x1d[o], xv[:, o, :], [f"x1c{o}"], [f"x1d{o}"])
        if dbg:
            dma("sp", dbg_o["d_x1"].rearrange("c p t -> p c t"), xv,
                [f"x1c{o}" for o in range(NCH)], ["d_x1"])

        stats_pass(xv, lambda c: f"x1c{c}")
        ln_rows_and_bcast(t_mb, "mbv", t_rb, "rbv")

        for c in range(NCH):
            ln_apply_ops(f"x1c{c}", xv[:, c, :], t_a3[:, c:c + 1],
                         t_mod[:, 48 + c:49 + c], t_mb, "mbv", t_rb, "rbv",
                         t_rs[:], ["t_rs", "h2c"])
            if c == 0:
                S.c("dve", lambda e: e.tensor_copy(out=t_rs2[:], in_=t_rs[:]),
                    reads=["h2c"], writes=["t_rs2"])
                S.c("dve", lambda e: e.tensor_copy(out=t_min[:], in_=t_rs[:]),
                    reads=["h2c"], writes=["t_min"])
            else:
                S.c("dve", lambda e: e.tensor_tensor(out=t_rs2[:], in0=t_rs2[:],
                                                     in1=t_rs[:], op=ALU.max),
                    reads=["h2c", "t_rs2"], writes=["t_rs2"])
                S.c("dve", lambda e: e.tensor_tensor(
                    out=t_min[:], in0=t_min[:], in1=t_rs[:], op=ALU.min),
                    reads=["h2c", "t_min"], writes=["t_min"])

        tree_to_cols(8, t_rs2, t_min, ["t_rs2", "t_min"])

        S.c("dve", lambda e: e.tensor_scalar(out=t_colst[:, 8:16],
                                             in0=t_colst[:, 8:16], scalar1=EPS,
                                             scalar2=None, op0=ALU.max),
            reads=["colv"], writes=["colv"])
        S.c("dve", lambda e: e.reciprocal(out=t_colst[:, 0:8], in_=t_colst[:, 8:16]),
            reads=["colv"], writes=["colv"])
        S.c("dve", lambda e: e.tensor_scalar(out=t_colst[:, 0:8], in0=t_colst[:, 0:8],
                                             scalar1=128.0, scalar2=None,
                                             op0=ALU.mult),
            reads=["colv"], writes=["colv"])
        S.c("dve", lambda e: e.tensor_copy(out=t_gmx[:, 160:168],
                                           in_=t_colst[:, 8:16]),
            reads=["colv"], writes=["gmx"])
        if dbg:
            dma("sp", dbg_o["d_m1c"], t_colst[:], ["colv"], ["d_m1c"])

        col_bcast(t_colst[:, 0:8], 8, lambda j: t_s1b[:, j * P:(j + 1) * P],
                  "sb1", "colv")
        if dbg:
            dma("sp", dbg_o["d_s1b"], t_s1b[:], ["sb1"], ["d_s1b"])

        def cg_colmath(e):
            return e.tensor_scalar(out=t_colst[:, 0:8], in0=t_colst[:, 8:16],
                                   scalar1=t_scol[:, 3:4], scalar2=None, op0=ALU.mult)
        S.c("dve", cg_colmath, reads=["colv", "scol"], writes=["colv"])
        col_bcast(t_colst[:, 0:8], 8, lambda j: t_wb[:, j * P:(j + 1) * P],
                  "wb", "colv")

        q1v = yv
        for c in range(NCH):
            ln_apply_ops(f"x1c{c}", xv[:, c, :], t_a3[:, c:c + 1],
                         t_mod[:, 48 + c:49 + c], t_mb, "mbv", t_rb, "rbv",
                         t_rs[:], ["t_rs", "h2c"])
            S.c("dve", lambda e: e.tensor_tensor(out=q1i[:], in0=t_rs[:],
                                                 in1=t_s1b[:], op=ALU.mult),
                reads=["h2c", "sb1"], writes=["q1i"])
            S.c("dve", lambda e, c=c: e.tensor_copy(out=q1v[:, c, :], in_=q1i[:]),
                reads=["q1i", "y"], writes=[f"q1{c}", "y"])
        if dbg:
            dma("pool", dbg_o["d_q1"].rearrange("c p t -> p c t"), q1v,
                [f"q1{c}" for c in range(NCH)], ["d_q1"])

        if KSTOP < 4:
            S.frozen = True
        p2b.close()
        p2a.close()
        xstack.close()
        S.fence(["x"] + [f"kr{c}" for c in range(NCH)] +
                ["cos0", "cos1", "sin0", "sin1", "kc0", "kc1", "xbf", "sqs",
                 "winb0", "winb1"], ["p4fence"])
        p4 = contextlib.ExitStack()
        wgf = [sb(p4, f"wgf{i}", [P, NCH * OSUP], F32) for i in range(2)]
        wuf = [sb(p4, f"wuf{i}", [P, NCH * OSUP], F32) for i in range(2)]
        t_d2 = sb(p4, "t_d2", [P, NCH * OSUP], I8)
        tern_g = sb(p4, "tern_g", [P, NCH * OSUP], BF16)
        tern_u = sb(p4, "tern_u", [P, NCH * OSUP], BF16)
        t_sg = sb(p4, "t_sg", [P, T], F32)
        t_ic = sb(p4, "t_ic", [P, T], BF16)
        t_pc = sb(p4, "t_pc", [P, T], BF16)

        for i in range(NSUP):
            bi = i % 2
            dma("sp", wgf[bi][:].rearrange("p (c m) -> p c m", m=OSUP),
                wg_t[i].rearrange("c p m -> p c m"), ["p4fence"], [f"wgf{bi}"])
            dma("sp", wuf[bi][:].rearrange("p (c m) -> p c m", m=OSUP),
                wu_t[i].rearrange("c p m -> p c m"), ["p4fence"], [f"wuf{bi}"])
            for wf, wtag, tt, ttag, sci in ((wgf[bi], f"wgf{bi}", tern_g, "tg", 0),
                                            (wuf[bi], f"wuf{bi}", tern_u, "tu", 1)):
                S.c("dve", lambda e, wf=wf, sci=sci: e.tensor_scalar(
                    out=t_d2[:], in0=wf[:], scalar1=t_scol[:, sci:sci + 1],
                    scalar2=None, op0=ALU.mult),
                    reads=[wtag, "scol"], writes=["t_d2"])
                S.c("dve", lambda e, tt=tt: e.tensor_scalar(
                    out=tt[:], in0=t_d2[:], scalar1=1.0, scalar2=-1.0,
                    op0=ALU.min, op1=ALU.max),
                    reads=["t_d2"], writes=[ttag, "t_d2"])

            for blk in range(OSUP // P):
                a = i * (OSUP // P) + blk
                gsl = tern_g[:].rearrange("p (c m) -> p c m", m=OSUP)[:, :, blk * P:(blk + 1) * P]
                usl = tern_u[:].rearrange("p (c m) -> p c m", m=OSUP)[:, :, blk * P:(blk + 1) * P]

                def gmm(e, gsl=gsl):
                    for c in range(NCH):
                        for hh in range(2):
                            ins = e.matmul(ps_a[:, hh * 512:(hh + 1) * 512],
                                           gsl[:, c, :],
                                           q1v[:, c, hh * 512:(hh + 1) * 512],
                                           start=(c == 0), stop=(c == NCH - 1))
                    return ins
                S.c("pe", gmm, reads=["tg"] + [f"q1{c}" for c in range(NCH)],
                    writes=["ps_a"])
                S.c("dve", lambda e: e.tensor_tensor(out=t_sg[:], in0=ps_a[:],
                                                     in1=t_wb[:], op=ALU.mult),
                    reads=["ps_a", "wb"], writes=["sg", "ps_a"])
                S.c("act", lambda e: e.activation(out=t_sg[:], in_=t_sg[:],
                                                  func=AF.Silu),
                    reads=["sg"], writes=["sg"])
                if dbg and a == 0:
                    dma("sp", dbg_o["d_inter"], t_sg[:], ["sg"], ["d_inter"])

                def umm(e, usl=usl):
                    for c in range(NCH):
                        for hh in range(2):
                            ins = e.matmul(ps_b[:, hh * 512:(hh + 1) * 512],
                                           usl[:, c, :],
                                           q1v[:, c, hh * 512:(hh + 1) * 512],
                                           start=(c == 0), stop=(c == NCH - 1))
                    return ins
                S.c("pe", umm, reads=["tu"] + [f"q1{c}" for c in range(NCH)],
                    writes=["ps_b"])
                S.c("dve", lambda e: e.tensor_tensor(out=t_ic[:], in0=t_sg[:],
                                                     in1=ps_b[:], op=ALU.mult),
                    reads=["sg", "ps_b"], writes=["ic", "ps_b"])

                def s1mm(e):
                    for hh in range(2):
                        ins = e.matmul(ps_p[:, hh * 512:(hh + 1) * 512], t_h128[:],
                                       t_ic[:, hh * 512:(hh + 1) * 512],
                                       start=True, stop=True)
                    return ins
                S.c("pe", s1mm, reads=["ic", "h128"], writes=["ps_p", "ps_rows"])
                S.c("dve", lambda e: e.tensor_copy(out=t_pc[:], in_=ps_p[:]),
                    reads=["ps_p"], writes=["pc", "ps_p"])
                dma("sp", pdt[:, a, :, :],
                    t_pc[:].rearrange("p (tc t) -> p tc t", t=TC), ["pc"], [f"pd{a}"])

        if KSTOP < 5:
            S.frozen = True
        p4.close()
        ystack.close()
        S.fence(["y", "tg", "tu", "t_d2", "wgf0", "wgf1", "wuf0", "wuf1",
                 "sg", "ic", "pc"] + [f"q1{c}" for c in range(NCH)], ["p5fence"])
        p5a = contextlib.ExitStack()
        q2b = sb(p5a, "q2b", [P, FCH * T], BF16)
        q2v = q2b[:].rearrange("p (k t) -> p k t", t=T)
        p5b = contextlib.ExitStack()
        gtc = sb(p5b, "gtc", [P, FCH * TC], BF16)
        gv = gtc[:].rearrange("p (k t) -> p k t", t=TC)
        prb = [sb(p5b, f"prb{i}", [64, TC], BF16) for i in range(2)]
        q2i8 = sb(p5b, "q2i8", [P, 2 * TC], I8)
        pdall = [f"pd{a}" for a in range(FCH)]

        for tc in range(NTC):
            for jj in range(32):
                for q in range(4):
                    bp = 4 * jj + q
                    bi = bp % 2
                    dma("sp", prb[bi][:], pdt[bp, :, tc, :],
                        pdall + ["p5fence"], [f"prb{bi}"])

                    def s2mm(e, q=q, bi=bi):
                        pbase = 64 * (q % 2)
                        fbase = 256 * (q // 2)
                        return e.matmul(ps_p[pbase:pbase + 64, fbase:fbase + TC],
                                        t_h64[:], prb[bi][:], start=True, stop=True)
                    S.c("pe", s2mm, reads=[f"prb{bi}", "h64"], writes=["ps_p"])

                def s2ev(e, jj=jj):
                    e.copy(out=gv[:, 2 * jj, :], in_=ps_p[:, 0:TC])
                    return e.copy(out=gv[:, 2 * jj + 1, :],
                                  in_=ps_p[:, TC:2 * TC])
                S.c("act", s2ev, reads=["ps_p"], writes=[f"g{jj}", "ps_p"])

            for k in range(FCH):
                jj = k // 2
                if k == 0:
                    S.c("dve", lambda e, k=k, jj=jj: e.tensor_copy(
                        out=t_rs2[:, 0:TC], in_=gv[:, k, :]),
                        reads=[f"g{jj}"], writes=["t_rs2"])
                    S.c("dve", lambda e, k=k, jj=jj: e.tensor_copy(
                        out=t_min[:, 0:TC], in_=gv[:, k, :]),
                        reads=[f"g{jj}"], writes=["t_min"])
                else:
                    S.c("dve", lambda e, k=k, jj=jj: e.tensor_tensor(
                        out=t_rs2[:, 0:TC], in0=t_rs2[:, 0:TC], in1=gv[:, k, :],
                        op=ALU.max), reads=[f"g{jj}", "t_rs2"], writes=["t_rs2"])
                    S.c("dve", lambda e, k=k, jj=jj: e.tensor_tensor(
                        out=t_min[:, 0:TC], in0=t_min[:, 0:TC], in1=gv[:, k, :],
                        op=ALU.min), reads=[f"g{jj}", "t_min"], writes=["t_min"])

            tree_to_cols(2, t_rs2[:, 0:TC], t_min[:, 0:TC], ["t_rs2", "t_min"])

            S.c("dve", lambda e: e.tensor_scalar(out=t_colst[:, 8:10],
                                                 in0=t_colst[:, 8:10], scalar1=EPS,
                                                 scalar2=None, op0=ALU.max),
                reads=["colv"], writes=["colv"])
            S.c("dve", lambda e: e.reciprocal(out=t_colst[:, 0:2],
                                              in_=t_colst[:, 8:10]),
                reads=["colv"], writes=["colv"])
            S.c("dve", lambda e: e.tensor_scalar(out=t_colst[:, 0:2],
                                                 in0=t_colst[:, 0:2], scalar1=128.0,
                                                 scalar2=None, op0=ALU.mult),
                reads=["colv"], writes=["colv"])
            S.c("dve", lambda e, tc=tc: e.tensor_tensor(
                out=t_colst[:, 4:6], in0=t_colst[:, 8:10],
                in1=t_gmx[:, 160 + 2 * tc:162 + 2 * tc], op=ALU.mult),
                reads=["colv", "gmx"], writes=["colv"])
            S.c("dve", lambda e: e.tensor_scalar(out=t_colst[:, 4:6],
                                                 in0=t_colst[:, 4:6],
                                                 scalar1=t_scol[:, 4:5],
                                                 scalar2=None, op0=ALU.mult),
                reads=["colv", "scol"], writes=["colv"])
            S.c("dve", lambda e, tc=tc: e.tensor_copy(
                out=t_gmx[:, 168 + 2 * tc:170 + 2 * tc], in_=t_colst[:, 4:6]),
                reads=["colv"], writes=["gmx"])

            col_bcast(t_colst[:, 0:2], 2,
                      lambda j, tc=tc: t_s1b[:, tc * TC + j * P:tc * TC + (j + 1) * P],
                      "sb1", "colv")

            for jj in range(32):
                def q2a(e, jj=jj, tc=tc):
                    sl = t_s1b[:, tc * TC:(tc + 1) * TC]
                    e.tensor_tensor(out=q2i8[:, 0:TC], in0=gv[:, 2 * jj, :], in1=sl,
                                    op=ALU.mult)
                    return e.tensor_tensor(out=q2i8[:, TC:2 * TC],
                                           in0=gv[:, 2 * jj + 1, :], in1=sl,
                                           op=ALU.mult)
                S.c("dve", q2a, reads=[f"g{jj}", "sb1"], writes=["q2i8"])

                def q2b_(e, jj=jj, tc=tc):
                    e.tensor_copy(out=q2v[:, 2 * jj, tc * TC:(tc + 1) * TC],
                                  in_=q2i8[:, 0:TC])
                    return e.tensor_copy(out=q2v[:, 2 * jj + 1, tc * TC:(tc + 1) * TC],
                                         in_=q2i8[:, TC:2 * TC])
                S.c("dve", q2b_, reads=["q2i8"], writes=[f"q2_{tc}_{jj}", f"g{jj}"])

        if dbg:
            dma("pool", dbg_o["d_q2"], q2b[:, 0:4 * T],
                [f"q2_{tc}_{jj}" for tc in range(NTC) for jj in range(4)], ["d_q2"])

        col_bcast(t_gmx[:, 168:176], 8, lambda j: t_wb[:, j * P:(j + 1) * P],
                  "wb", "gmx")

        if KSTOP < 6:
            S.frozen = True
        S.fence(["t_rs2", "t_min", "sb1"] + [f"g{jj}" for jj in range(32)] +
                [f"s2b{tc}" for tc in range(NTC)], ["p6fence"])
        p5b.close()
        p6 = contextlib.ExitStack()
        wdf = sb(p6, "wdf", [P, 32 * P], F32)
        t_dd = sb(p6, "t_dd", [P, 32 * P], I8)
        tern_d = sb(p6, "tern_d", [P, 32 * P], BF16)
        t_x1c = sb(p6, "t_x1c", [P, T], F32)
        t_oc = sb(p6, "t_oc", [P, T], F32)
        q2tags = [f"q2_{tc}_{jj}" for tc in range(NTC) for jj in range(32)]

        for o2 in range(NCH):
            for kh in range(2):
                dma("sp", wdf[:].rearrange("p (c m) -> p c m", m=P),
                    wd_t[o2, kh].rearrange("c p m -> p c m"), ["p6fence"], ["wdf"])
                S.c("dve", lambda e: e.tensor_scalar(
                    out=t_dd[:], in0=wdf[:], scalar1=t_scol[:, 2:3],
                    scalar2=None, op0=ALU.mult),
                    reads=["wdf", "scol"], writes=["tdd"])
                S.c("dve", lambda e: e.tensor_scalar(
                    out=tern_d[:], in0=t_dd[:], scalar1=1.0, scalar2=-1.0,
                    op0=ALU.min, op1=ALU.max),
                    reads=["tdd"], writes=["ternd", "tdd"])

                def dmm(e, kh=kh):
                    for ck in range(32):
                        kc = kh * 32 + ck
                        for hh in range(2):
                            ins = e.matmul(ps_a[:, hh * 512:(hh + 1) * 512],
                                           tern_d[:, ck * P:(ck + 1) * P],
                                           q2v[:, kc, hh * 512:(hh + 1) * 512],
                                           start=(kh == 0 and ck == 0),
                                           stop=(kh == 1 and ck == 31))
                    return ins
                S.c("pe", dmm, reads=["ternd"] + q2tags, writes=["ps_a"])

            dma("sp", t_x1c[:], x1d[o2], [f"x1d{o2}", "x1used"], ["x1c"])

            def down_ev(e, o2=o2):
                e.scalar_tensor_tensor(out=t_oc[:], in0=ps_a[:],
                                       scalar=t_mod[:, 80 + o2:81 + o2], in1=t_wb[:],
                                       op0=ALU.mult, op1=ALU.mult)
                return e.tensor_tensor(out=t_oc[:], in0=t_oc[:], in1=t_x1c[:],
                                       op=ALU.add)
            S.c("dve", down_ev, reads=["ps_a", "mod", "wb", "x1c"],
                writes=["toc", "ps_a", "x1used"])
            dma("sp", outT[o2], t_oc[:], ["toc"], [f"out{o2}", "toc"])

        S.c("dve", lambda e: e.engine_nop(),
            reads=[f"out{o2}" for o2 in range(NCH)] + dbg_tags, writes=["fin"])

        S.emit(block, sems)
        p6.close()
        p5a.close()

    return nc


def _sylvester(n):
    h = np.array([[1.0]], dtype=np.float32)
    while h.shape[0] < n:
        h = np.block([[h, h], [h, -h]])
    return h.astype(np.float32)


def _prep_inputs(inputs):
    X = np.asarray(inputs["L0_attn_X"])
    W_in = np.asarray(inputs["L0_attn_W_in"])
    b_in = np.asarray(inputs["L0_attn_b_in"])
    W_out = np.asarray(inputs["L0_attn_W_out"])
    b_out = np.asarray(inputs["L0_attn_b_out"])
    gamma = np.asarray(inputs["L0_attn_gamma"])
    beta = np.asarray(inputs["L0_attn_beta"])
    w_gate = np.asarray(inputs["L0_ffn_w_gate"])
    w_up = np.asarray(inputs["L0_ffn_w_up"])
    w_down = np.asarray(inputs["L0_ffn_w_down"])
    c = np.asarray(inputs["L0_c"])
    wmod = np.asarray(inputs["L0_wmod"])
    f32 = np.float32

    w_in_t = np.ascontiguousarray(
        W_in.reshape(NCH, P, OQKV, P).transpose(2, 0, 1, 3)).astype(f32, copy=False)
    w_out_t = np.ascontiguousarray(
        W_out.reshape(NCH, P, NCH, P).transpose(2, 0, 1, 3)).astype(f32, copy=False)
    wgT = np.ascontiguousarray(w_gate.T)
    wuT = np.ascontiguousarray(w_up.T)
    wg_tile = np.ascontiguousarray(wgT.reshape(NCH, P, NSUP, OSUP).transpose(2, 0, 1, 3))
    wu_tile = np.ascontiguousarray(wuT.reshape(NCH, P, NSUP, OSUP).transpose(2, 0, 1, 3))
    wdT = np.ascontiguousarray(w_down.T)
    perm = np.zeros(DFF, dtype=np.int64)
    for k in range(FCH):
        jj, half = k // 2, k % 2
        for p in range(P):
            bprime = 4 * jj + 2 * half + (1 if p >= 64 else 0)
            aprime = p % 64
            perm[k * P + p] = aprime * P + bprime
    wdT_perm = wdT[perm]
    wd_tile = np.ascontiguousarray(
        wdT_perm.reshape(2, 32, P, NCH, P).transpose(3, 0, 1, 2, 4))

    b_in_c = np.ascontiguousarray(b_in.reshape(OQKV, P).T)
    b_out_cc = np.ascontiguousarray(b_out.reshape(NCH, P).T)
    gam_cc = np.ascontiguousarray(gamma.reshape(NCH, P).T)
    bet_cc = np.ascontiguousarray(beta.reshape(NCH, P).T)
    cTt = np.ascontiguousarray(c.T.reshape(NCH, P, 2))

    h128m = _sylvester(128).astype(ml_dtypes.bfloat16)
    h64m = _sylvester(64).astype(ml_dtypes.bfloat16)
    identm = np.eye(P, dtype=f32)
    onesc = np.ones((P, 1), dtype=ml_dtypes.bfloat16)
    constv = np.tile(np.array([[EPS, 1.0, 0.0, 0.5]], dtype=f32), (P, 1))
    bchot = np.zeros((8, 8, P), dtype=ml_dtypes.bfloat16)
    for j in range(8):
        bchot[j, j, :] = 1.0

    rot_dim = D // 2
    inv_freq = (1.0 / (10000.0 ** (np.arange(0, rot_dim, 2, dtype=f32) / f32(rot_dim))))
    in_maps = []
    for core in range(NCORES):
        b = core // 4
        s0 = (core % 4) * T
        pos = np.arange(s0, s0 + T, dtype=f32)
        fr = pos[None, :] * inv_freq[:, None]
        cosf = np.repeat(np.cos(fr), 2, axis=0)
        sinf = np.repeat(np.sin(fr), 2, axis=0)
        sgn = np.where(np.arange(rot_dim) % 2 == 0, -1.0, 1.0).astype(f32)
        sinf = sinf * sgn[:, None]
        costabc = np.ascontiguousarray(cosf.reshape(8, P, T)).astype(ml_dtypes.bfloat16)
        sintabc = np.ascontiguousarray(sinf.reshape(8, P, T)).astype(ml_dtypes.bfloat16)

        xTc = np.ascontiguousarray(X[b].T[:, s0:s0 + T]).reshape(NCH, P, T).astype(f32)
        wmod_s = np.ascontiguousarray(
            wmod[:, core * 1536:(core + 1) * 1536]
            .reshape(NCH, P, 12, P).transpose(2, 0, 1, 3))
        bm = np.tile(np.array([[1.0, 0.0]] if b == 0 else [[0.0, 1.0]], dtype=f32), (P, 1))
        am = np.zeros((P, 8), dtype=f32)
        am[:, 4 * b:4 * b + 4] = 1.0
        mg = np.ascontiguousarray(
            wgT[:, core * 1024:(core + 1) * 1024].reshape(NCH, P, T))
        mu = np.ascontiguousarray(
            wuT[:, core * 1024:(core + 1) * 1024].reshape(NCH, P, T))
        md = np.ascontiguousarray(
            wdT_perm[core * 1024:(core + 1) * 1024].reshape(NCH, P, T))
        mshard = np.stack([mg, mu, md], axis=0)

        in_maps.append({
            "xT": xTc, "w_in": w_in_t, "b_in_c": b_in_c,
            "w_out": w_out_t, "b_out_c": b_out_cc, "gam_c": gam_cc, "bet_c": bet_cc,
            "wg_t": wg_tile, "wu_t": wu_tile, "wd_t": wd_tile,
            "wmod_sh": wmod_s, "cT": cTt, "costab": costabc, "sintab": sintabc,
            "h128": h128m, "h64": h64m, "ident": identm, "onescol": onesc,
            "bchot": bchot, "constv": constv, "bmask": bm, "amask": am,
            "msh": mshard,
        })
    return in_maps


_prog_cache = {}


def kernel(**inputs):
    in_maps = _prep_inputs(inputs)
    if "main" not in _prog_cache:
        _prog_cache["main"] = build_program(dbg=False, serialize=False)
    nc = _prog_cache["main"]
    res = run_bass_kernel_spmd(nc, in_maps, core_ids=list(range(NCORES)))
    out = np.zeros((2, 4096, D), dtype=np.float32)
    for core in range(NCORES):
        b = core // 4
        s0 = (core % 4) * T
        o = np.asarray(res.results[core]["outT"])
        out[b, s0:s0 + T, :] = o.reshape(D, T).T
    return out


# version 03
# speedup vs baseline: 2.1168x; 1.0131x over previous
import contextlib
import numpy as np
import ml_dtypes
import concourse.bass as bass
import concourse.mybir as mybir
from concourse.bass_utils import run_bass_kernel_spmd

F32 = mybir.dt.float32
BF16 = mybir.dt.bfloat16
I8 = mybir.dt.int8
AF = mybir.ActivationFunctionType
ALU = mybir.AluOpType
AX = mybir.AxisListType

P = 128
D = 2048
DFF = 8192
T = 1024
NCH = D // P
FCH = DFF // P
NCORES = 8
EPS = 1e-5
OQKV = 3 * D // P
NSUP = 32
OSUP = DFF // NSUP
TC = 256
NTC = T // TC
INVD = 1.0 / D
CDOWN = float(1.0 / (np.sqrt(2.0) * 64.0 * 128.0 * 128.0))

ENGINES = ("sp", "act", "dve", "pool", "pe")
SAME_ENGINE_EDGES = True


def o_kind(k):
    return k


class Sched:

    def __init__(self, serialize=False):
        self.frozen = False
        self.ops = []
        self.last_w = {}
        self.readers = {}
        self.serialize = serialize
        self.stream_last = {}

    def op(self, stream, kind, emit_fn, reads=(), writes=()):
        if self.frozen:
            return -1
        oid = len(self.ops)
        deps = set()
        if self.serialize:
            for s, last in self.stream_last.items():
                if s != stream:
                    deps.add(last)
        for t in reads:
            if t in self.last_w:
                deps.add(self.last_w[t])
        for t in writes:
            if t in self.last_w:
                deps.add(self.last_w[t])
            for r in self.readers.get(t, ()):
                deps.add(r)
        if not SAME_ENGINE_EDGES:
            deps = {dd for dd in deps if self.ops[dd][0] != stream}
        else:
            deps = {dd for dd in deps
                    if self.ops[dd][0] != stream or self.ops[dd][1] != o_kind(kind)
                    or True}
        self.ops.append([stream, kind, emit_fn, deps, list(reads), list(writes)])
        for t in reads:
            self.readers.setdefault(t, []).append(oid)
        for t in writes:
            self.last_w[t] = oid
            self.readers[t] = []
        self.stream_last[stream] = oid
        return oid

    def c(self, stream, emit_fn, reads=(), writes=()):
        return self.op(stream, "c", emit_fn, list(reads), list(writes))

    def d(self, stream, emit_fn, reads=(), writes=()):
        return self.op(stream, "d", emit_fn, list(reads), list(writes))

    def fence(self, reads, writes):
        self.c("dve", lambda e: e.engine_nop(), reads=reads, writes=writes)

    NSLOT = 8

    def lower(self):
        n = len(self.ops)
        marked = [False] * n
        for o in self.ops:
            for dd in o[3]:
                marked[dd] = True
        cnt = {}
        comp = [0] * n
        self.slot = [None] * n
        self.use = [0] * n
        dma_idx = {}
        for i, o in enumerate(self.ops):
            if o[1] == "c":
                key = (o[0], "c")
                if marked[i]:
                    cnt[key] = cnt.get(key, 0) + 1
                comp[i] = cnt.get(key, 0)
            elif o[1] == "d":
                k = dma_idx.get(o[0], 0)
                dma_idx[o[0]] = k + 1
                self.slot[i] = k % self.NSLOT
                self.use[i] = k // self.NSLOT
            else:
                k = dma_idx.get("cc", 0)
                dma_idx["cc"] = k + 1
                self.slot[i] = "cc"
                self.use[i] = k
        waits = [[] for _ in range(n)]
        clock = {}
        for i, o in enumerate(self.ops):
            need = {}
            for dd in sorted(o[3]):
                dop = self.ops[dd]
                if dop[1] == "c":
                    key = (dop[0], "c")
                    need[key] = max(need.get(key, 0), comp[dd])
                elif dop[1] == "d":
                    key = (dop[0], "s", self.slot[dd])
                    need[key] = max(need.get(key, 0), 16 * (self.use[dd] + 1))
                else:
                    key = ("cc",)
                    need[key] = max(need.get(key, 0), self.use[dd] + 1)
            for key, v in sorted(need.items(), key=str):
                ck = (o[0], key)
                if clock.get(ck, 0) >= v:
                    continue
                clock[ck] = v
                waits[i].append((key, v))
        return marked, waits

    def emit(self, block, sems):
        marked, waits = self.lower()
        per_stream = {}
        for i, o in enumerate(self.ops):
            per_stream.setdefault(o[0], []).append(i)
        builders = {"sp": block.sync, "act": block.scalar, "dve": block.vector,
                    "pool": block.gpsimd, "pe": block.tensor}
        for stream, ids in per_stream.items():
            def body(eng, ids=ids, stream=stream):
                for i in ids:
                    o = self.ops[i]
                    if o[1] == "d":
                        skey = (stream, "s", self.slot[i])
                        if self.use[i] > 0:
                            eng.wait_ge(sems[skey], 16 * self.use[i])
                    for (key, v) in waits[i]:
                        eng.wait_ge(sems[key], v)
                    ins = o[2](eng)
                    if o[1] == "c":
                        if marked[i]:
                            ins.then_inc(sems[(stream, "c")], 1)
                    elif o[1] == "d":
                        ins.then_inc(sems[(stream, "s", self.slot[i])], 16)
                    else:
                        ins.then_inc(sems[("cc",)], 1)
            builders[stream](body)


class _StopBuild(Exception):
    pass


def build_program(dbg=False, serialize=False):
    import os as _os0
    KSTOP = int(_os0.environ.get("KSTOP", "99"))
    NOMODF = bool(_os0.environ.get("KNOMODF"))
    NBUF = int(_os0.environ.get("KNBUF", "2"))
    Sched.NSLOT = int(_os0.environ.get("KNSLOT", "8"))
    nc = bass.Bass("TRN2", target_bir_lowering=False)

    def ein(name, shape, dt=F32):
        return nc.dram_tensor(name, shape, dt, kind="ExternalInput").ap()

    xT = ein("xT", [NCH, P, T])
    w_in = ein("w_in", [OQKV, NCH, P, P])
    b_in_c = ein("b_in_c", [P, OQKV])
    w_out = ein("w_out", [NCH, NCH, P, P])
    b_out_c = ein("b_out_c", [P, NCH])
    gam_c = ein("gam_c", [P, NCH])
    bet_c = ein("bet_c", [P, NCH])
    wg_t = ein("wg_t", [NSUP, NCH, P, OSUP])
    wu_t = ein("wu_t", [NSUP, NCH, P, OSUP])
    wd_t = ein("wd_t", [NCH, 2, 32, P, P])
    wmod_sh = ein("wmod_sh", [12, NCH, P, P])
    cT = ein("cT", [NCH, P, 2])
    costab = ein("costab", [8, P, T], BF16)
    sintab = ein("sintab", [8, P, T], BF16)
    h128 = ein("h128", [P, P], BF16)
    h64 = ein("h64", [64, 64], BF16)
    ident = ein("ident", [P, P])
    onescol = ein("onescol", [P, 1], BF16)
    bchot = ein("bchot", [8, 8, P], BF16)
    constv = ein("constv", [P, 4])
    bmask = ein("bmask", [P, 2])
    amask = ein("amask", [P, 8])
    msh = ein("msh", [3, NCH, P, T])

    outT = nc.dram_tensor("outT", [NCH, P, T], F32, kind="ExternalOutput").ap()
    dbg_o = {}
    dbg_tags = []
    if dbg:
        for nm, shp in [("d_h", [NCH, P, T]), ("d_xn", [NCH, P, T]),
                        ("d_kv", [NCH, P, T]), ("d_acol", [P, 16]),
                        ("d_x1", [NCH, P, T]), ("d_q1", [NCH, P, T]),
                        ("d_mod", [P, 96]), ("d_mrow", [1, 16]),
                        ("d_q2", [P, 4 * T]), ("d_inter", [P, T]),
                        ("d_m1c", [P, 16]), ("d_s1b", [P, T]),
                        ("d_modsh", [P, 24]), ("d_silu", [P, NCH * 2]),
                        ("d_agp", [P, 20]), ("d_ct", [P, NCH * 2]),
                        ("d_siluf", [P, NCH * 2]), ("d_wmb", [P, NCH * P])]:
            dbg_o[nm] = nc.dram_tensor(nm, shp, F32, kind="ExternalOutput").ap()
            dbg_tags.append(nm)

    qd = nc.dram_tensor("qd", [NCH, P, T], BF16).ap()
    x1d = nc.dram_tensor("x1d", [NCH, P, T], F32).ap()
    pdt = nc.dram_tensor("pdt", [P, FCH, NTC, TC], BF16).ap()
    rowd = nc.dram_tensor("rowd", [16, T], F32).ap()
    ag1i = nc.dram_tensor("ag1i", [P, 24], F32).ap()
    ag1o = nc.dram_tensor("ag1o", [NCORES * P, 24], F32).ap()
    ag2i = nc.dram_tensor("ag2i", [P, 20], F32).ap()
    ag2o = nc.dram_tensor("ag2o", [NCORES * P, 20], F32).ap()
    agfi = nc.dram_tensor("agfi", [P, 4], F32).ap()
    agf1o = nc.dram_tensor("agf1o", [NCORES * P, 4], F32).ap()
    agf2o = nc.dram_tensor("agf2o", [NCORES * P, 4], F32).ap()

    S = Sched(serialize=serialize)

    with contextlib.ExitStack() as ST:
        sems = {}
        for e in ENGINES:
            sems[(e, "c")] = ST.enter_context(nc.semaphore(f"s_{e}_c"))
        for e in ("sp", "pool", "act"):
            for sl in range(Sched.NSLOT):
                sems[(e, "s", sl)] = ST.enter_context(nc.semaphore(f"s_{e}_{sl}"))
        sems[("cc",)] = ST.enter_context(nc.semaphore("s_cc"))
        block = ST.enter_context(nc.Block())

        def sb(stack, name, shape, dt=F32):
            return stack.enter_context(nc.sbuf_tensor(name, shape, dt))

        t_binc = sb(ST, "t_binc", [P, OQKV])
        t_boutc = sb(ST, "t_boutc", [P, NCH])
        t_gam = sb(ST, "t_gam", [P, NCH])
        t_bet = sb(ST, "t_bet", [P, NCH])
        t_h128 = sb(ST, "t_h128", [P, P], BF16)
        t_h64 = sb(ST, "t_h64", [64, 64], BF16)
        t_ident = sb(ST, "t_ident", [P, P])
        t_ones = sb(ST, "t_ones", [P, 1], BF16)
        t_bchot = sb(ST, "t_bchot", [8, 8 * P], BF16)
        t_const = sb(ST, "t_const", [P, 4])
        t_bmask = sb(ST, "t_bmask", [P, 2])
        t_amask = sb(ST, "t_amask", [P, 8])
        t_mod = sb(ST, "t_mod", [P, 96])
        t_a1 = sb(ST, "t_a1", [P, NCH])
        t_a3 = sb(ST, "t_a3", [P, NCH])
        t_acol = sb(ST, "t_acol", [P, NCH])
        t_agp = sb(ST, "t_agp", [P, 20])
        t_mrow = sb(ST, "t_mrow", [1, 16])
        t_scol = sb(ST, "t_scol", [P, 8])
        t_colst = sb(ST, "t_colst", [P, 16])
        t_bchh = sb(ST, "t_bchh", [8, P], BF16)
        t_bchl = sb(ST, "t_bchl", [8, P], BF16)
        t_gmx = sb(ST, "t_gmx", [P, 176])
        t_rs = sb(ST, "t_rs", [P, T], F32)
        t_wb = sb(ST, "t_wb", [P, T])
        t_mb = sb(ST, "t_mb", [P, T])
        t_rb = sb(ST, "t_rb", [P, T])
        t_s1b = sb(ST, "t_s1b", [P, T])
        q1i = sb(ST, "q1i", [P, T], I8)

        nc.const_aps.register(F32, EPS, t_const[:, 0:1])
        nc.const_aps.register(F32, 1.0, t_const[:, 1:2])
        nc.const_aps.register(F32, 0.0, t_const[:, 2:3])

        ps_a = ST.enter_context(nc.psum_tensor("ps_a", [P, T], F32))
        ps_b = ST.enter_context(nc.psum_tensor("ps_b", [P, T], F32))
        ps_p = ST.enter_context(nc.psum_tensor("ps_p", [P, T], F32))
        ps_tr = ST.enter_context(nc.psum_tensor("ps_tr", [P, P], F32))

        def dma(stream, dst, src, reads, writes):
            S.d(stream, lambda e, dst=dst, src=src: e.dma_start(out=dst, in_=src),
                reads=list(reads), writes=list(writes))

        t_rs2 = sb(ST, "t_rs2", [P, T], F32)
        t_min = sb(ST, "t_min", [P, T], F32)
        t_xbf = sb(ST, "t_xbf", [P, T], BF16)
        t_sqs = sb(ST, "t_sqs", [P, T], BF16)
        ystack = contextlib.ExitStack()
        xnb = sb(ystack, "xnb", [P, NCH * T], BF16)
        xnv = xnb[:].rearrange("p (c t) -> p c t", t=T)
        xstack = contextlib.ExitStack()
        xbuf = sb(xstack, "xbuf", [P, NCH * T], F32)
        xv = xbuf[:].rearrange("p (c t) -> p c t", t=T)

        dma("sp", xbuf[:].rearrange("p (c t) -> p c t", t=T),
            xT.rearrange("c p t -> p c t"), [], ["x"])
        for src, dst, tag in [(b_in_c, t_binc, "binc"), (b_out_c, t_boutc, "boutc"),
                              (gam_c, t_gam, "gam"), (bet_c, t_bet, "bet"),
                              (ident, t_ident, "ident"), (constv, t_const, "const"),
                              (bmask, t_bmask, "bmask"), (amask, t_amask, "amask")]:
            dma("sp", dst[:], src, [], [tag])
        for src, dst, tag in [(h128, t_h128, "h128"), (h64, t_h64, "h64"),
                              (onescol, t_ones, "ones")]:
            dma("pool", dst[:], src, [], [tag])
        dma("pool", t_bchot[:].rearrange("k (j p) -> k j p", p=P),
            bchot.rearrange("j k p -> k j p"), [], ["bchot"])

        mstack = contextlib.ExitStack()
        t_ct = sb(mstack, "t_ct", [P, NCH * 2])
        t_sbf = sb(mstack, "t_sbf", [P, NCH * 2], BF16)
        t_slb = sb(mstack, "t_slb", [P, NCH * 2], BF16)
        t_siluf = sb(mstack, "t_siluf", [P, NCH * 2])
        t_slo = sb(mstack, "t_slo", [P, NCH * 2])
        wmb = [sb(mstack, f"wmb{i}", [P, NCH * P], BF16) for i in range(NBUF)]
        wmf = sb(mstack, "wmf", [P, NCH * P], F32)
        wml = sb(mstack, "wml", [P, NCH * P], BF16)
        t_modsh = sb(mstack, "t_modsh", [P, 24])

        dma("sp", t_ct[:].rearrange("p (c b) -> p c b", b=2),
            cT.rearrange("c p b -> p c b"), [], ["ct"])
        S.c("act", lambda e: e.activation(out=t_siluf[:], in_=t_ct[:], func=AF.Sigmoid),
            reads=["ct", "const"], writes=["siluf"])
        S.c("dve", lambda e: e.tensor_tensor(out=t_siluf[:], in0=t_siluf[:],
                                             in1=t_ct[:], op=ALU.mult),
            reads=["siluf", "ct"], writes=["siluf"])
        S.c("dve", lambda e: e.tensor_copy(out=t_sbf[:], in_=t_siluf[:]),
            reads=["siluf"], writes=["sbf"])
        S.c("dve", lambda e: e.tensor_tensor(out=t_slo[:], in0=t_siluf[:],
                                             in1=t_sbf[:], op=ALU.subtract),
            reads=["siluf", "sbf"], writes=["slo"])
        S.c("dve", lambda e: e.tensor_copy(out=t_slb[:], in_=t_slo[:]),
            reads=["slo"], writes=["slb"])

        MODF = () if NOMODF else tuple(range(12))
        for i in range(12):
            bi = i % NBUF
            dma("pool", wmb[bi][:].rearrange("p (c m) -> p c m", m=P),
                wmod_sh[i].rearrange("c p m -> p c m"), [], [f"wmb{bi}"])
            extra = []
            if i in MODF:
                dma("pool", wmf[:].rearrange("p (c m) -> p c m", m=P),
                    wmod_sh[i].rearrange("c p m -> p c m"), [], ["wmf"])
                S.c("dve", lambda e, bi=bi: e.tensor_tensor(
                    out=wml[:], in0=wmf[:], in1=wmb[bi][:], op=ALU.subtract),
                    reads=["wmf", f"wmb{bi}"], writes=["wml"])
                extra = ["wml"]

            def mod_mm(e, i=i, bi=bi):
                for c in range(NCH):
                    e.matmul(ps_tr[:, 0:2], wmb[bi][:, c * P:(c + 1) * P],
                             t_sbf[:, 2 * c:2 * c + 2], start=(c == 0),
                             stop=False)
                    ins = e.matmul(ps_tr[:, 2:4], wmb[bi][:, c * P:(c + 1) * P],
                                   t_slb[:, 2 * c:2 * c + 2], start=False,
                                   stop=False)
                if i in MODF:
                    for c in range(NCH):
                        ins = e.matmul(ps_tr[:, 4:6], wml[:, c * P:(c + 1) * P],
                                       t_sbf[:, 2 * c:2 * c + 2], start=False,
                                       stop=(c == NCH - 1))
                else:
                    ins = e.matmul(ps_tr[:, 2:4], wmb[bi][:, 0:P],
                                   t_slb[:, 0:2], start=False, stop=True) if False else ins
                return ins
            S.c("pe", mod_mm, reads=[f"wmb{bi}", "sbf", "slb"] + extra, writes=["ps_tr"])

            S.c("dve", lambda e: e.tensor_copy(out=t_rs[:, 0:6], in_=ps_tr[:, 0:6]),
                reads=["ps_tr"], writes=["t_rs", "ps_tr"])

            def mod_ev(e, i=i):
                a0 = t_modsh[:, i:i + 1]
                a1 = t_modsh[:, 12 + i:13 + i]
                e.tensor_add(out=a0, in0=t_rs[:, 0:1], in1=t_rs[:, 2:3])
                ins = e.tensor_add(out=a1, in0=t_rs[:, 1:2], in1=t_rs[:, 3:4])
                if i in MODF:
                    e.tensor_add(out=a0, in0=a0, in1=t_rs[:, 4:5])
                    return e.tensor_add(out=a1, in0=a1, in1=t_rs[:, 5:6])
                return ins
            S.c("dve", mod_ev, reads=["t_rs"], writes=["modsh"])

        if dbg:
            dma("sp", dbg_o["d_ct"], t_ct[:], ["ct", "siluf"], ["d_ct"])
            dma("sp", dbg_o["d_siluf"], t_siluf[:], ["siluf", "silu"], ["d_siluf"])
            dma("pool", dbg_o["d_wmb"], wmb[1][:], ["wmb1", "modsh"], ["d_wmb"])
            dma("sp", dbg_o["d_modsh"], t_modsh[:], ["modsh"], ["d_modsh"])
            dma("pool", dbg_o["d_silu"], t_sbf[:], ["sbf", "slb"], ["d_silu"])
        dma("pool", ag1i[:, 0:24], t_modsh[:], ["modsh"], ["ag1i"])
        import os as _os
        if _os.environ.get("KNOCC1"):
            for _r in range(NCORES):
                dma("pool", ag1o[_r * P:(_r + 1) * P, :], ag1i[:, :],
                    ["ag1i"], ["ag1o"])
        else:
            S.op("pool", "cc", lambda e: e.collective_compute(
                "AllGather", ALU.bypass, replica_groups=[list(range(NCORES))],
                ins=[ag1i.opt()], outs=[ag1o.opt()]), reads=["ag1i"], writes=["ag1o"])
            S.op("pool", "cc", lambda e: e.collective_compute(
                "AllGather", ALU.bypass, replica_groups=[list(range(NCORES))],
                ins=[agfi.opt()], outs=[agf1o.opt()]),
                reads=["ag1o"], writes=["ag1f"])
        src_m = ag1o.rearrange("(r p) x -> p r x", p=P)
        dma("pool", t_rs[:, 0:96].rearrange("p (r c) -> p r c", c=12),
            src_m[:, :, 0:12], ["ag1o", "ag1f"], ["mg0", "t_rs"])
        dma("pool", t_mb[:, 0:96].rearrange("p (r c) -> p r c", c=12),
            src_m[:, :, 12:24], ["ag1o", "ag1f"], ["mbv"])

        S.c("dve", lambda e: e.tensor_scalar(out=t_mod[:], in0=t_rs[:, 0:96],
                                             scalar1=t_bmask[:, 0:1], scalar2=None,
                                             op0=ALU.mult),
            reads=["mg0", "bmask"], writes=["mod"])
        S.c("dve", lambda e: e.tensor_scalar(out=t_rs[:, 96:192], in0=t_mb[:, 0:96],
                                             scalar1=t_bmask[:, 1:2], scalar2=None,
                                             op0=ALU.mult),
            reads=["mbv", "bmask"], writes=["t_rs", "mbv"])
        S.c("dve", lambda e: e.tensor_tensor(out=t_mod[:], in0=t_mod[:],
                                             in1=t_rs[:, 96:192], op=ALU.add),
            reads=["mod", "t_rs"], writes=["mod"])
        for base in (32, 80):
            gaap = t_mod[:, base:base + 16]
            S.c("dve", lambda e, g=gaap: e.tensor_scalar(
                out=t_rs[:, 0:16], in0=g, scalar1=0.0, scalar2=None, op0=ALU.is_ge),
                reads=["mod"], writes=["t_rs"])
            S.c("dve", lambda e: e.tensor_scalar(
                out=t_rs[:, 16:32], in0=t_rs[:, 0:16], scalar1=2.0, scalar2=-1.0,
                op0=ALU.mult, op1=ALU.add), reads=["t_rs"], writes=["t_rs"])
            S.c("dve", lambda e, g=gaap: e.tensor_tensor(
                out=t_rs[:, 32:48], in0=g, in1=t_rs[:, 16:32], op=ALU.mult),
                reads=["mod", "t_rs"], writes=["t_rs"])
            S.c("dve", lambda e: e.tensor_scalar(
                out=t_rs[:, 48:64], in0=t_rs[:, 32:48], scalar1=0.25, scalar2=None,
                op0=ALU.max), reads=["t_rs"], writes=["t_rs"])
            S.c("dve", lambda e, g=gaap: e.tensor_tensor(
                out=g, in0=t_rs[:, 48:64], in1=t_rs[:, 16:32], op=ALU.mult),
                reads=["t_rs"], writes=["mod"])
        S.c("dve", lambda e: e.tensor_scalar(out=t_a1[:], in0=t_mod[:, 16:32],
                                             scalar1=1.0, scalar2=None, op0=ALU.add),
            reads=["mod"], writes=["a1"])
        S.c("dve", lambda e: e.tensor_scalar(out=t_a3[:], in0=t_mod[:, 64:80],
                                             scalar1=1.0, scalar2=None, op0=ALU.add),
            reads=["mod"], writes=["a3"])
        if dbg:
            dma("sp", dbg_o["d_mod"], t_mod[:], ["mod"], ["d_mod"])
        mstack.close()

        if KSTOP < 1:
            S.frozen = True
        def stats_pass(srcv, src_tag_fn):
            for c in range(NCH):
                S.c("dve", lambda e, c=c: e.tensor_copy(out=t_xbf[:], in_=srcv[:, c, :]),
                    reads=[src_tag_fn(c)], writes=["xbf"])
                S.c("act", lambda e, c=c: e.activation(out=t_sqs[:], in_=srcv[:, c, :],
                                                       func=AF.Square),
                    reads=[src_tag_fn(c)], writes=["sqs"])

                def mm(e, c=c):
                    for hh in range(2):
                        e.matmul(ps_p[0:1, hh * 512:(hh + 1) * 512], t_ones[:],
                                 t_xbf[:, hh * 512:(hh + 1) * 512],
                                 start=(c == 0), stop=False)
                        ins = e.matmul(ps_p[64:65, hh * 512:(hh + 1) * 512], t_ones[:],
                                       t_sqs[:, hh * 512:(hh + 1) * 512],
                                       start=(c == 0), stop=(c == NCH - 1))
                    return ins
                S.c("pe", mm, reads=["xbf", "sqs", "ones"],
                    writes=["ps_rows", "xbf", "sqs"])

        def ln_rows_and_bcast(mtile, mtag, rtile, rtag):
            def rmath1(e):
                e.tensor_scalar(out=mtile[0:1, 0:T], in0=ps_p[0:1, :], scalar1=INVD,
                                scalar2=None, op0=ALU.mult)
                return e.tensor_scalar(out=rtile[0:1, 0:T], in0=ps_p[64:65, :],
                                       scalar1=INVD, scalar2=None, op0=ALU.mult)
            S.c("dve", rmath1, reads=["ps_rows"], writes=[mtag, rtag, "ps_rows"])

            def rmath2(e):
                m = mtile[0:1, 0:T]
                v = rtile[0:1, 0:T]
                e.tensor_tensor(out=t_rs[0:1, 0:T], in0=m, in1=m, op=ALU.mult)
                return e.tensor_tensor(out=v, in0=v, in1=t_rs[0:1, 0:T], op=ALU.subtract)
            S.c("dve", rmath2, reads=[mtag, rtag], writes=[rtag, "t_rs"])
            S.c("act", lambda e: e.activation(out=t_rs[0:1, 0:T],
                                              in_=rtile[0:1, 0:T],
                                              func=AF.Sqrt, bias=EPS),
                reads=[rtag, "const"], writes=["t_rs"])
            S.c("dve", lambda e: e.reciprocal(out=rtile[0:1, 0:T],
                                              in_=t_rs[0:1, 0:T]),
                reads=["t_rs"], writes=[rtag])
            dma("sp", rowd[0:1, :], mtile[0:1, 0:T], [mtag], ["rowd01"])
            dma("sp", rowd[1:2, :], rtile[0:1, 0:T], [rtag], ["rowd01"])
            dma("sp", mtile[:], rowd[0:1, 0:T].partition_broadcast(P),
                ["rowd01"], [mtag])
            dma("sp", rtile[:], rowd[1:2, 0:T].partition_broadcast(P),
                ["rowd01"], [rtag])

        def col_bcast(srccols, nblk, dst_fn, dsttag, rtag):
            S.c("pe", lambda e: e.transpose(ps_tr[0:nblk, :], srccols, t_ident[:]),
                reads=[rtag, "ident"], writes=["ps_tr"])

            S.c("dve", lambda e: e.tensor_copy(out=t_bchh[:], in_=ps_tr[0:8, :]),
                reads=["ps_tr"], writes=["bch"])
            S.c("dve", lambda e: e.tensor_tensor(out=t_rs[0:8, 0:P],
                                                 in0=ps_tr[0:8, :], in1=t_bchh[:],
                                                 op=ALU.subtract),
                reads=["ps_tr", "bch"], writes=["t_rs", "ps_tr"])
            S.c("dve", lambda e: e.tensor_copy(out=t_bchl[:], in_=t_rs[0:8, 0:P]),
                reads=["t_rs"], writes=["bch"])
            for j in range(nblk):
                def bmm(e, j=j):
                    e.matmul(ps_tr[:, 0:P], t_bchot[:, j * P:(j + 1) * P], t_bchh[:],
                             start=True, stop=False)
                    return e.matmul(ps_tr[:, 0:P], t_bchot[:, j * P:(j + 1) * P],
                                    t_bchl[:], start=False, stop=True)
                S.c("pe", bmm, reads=["bch", "bchot"], writes=["ps_tr"])
                S.c("dve", lambda e, j=j: e.tensor_copy(out=dst_fn(j), in_=ps_tr[:, 0:P]),
                    reads=["ps_tr"], writes=[dsttag, "ps_tr"])

        def tree_to_cols(nblk, srcmax, srcmin, rtags):
            for j in range(nblk):
                S.c("pe", lambda e, j=j: e.transpose(
                    ps_tr[:], srcmax[:, j * P:(j + 1) * P], t_ident[:]),
                    reads=rtags + ["ident"], writes=["ps_tr"])
                S.c("dve", lambda e, j=j: e.tensor_reduce(
                    out=t_colst[:, 8 + j:9 + j], in_=ps_tr[:], axis=AX.X, op=ALU.max),
                    reads=["ps_tr"], writes=["colv", "ps_tr"])
            for j in range(nblk):
                S.c("pe", lambda e, j=j: e.transpose(
                    ps_tr[:], srcmin[:, j * P:(j + 1) * P], t_ident[:]),
                    reads=rtags + ["ident"], writes=["ps_tr"])

                S.c("dve", lambda e: e.tensor_reduce(out=t_rs[:, 0:1], in_=ps_tr[:],
                                                     axis=AX.X, op=ALU.min),
                    reads=["ps_tr"], writes=["t_rs", "ps_tr"])

                def mr(e, j=j):
                    e.tensor_scalar(out=t_rs[:, 0:1], in0=t_rs[:, 0:1], scalar1=-1.0,
                                    scalar2=None, op0=ALU.mult)
                    return e.tensor_tensor(out=t_colst[:, 8 + j:9 + j],
                                           in0=t_colst[:, 8 + j:9 + j],
                                           in1=t_rs[:, 0:1], op=ALU.max)
                S.c("dve", mr, reads=["t_rs"], writes=["colv", "t_rs"])

        def ln_apply_ops(srctag, srcslice, acol, bcol, mtile, mtag, rtile, rtag,
                         outslice, outtags, extra_reads=()):
            S.c("dve", lambda e: e.tensor_tensor(out=t_rs[:], in0=srcslice,
                                                 in1=mtile[:], op=ALU.subtract),
                reads=[srctag, mtag] + list(extra_reads), writes=["t_rs"])
            S.c("dve", lambda e: e.scalar_tensor_tensor(
                out=t_rs[:], in0=t_rs[:], scalar=acol, in1=rtile[:],
                op0=ALU.mult, op1=ALU.mult),
                reads=["t_rs", rtag, "mod", "a1", "a3", "gam"], writes=["t_rs"])
            S.c("dve", lambda e: e.tensor_scalar(out=outslice, in0=t_rs[:],
                                                 scalar1=bcol, scalar2=None,
                                                 op0=ALU.add),
                reads=["t_rs", "mod", "bet"], writes=list(outtags))

        stats_pass(xv, lambda c: "x")
        ln_rows_and_bcast(t_mb, "mbv", t_rb, "rbv")
        for c in range(NCH):
            ln_apply_ops("x", xv[:, c, :], t_a1[:, c:c + 1], t_mod[:, c:c + 1],
                         t_mb, "mbv", t_rb, "rbv", t_rs2[:], ["t_rs2"])
            S.c("dve", lambda e: e.tensor_copy(out=t_xbf[:], in_=t_rs2[:]),
                reads=["t_rs2"], writes=["xbf"])
            S.c("act", lambda e: e.activation(out=t_sqs[:], in_=t_rs2[:], func=AF.Square),
                reads=["t_rs2"], writes=["sqs"])

            def hmm(e, c=c):
                for hh in range(2):
                    e.matmul(ps_p[0:1, hh * 512:(hh + 1) * 512], t_ones[:],
                             t_xbf[:, hh * 512:(hh + 1) * 512],
                             start=(c == 0), stop=False)
                    ins = e.matmul(ps_p[64:65, hh * 512:(hh + 1) * 512], t_ones[:],
                                   t_sqs[:, hh * 512:(hh + 1) * 512],
                                   start=(c == 0), stop=(c == NCH - 1))
                return ins
            S.c("pe", hmm, reads=["xbf", "sqs", "ones"],
                writes=["ps_rows", "xbf", "sqs", "t_rs2"])
        ln_rows_and_bcast(t_s1b, "m2b", t_wb, "r2b")
        for c in range(NCH):
            ln_apply_ops("x", xv[:, c, :], t_a1[:, c:c + 1], t_mod[:, c:c + 1],
                         t_mb, "mbv", t_rb, "rbv", t_rs2[:], ["t_rs2"])
            ln_apply_ops("t_rs2", t_rs2[:], t_gam[:, c:c + 1], t_bet[:, c:c + 1],
                         t_s1b, "m2b", t_wb, "r2b", xnv[:, c, :], [f"xn{c}"])
            S.c("dve", lambda e, c=c: e.tensor_scalar(
                out=t_rs[:], in0=t_rs2[:], scalar1=t_mod[:, 32 + c:33 + c],
                scalar2=None, op0=ALU.mult),
                reads=["t_rs2", "mod"], writes=["t_rs"])
            S.c("dve", lambda e, c=c: e.tensor_tensor(
                out=xv[:, c, :], in0=xv[:, c, :], in1=t_rs[:], op=ALU.add),
                reads=["t_rs", "x"], writes=["x"])
        if dbg:
            dma("pool", dbg_o["d_xn"].rearrange("c p t -> p c t"), xnv,
                [f"xn{c}" for c in range(NCH)], ["d_xn"])
            dma("sp", dbg_o["d_h"].rearrange("c p t -> p c t"), xv,
                ["x"], ["d_h"])

        if KSTOP < 2:
            S.frozen = True
        p2a = contextlib.ExitStack()
        krotb = sb(p2a, "krotb", [P, NCH * T], BF16)
        krv = krotb[:].rearrange("p (c t) -> p c t", t=T)
        p2b = contextlib.ExitStack()
        winb = [sb(p2b, f"winb{i}", [P, NCH * P], BF16) for i in range(2)]
        t_cos = [sb(p2b, f"tcos{i}", [P, T], BF16) for i in range(2)]
        t_sin = [sb(p2b, f"tsin{i}", [P, T], BF16) for i in range(2)]
        t_kc = [sb(p2b, f"tkc{i}", [P, T], F32) for i in range(2)]

        for o in range(OQKV):
            bi = o % 2
            dma("pool", winb[bi][:].rearrange("p (c m) -> p c m", m=P),
                w_in[o].rearrange("c p m -> p c m"), [], [f"winb{bi}"])
            pstag = "ps_a" if o % 2 == 0 else "ps_b"
            pb = ps_a if o % 2 == 0 else ps_b

            def qkv_mm(e, bi=bi, pb=pb):
                for c in range(NCH):
                    for hh in range(2):
                        ins = e.matmul(pb[:, hh * 512:(hh + 1) * 512],
                                       winb[bi][:, c * P:(c + 1) * P],
                                       xnv[:, c, hh * 512:(hh + 1) * 512],
                                       start=(c == 0), stop=(c == NCH - 1))
                return ins
            S.c("pe", qkv_mm, reads=[f"winb{bi}"] + [f"xn{c}" for c in range(NCH)],
                writes=[pstag])

            if o < 16:
                S.c("act", lambda e, o=o, pb=pb: e.activation(
                    out=t_xbf[:], in_=pb[:], func=AF.Identity,
                    bias=t_binc[:, o:o + 1], scale=1.0),
                    reads=[pstag, "binc"], writes=[pstag, "xbf"])
                dma("sp", qd[o], t_xbf[:], ["xbf"], [f"qd{o}"])
            else:
                kvi = o % 2
                cch = (o - 16) % NCH
                isK = o < 32
                S.c("act", lambda e, o=o, pb=pb, kvi=kvi: e.activation(
                    out=t_kc[kvi][:], in_=pb[:], func=AF.Identity,
                    bias=t_binc[:, o:o + 1], scale=1.0),
                    reads=[pstag, "binc"], writes=[f"kc{kvi}", pstag])
                S.c("act", lambda e, kvi=kvi: e.activation(
                    out=t_sqs[:], in_=t_kc[kvi][:], func=AF.Square),
                    reads=[f"kc{kvi}"], writes=["sqs"])

                def nmm(e, o=o):
                    pr = 0 if o < 32 else 64
                    for hh in range(2):
                        ins = e.matmul(ps_p[pr:pr + 1, hh * 512:(hh + 1) * 512],
                                       t_ones[:], t_sqs[:, hh * 512:(hh + 1) * 512],
                                       start=(o in (16, 32)), stop=(o == 47))
                    return ins
                S.c("pe", nmm, reads=["sqs", "ones"], writes=["ps_rows", "sqs"])

                if cch < 8:
                    ci = cch % 2
                    dma("sp", t_cos[ci][:], costab[cch], [], [f"cos{ci}"])
                    dma("sp", t_sin[ci][:], sintab[cch], [], [f"sin{ci}"])

                    mask = [ii + 1 if ii % 2 == 0 else ii - 1 for ii in range(32)]
                    S.c("dve", lambda e, kvi=kvi, mask=mask: e.stream_shuffle(
                        out=t_rs[:], in_=t_kc[kvi][:], mask=mask),
                        reads=[f"kc{kvi}"], writes=["t_rs"])
                    S.c("dve", lambda e, ci=ci: e.tensor_tensor(
                        out=t_rs[:], in0=t_rs[:], in1=t_sin[ci][:], op=ALU.mult),
                        reads=["t_rs", f"sin{ci}"], writes=["t_rs"])
                    S.c("dve", lambda e, kvi=kvi, ci=ci: e.tensor_tensor(
                        out=t_rs2[:], in0=t_kc[kvi][:], in1=t_cos[ci][:], op=ALU.mult),
                        reads=[f"kc{kvi}", f"cos{ci}"], writes=["t_rs2"])
                    if isK:
                        S.c("dve", lambda e, cch=cch: e.tensor_tensor(
                            out=krv[:, cch, :], in0=t_rs[:], in1=t_rs2[:], op=ALU.add),
                            reads=["t_rs", "t_rs2"],
                            writes=[f"kr{cch}", f"kc{kvi}"])
                    else:
                        S.c("dve", lambda e: e.tensor_tensor(
                            out=t_rs[:], in0=t_rs[:], in1=t_rs2[:], op=ALU.add),
                            reads=["t_rs", "t_rs2"], writes=["t_rs"])
                        S.c("dve", lambda e, cch=cch: e.tensor_tensor(
                            out=krv[:, cch, :], in0=t_rs[:], in1=krv[:, cch, :],
                            op=ALU.mult),
                            reads=["t_rs", f"kr{cch}"],
                            writes=[f"kr{cch}", f"kc{kvi}"])
                else:
                    if isK:
                        S.c("dve", lambda e, kvi=kvi, cch=cch: e.tensor_copy(
                            out=krv[:, cch, :], in_=t_kc[kvi][:]),
                            reads=[f"kc{kvi}"], writes=[f"kr{cch}", f"kc{kvi}"])
                    else:
                        S.c("dve", lambda e, kvi=kvi, cch=cch: e.tensor_tensor(
                            out=krv[:, cch, :], in0=t_kc[kvi][:], in1=krv[:, cch, :],
                            op=ALU.mult),
                            reads=[f"kc{kvi}", f"kr{cch}"],
                            writes=[f"kr{cch}", f"kc{kvi}"])

        S.c("dve", lambda e: e.tensor_copy(out=t_wb[0:1, 0:T], in_=ps_p[0:1, :]),
            reads=["ps_rows"], writes=["wb", "r2b"])
        S.c("dve", lambda e: e.tensor_tensor(out=t_wb[0:1, 0:T], in0=t_wb[0:1, 0:T],
                                             in1=ps_p[64:65, :], op=ALU.mult),
            reads=["ps_rows", "wb"], writes=["wb", "ps_rows"])
        S.c("act", lambda e: e.activation(out=t_rs[0:1, 0:T],
                                          in_=t_wb[0:1, 0:T], func=AF.Sqrt),
            reads=["wb"], writes=["t_rs"])
        S.c("dve", lambda e: e.reciprocal(out=t_wb[0:1, 0:T],
                                          in_=t_rs[0:1, 0:T]),
            reads=["t_rs"], writes=["wb"])
        dma("sp", rowd[4:5, :], t_wb[0:1, 0:T], ["wb"], ["rowd4"])
        dma("sp", t_wb[:], rowd[4:5, :].partition_broadcast(P),
            ["rowd4"], ["wb"])

        for c in range(NCH):
            def apart(e, c=c):
                e.tensor_tensor(out=t_rs[:], in0=krv[:, c, :], in1=t_wb[:], op=ALU.mult)
                return e.tensor_reduce(out=t_agp[:, c:c + 1], in_=t_rs[:], axis=AX.X,
                                       op=ALU.add)
            S.c("dve", apart, reads=[f"kr{c}", "wb"], writes=["agp", "t_rs"])
        if dbg:
            dma("pool", dbg_o["d_kv"].rearrange("c p t -> p c t"), krv,
                [f"kr{c}" for c in range(NCH)], ["d_kv"])

        S.fence(["sb1", "m2b"], ["mshfence"])
        for widx in range(3):
            for cs in range(NCH):
                dma("sp", t_s1b[:], msh[widx, cs], ["mshfence", "mshabs"],
                    ["sb1", "m2b"])
                S.c("act", lambda e: e.activation(out=t_rs2[:], in_=t_s1b[:], func=AF.Abs),
                    reads=["sb1"], writes=["mshabs", "t_rs2"])

                S.c("dve", lambda e: e.tensor_reduce(out=t_colst[:, 1:2], in_=t_rs2[:],
                                                     axis=AX.X, op=ALU.add),
                    reads=["mshabs"], writes=["colv"])

                def mred(e, widx=widx, cs=cs):
                    if cs == 0:
                        return e.tensor_copy(out=t_agp[:, 16 + widx:17 + widx],
                                             in_=t_colst[:, 1:2])
                    return e.tensor_tensor(out=t_agp[:, 16 + widx:17 + widx],
                                           in0=t_agp[:, 16 + widx:17 + widx],
                                           in1=t_colst[:, 1:2], op=ALU.add)
                S.c("dve", mred, reads=["colv"], writes=["agp"])

        if dbg:
            dma("sp", dbg_o["d_agp"], t_agp[:], ["agp"], ["d_agp"])
        dma("pool", ag2i[:, 0:20], t_agp[:], ["agp"], ["ag2i"])
        import os as _os
        if _os.environ.get("KNOCC2"):
            for _r in range(NCORES):
                dma("pool", ag2o[_r * P:(_r + 1) * P, :], ag2i[:, :],
                    ["ag2i"], ["ag2o"])
        else:
            S.op("pool", "cc", lambda e: e.collective_compute(
                "AllGather", ALU.bypass, replica_groups=[list(range(NCORES))],
                ins=[ag2i.opt()], outs=[ag2o.opt()]), reads=["ag2i"], writes=["ag2o"])
            S.op("pool", "cc", lambda e: e.collective_compute(
                "AllGather", ALU.bypass, replica_groups=[list(range(NCORES))],
                ins=[agfi.opt()], outs=[agf2o.opt()]),
                reads=["ag2o"], writes=["ag2f"])
        dma("pool", t_gmx[:, 0:160].rearrange("p (r x) -> p r x", x=20),
            ag2o.rearrange("(r p) x -> p r x", p=P), ["ag2o", "ag2f"], ["gmx"])

        blocks = t_gmx[:, 0:160].rearrange("p (r x) -> p r x", x=20)
        for r in range(NCORES):
            if r == 0:
                S.c("dve", lambda e, r=r: e.tensor_scalar(
                    out=t_rs[:, 0:20], in0=blocks[:, r, :],
                    scalar1=t_amask[:, r:r + 1], scalar2=None, op0=ALU.mult),
                    reads=["gmx", "amask"], writes=["t_rs"])
                S.c("dve", lambda e, r=r: e.tensor_copy(out=t_rs[:, 32:52],
                                                        in_=blocks[:, r, :]),
                    reads=["gmx"], writes=["t_rs"])
            else:
                S.c("dve", lambda e, r=r: e.tensor_scalar(
                    out=t_rs[:, 64:84], in0=blocks[:, r, :],
                    scalar1=t_amask[:, r:r + 1], scalar2=None, op0=ALU.mult),
                    reads=["gmx", "amask"], writes=["t_rs"])
                S.c("dve", lambda e: e.tensor_tensor(
                    out=t_rs[:, 0:20], in0=t_rs[:, 0:20], in1=t_rs[:, 64:84],
                    op=ALU.add), reads=["t_rs"], writes=["t_rs"])
                S.c("dve", lambda e, r=r: e.tensor_tensor(
                    out=t_rs[:, 32:52], in0=t_rs[:, 32:52], in1=blocks[:, r, :],
                    op=ALU.add), reads=["t_rs", "gmx"], writes=["t_rs"])
        S.c("dve", lambda e: e.tensor_scalar(out=t_acol[:], in0=t_rs[:, 0:16],
                                             scalar1=0.5, scalar2=0.5,
                                             op0=ALU.mult, op1=ALU.add),
            reads=["t_rs"], writes=["acol"])
        S.c("dve", lambda e: e.tensor_copy(out=t_rs2[:, 0:3], in_=t_rs[:, 48:51]),
            reads=["t_rs"], writes=["t_rs2"])
        if dbg:
            dma("sp", dbg_o["d_acol"], t_acol[:], ["acol"], ["d_acol"])

        S.c("pe", lambda e: e.transpose(ps_tr[0:3, :], t_rs2[:, 0:3], t_ident[:]),
            reads=["t_rs2", "ident"], writes=["ps_tr"])

        S.c("dve", lambda e: e.tensor_copy(out=t_rs[0:3, 0:P], in_=ps_tr[0:3, :]),
            reads=["ps_tr"], writes=["t_rs", "ps_tr"])
        S.c("dve", lambda e: e.tensor_reduce(out=t_colst[0:3, 4:5], in_=t_rs[0:3, 0:P],
                                             axis=AX.X, op=ALU.add),
            reads=["t_rs"], writes=["colv"])
        dma("sp", rowd[5:6, 0:3], t_colst[0:3, 4:5], ["colv"], ["rowd5"])
        dma("sp", t_mrow[0:1, 0:3], rowd[5:6, 0:3], ["rowd5"], ["mrow"])

        S.c("dve", lambda e: e.tensor_scalar(out=t_mrow[0:1, 4:7],
                                             in0=t_mrow[0:1, 0:3],
                                             scalar1=1.0 / (DFF * D), scalar2=EPS,
                                             op0=ALU.mult, op1=ALU.max),
            reads=["mrow"], writes=["mrow"])
        S.c("dve", lambda e: e.reciprocal(out=t_mrow[0:1, 8:11],
                                          in_=t_mrow[0:1, 4:7]),
            reads=["mrow"], writes=["mrow"])
        S.c("dve", lambda e: e.tensor_scalar(out=t_mrow[0:1, 11:12],
                                             in0=t_mrow[0:1, 4:5],
                                             scalar1=1.0 / 128.0, scalar2=None,
                                             op0=ALU.mult),
            reads=["mrow"], writes=["mrow"])
        S.c("dve", lambda e: e.tensor_tensor(out=t_mrow[0:1, 12:13],
                                             in0=t_mrow[0:1, 5:6],
                                             in1=t_mrow[0:1, 6:7], op=ALU.mult),
            reads=["mrow"], writes=["mrow"])
        S.c("dve", lambda e: e.tensor_scalar(out=t_mrow[0:1, 12:13],
                                             in0=t_mrow[0:1, 12:13],
                                             scalar1=CDOWN, scalar2=None,
                                             op0=ALU.mult),
            reads=["mrow"], writes=["mrow"])
        dma("sp", rowd[6:7, 0:16], t_mrow[:], ["mrow"], ["rowd6"])
        dma("sp", t_scol[:, 0:5], rowd[6:7, 8:13].partition_broadcast(P),
            ["rowd6"], ["scol"])
        if dbg:
            dma("sp", dbg_o["d_mrow"], t_mrow[:], ["mrow"], ["d_mrow"])

        if KSTOP < 3:
            S.frozen = True
        yv = xnv
        dma("sp", xnb[:].rearrange("p (c t) -> p c t", t=T),
            qd.rearrange("c p t -> p c t"),
            [f"qd{o}" for o in range(16)], ["y"] + [f"xn{c}" for c in range(NCH)])
        for c in range(NCH):
            S.c("dve", lambda e, c=c: e.tensor_scalar(
                out=yv[:, c, :], in0=yv[:, c, :], scalar1=t_acol[:, c:c + 1],
                scalar2=None, op0=ALU.mult), reads=["acol"], writes=["y"])

        for o in range(NCH):
            bi = o % 2
            dma("pool", winb[bi][:].rearrange("p (c m) -> p c m", m=P),
                w_out[o].rearrange("c p m -> p c m"), [], [f"winb{bi}"])
            pstag = "ps_a" if o % 2 == 0 else "ps_b"
            pb = ps_a if o % 2 == 0 else ps_b

            def wo_mm(e, bi=bi, pb=pb):
                for c in range(NCH):
                    for hh in range(2):
                        ins = e.matmul(pb[:, hh * 512:(hh + 1) * 512],
                                       winb[bi][:, c * P:(c + 1) * P],
                                       yv[:, c, hh * 512:(hh + 1) * 512],
                                       start=(c == 0), stop=(c == NCH - 1))
                return ins
            S.c("pe", wo_mm, reads=[f"winb{bi}", "y"], writes=[pstag])

            S.c("dve", lambda e, o=o, pb=pb: e.tensor_scalar(
                out=t_rs[:], in0=pb[:], scalar1=t_boutc[:, o:o + 1],
                scalar2=t_mod[:, 32 + o:33 + o], op0=ALU.add, op1=ALU.mult),
                reads=[pstag, "boutc", "mod"], writes=["t_rs", pstag])
            S.c("dve", lambda e, o=o: e.tensor_tensor(
                out=xv[:, o, :], in0=xv[:, o, :], in1=t_rs[:], op=ALU.add),
                reads=["t_rs", "x"], writes=["x", f"x1c{o}"])
            dma("sp", x1d[o], xv[:, o, :], [f"x1c{o}"], [f"x1d{o}"])
        if dbg:
            dma("sp", dbg_o["d_x1"].rearrange("c p t -> p c t"), xv,
                [f"x1c{o}" for o in range(NCH)], ["d_x1"])

        stats_pass(xv, lambda c: f"x1c{c}")
        ln_rows_and_bcast(t_mb, "mbv", t_rb, "rbv")

        for c in range(NCH):
            ln_apply_ops(f"x1c{c}", xv[:, c, :], t_a3[:, c:c + 1],
                         t_mod[:, 48 + c:49 + c], t_mb, "mbv", t_rb, "rbv",
                         t_rs[:], ["t_rs", "h2c"])
            if c == 0:
                S.c("dve", lambda e: e.tensor_copy(out=t_rs2[:], in_=t_rs[:]),
                    reads=["h2c"], writes=["t_rs2"])
                S.c("dve", lambda e: e.tensor_copy(out=t_min[:], in_=t_rs[:]),
                    reads=["h2c"], writes=["t_min"])
            else:
                S.c("dve", lambda e: e.tensor_tensor(out=t_rs2[:], in0=t_rs2[:],
                                                     in1=t_rs[:], op=ALU.max),
                    reads=["h2c", "t_rs2"], writes=["t_rs2"])
                S.c("dve", lambda e: e.tensor_tensor(
                    out=t_min[:], in0=t_min[:], in1=t_rs[:], op=ALU.min),
                    reads=["h2c", "t_min"], writes=["t_min"])

        tree_to_cols(8, t_rs2, t_min, ["t_rs2", "t_min"])

        S.c("dve", lambda e: e.tensor_scalar(out=t_colst[:, 8:16],
                                             in0=t_colst[:, 8:16], scalar1=EPS,
                                             scalar2=None, op0=ALU.max),
            reads=["colv"], writes=["colv"])
        S.c("dve", lambda e: e.reciprocal(out=t_colst[:, 0:8], in_=t_colst[:, 8:16]),
            reads=["colv"], writes=["colv"])
        S.c("dve", lambda e: e.tensor_scalar(out=t_colst[:, 0:8], in0=t_colst[:, 0:8],
                                             scalar1=128.0, scalar2=None,
                                             op0=ALU.mult),
            reads=["colv"], writes=["colv"])
        S.c("dve", lambda e: e.tensor_copy(out=t_gmx[:, 160:168],
                                           in_=t_colst[:, 8:16]),
            reads=["colv"], writes=["gmx"])
        if dbg:
            dma("sp", dbg_o["d_m1c"], t_colst[:], ["colv"], ["d_m1c"])

        col_bcast(t_colst[:, 0:8], 8, lambda j: t_s1b[:, j * P:(j + 1) * P],
                  "sb1", "colv")
        if dbg:
            dma("sp", dbg_o["d_s1b"], t_s1b[:], ["sb1"], ["d_s1b"])

        def cg_colmath(e):
            return e.tensor_scalar(out=t_colst[:, 0:8], in0=t_colst[:, 8:16],
                                   scalar1=t_scol[:, 3:4], scalar2=None, op0=ALU.mult)
        S.c("dve", cg_colmath, reads=["colv", "scol"], writes=["colv"])
        col_bcast(t_colst[:, 0:8], 8, lambda j: t_wb[:, j * P:(j + 1) * P],
                  "wb", "colv")

        q1v = yv
        for c in range(NCH):
            ln_apply_ops(f"x1c{c}", xv[:, c, :], t_a3[:, c:c + 1],
                         t_mod[:, 48 + c:49 + c], t_mb, "mbv", t_rb, "rbv",
                         t_rs[:], ["t_rs", "h2c"])
            S.c("dve", lambda e: e.tensor_tensor(out=q1i[:], in0=t_rs[:],
                                                 in1=t_s1b[:], op=ALU.mult),
                reads=["h2c", "sb1"], writes=["q1i"])
            S.c("dve", lambda e, c=c: e.tensor_copy(out=q1v[:, c, :], in_=q1i[:]),
                reads=["q1i", "y"], writes=[f"q1{c}", "y"])
        if dbg:
            dma("pool", dbg_o["d_q1"].rearrange("c p t -> p c t"), q1v,
                [f"q1{c}" for c in range(NCH)], ["d_q1"])

        if KSTOP < 4:
            S.frozen = True
        p2b.close()
        p2a.close()
        xstack.close()
        S.fence(["x"] + [f"kr{c}" for c in range(NCH)] +
                ["cos0", "cos1", "sin0", "sin1", "kc0", "kc1", "xbf", "sqs",
                 "winb0", "winb1"], ["p4fence"])
        p4 = contextlib.ExitStack()
        wgf = [sb(p4, f"wgf{i}", [P, NCH * OSUP], F32) for i in range(2)]
        wuf = [sb(p4, f"wuf{i}", [P, NCH * OSUP], F32) for i in range(2)]
        t_d2 = sb(p4, "t_d2", [P, NCH * OSUP], I8)
        tern_g = sb(p4, "tern_g", [P, NCH * OSUP], BF16)
        tern_u = sb(p4, "tern_u", [P, NCH * OSUP], BF16)
        t_sg = sb(p4, "t_sg", [P, T], F32)
        t_ic = sb(p4, "t_ic", [P, T], BF16)
        t_pc = sb(p4, "t_pc", [P, T], BF16)

        for i in range(NSUP):
            bi = i % 2
            dma("sp", wgf[bi][:].rearrange("p (c m) -> p c m", m=OSUP),
                wg_t[i].rearrange("c p m -> p c m"), ["p4fence"], [f"wgf{bi}"])
            dma("sp", wuf[bi][:].rearrange("p (c m) -> p c m", m=OSUP),
                wu_t[i].rearrange("c p m -> p c m"), ["p4fence"], [f"wuf{bi}"])
            for wf, wtag, tt, ttag, sci in ((wgf[bi], f"wgf{bi}", tern_g, "tg", 0),
                                            (wuf[bi], f"wuf{bi}", tern_u, "tu", 1)):
                S.c("dve", lambda e, wf=wf, sci=sci: e.tensor_scalar(
                    out=t_d2[:], in0=wf[:], scalar1=t_scol[:, sci:sci + 1],
                    scalar2=None, op0=ALU.mult),
                    reads=[wtag, "scol"], writes=["t_d2"])
                S.c("dve", lambda e, tt=tt: e.tensor_scalar(
                    out=tt[:], in0=t_d2[:], scalar1=1.0, scalar2=-1.0,
                    op0=ALU.min, op1=ALU.max),
                    reads=["t_d2"], writes=[ttag, "t_d2"])

            for blk in range(OSUP // P):
                a = i * (OSUP // P) + blk
                gsl = tern_g[:].rearrange("p (c m) -> p c m", m=OSUP)[:, :, blk * P:(blk + 1) * P]
                usl = tern_u[:].rearrange("p (c m) -> p c m", m=OSUP)[:, :, blk * P:(blk + 1) * P]

                def gmm(e, gsl=gsl):
                    for c in range(NCH):
                        for hh in range(2):
                            ins = e.matmul(ps_a[:, hh * 512:(hh + 1) * 512],
                                           gsl[:, c, :],
                                           q1v[:, c, hh * 512:(hh + 1) * 512],
                                           start=(c == 0), stop=(c == NCH - 1))
                    return ins
                S.c("pe", gmm, reads=["tg"] + [f"q1{c}" for c in range(NCH)],
                    writes=["ps_a"])
                S.c("dve", lambda e: e.tensor_tensor(out=t_sg[:], in0=ps_a[:],
                                                     in1=t_wb[:], op=ALU.mult),
                    reads=["ps_a", "wb"], writes=["sg", "ps_a"])
                S.c("act", lambda e: e.activation(out=t_sg[:], in_=t_sg[:],
                                                  func=AF.Silu),
                    reads=["sg"], writes=["sg"])
                if dbg and a == 0:
                    dma("sp", dbg_o["d_inter"], t_sg[:], ["sg"], ["d_inter"])

                def umm(e, usl=usl):
                    for c in range(NCH):
                        for hh in range(2):
                            ins = e.matmul(ps_b[:, hh * 512:(hh + 1) * 512],
                                           usl[:, c, :],
                                           q1v[:, c, hh * 512:(hh + 1) * 512],
                                           start=(c == 0), stop=(c == NCH - 1))
                    return ins
                S.c("pe", umm, reads=["tu"] + [f"q1{c}" for c in range(NCH)],
                    writes=["ps_b"])
                S.c("dve", lambda e: e.tensor_tensor(out=t_ic[:], in0=t_sg[:],
                                                     in1=ps_b[:], op=ALU.mult),
                    reads=["sg", "ps_b"], writes=["ic", "ps_b"])

                def s1mm(e):
                    for hh in range(2):
                        ins = e.matmul(ps_p[:, hh * 512:(hh + 1) * 512], t_h128[:],
                                       t_ic[:, hh * 512:(hh + 1) * 512],
                                       start=True, stop=True)
                    return ins
                S.c("pe", s1mm, reads=["ic", "h128"], writes=["ps_p", "ps_rows"])
                S.c("dve", lambda e: e.tensor_copy(out=t_pc[:], in_=ps_p[:]),
                    reads=["ps_p"], writes=["pc", "ps_p"])
                dma("sp", pdt[:, a, :, :],
                    t_pc[:].rearrange("p (tc t) -> p tc t", t=TC), ["pc"], [f"pd{a}"])

        if KSTOP < 5:
            S.frozen = True
        p4.close()
        ystack.close()
        S.fence(["y", "tg", "tu", "t_d2", "wgf0", "wgf1", "wuf0", "wuf1",
                 "sg", "ic", "pc"] + [f"q1{c}" for c in range(NCH)], ["p5fence"])
        p5a = contextlib.ExitStack()
        q2b = sb(p5a, "q2b", [P, FCH * T], BF16)
        q2v = q2b[:].rearrange("p (k t) -> p k t", t=T)
        p5b = contextlib.ExitStack()
        gtc = sb(p5b, "gtc", [P, FCH * TC], BF16)
        gv = gtc[:].rearrange("p (k t) -> p k t", t=TC)
        prb = [sb(p5b, f"prb{i}", [64, TC], BF16) for i in range(2)]
        q2i8 = sb(p5b, "q2i8", [P, 2 * TC], I8)
        pdall = [f"pd{a}" for a in range(FCH)]

        for tc in range(NTC):
            for jj in range(32):
                for q in range(4):
                    bp = 4 * jj + q
                    bi = bp % 2
                    dma("sp", prb[bi][:], pdt[bp, :, tc, :],
                        pdall + ["p5fence"], [f"prb{bi}"])

                    def s2mm(e, q=q, bi=bi):
                        pbase = 64 * (q % 2)
                        fbase = 256 * (q // 2)
                        return e.matmul(ps_p[pbase:pbase + 64, fbase:fbase + TC],
                                        t_h64[:], prb[bi][:], start=True, stop=True)
                    S.c("pe", s2mm, reads=[f"prb{bi}", "h64"], writes=["ps_p"])

                def s2ev(e, jj=jj):
                    e.copy(out=gv[:, 2 * jj, :], in_=ps_p[:, 0:TC])
                    return e.copy(out=gv[:, 2 * jj + 1, :],
                                  in_=ps_p[:, TC:2 * TC])
                S.c("act", s2ev, reads=["ps_p"], writes=[f"g{jj}", "ps_p"])

            for k in range(FCH):
                jj = k // 2
                if k == 0:
                    S.c("dve", lambda e, k=k, jj=jj: e.tensor_copy(
                        out=t_rs2[:, 0:TC], in_=gv[:, k, :]),
                        reads=[f"g{jj}"], writes=["t_rs2"])
                    S.c("dve", lambda e, k=k, jj=jj: e.tensor_copy(
                        out=t_min[:, 0:TC], in_=gv[:, k, :]),
                        reads=[f"g{jj}"], writes=["t_min"])
                else:
                    S.c("dve", lambda e, k=k, jj=jj: e.tensor_tensor(
                        out=t_rs2[:, 0:TC], in0=t_rs2[:, 0:TC], in1=gv[:, k, :],
                        op=ALU.max), reads=[f"g{jj}", "t_rs2"], writes=["t_rs2"])
                    S.c("dve", lambda e, k=k, jj=jj: e.tensor_tensor(
                        out=t_min[:, 0:TC], in0=t_min[:, 0:TC], in1=gv[:, k, :],
                        op=ALU.min), reads=[f"g{jj}", "t_min"], writes=["t_min"])

            tree_to_cols(2, t_rs2[:, 0:TC], t_min[:, 0:TC], ["t_rs2", "t_min"])

            S.c("dve", lambda e: e.tensor_scalar(out=t_colst[:, 8:10],
                                                 in0=t_colst[:, 8:10], scalar1=EPS,
                                                 scalar2=None, op0=ALU.max),
                reads=["colv"], writes=["colv"])
            S.c("dve", lambda e: e.reciprocal(out=t_colst[:, 0:2],
                                              in_=t_colst[:, 8:10]),
                reads=["colv"], writes=["colv"])
            S.c("dve", lambda e: e.tensor_scalar(out=t_colst[:, 0:2],
                                                 in0=t_colst[:, 0:2], scalar1=128.0,
                                                 scalar2=None, op0=ALU.mult),
                reads=["colv"], writes=["colv"])
            S.c("dve", lambda e, tc=tc: e.tensor_tensor(
                out=t_colst[:, 4:6], in0=t_colst[:, 8:10],
                in1=t_gmx[:, 160 + 2 * tc:162 + 2 * tc], op=ALU.mult),
                reads=["colv", "gmx"], writes=["colv"])
            S.c("dve", lambda e: e.tensor_scalar(out=t_colst[:, 4:6],
                                                 in0=t_colst[:, 4:6],
                                                 scalar1=t_scol[:, 4:5],
                                                 scalar2=None, op0=ALU.mult),
                reads=["colv", "scol"], writes=["colv"])
            S.c("dve", lambda e, tc=tc: e.tensor_copy(
                out=t_gmx[:, 168 + 2 * tc:170 + 2 * tc], in_=t_colst[:, 4:6]),
                reads=["colv"], writes=["gmx"])

            col_bcast(t_colst[:, 0:2], 2,
                      lambda j, tc=tc: t_s1b[:, tc * TC + j * P:tc * TC + (j + 1) * P],
                      "sb1", "colv")

            for jj in range(32):
                def q2a(e, jj=jj, tc=tc):
                    sl = t_s1b[:, tc * TC:(tc + 1) * TC]
                    e.tensor_tensor(out=q2i8[:, 0:TC], in0=gv[:, 2 * jj, :], in1=sl,
                                    op=ALU.mult)
                    return e.tensor_tensor(out=q2i8[:, TC:2 * TC],
                                           in0=gv[:, 2 * jj + 1, :], in1=sl,
                                           op=ALU.mult)
                S.c("dve", q2a, reads=[f"g{jj}", "sb1"], writes=["q2i8"])

                def q2b_(e, jj=jj, tc=tc):
                    e.tensor_copy(out=q2v[:, 2 * jj, tc * TC:(tc + 1) * TC],
                                  in_=q2i8[:, 0:TC])
                    return e.tensor_copy(out=q2v[:, 2 * jj + 1, tc * TC:(tc + 1) * TC],
                                         in_=q2i8[:, TC:2 * TC])
                S.c("dve", q2b_, reads=["q2i8"], writes=[f"q2_{tc}_{jj}", f"g{jj}"])

        if dbg:
            dma("pool", dbg_o["d_q2"], q2b[:, 0:4 * T],
                [f"q2_{tc}_{jj}" for tc in range(NTC) for jj in range(4)], ["d_q2"])

        col_bcast(t_gmx[:, 168:176], 8, lambda j: t_wb[:, j * P:(j + 1) * P],
                  "wb", "gmx")

        if KSTOP < 6:
            S.frozen = True
        S.fence(["t_rs2", "t_min", "sb1"] + [f"g{jj}" for jj in range(32)] +
                [f"s2b{tc}" for tc in range(NTC)], ["p6fence"])
        p5b.close()
        p6 = contextlib.ExitStack()
        wdf = [sb(p6, f"wdf{i}", [P, 16 * P], F32) for i in range(2)]
        t_dd = sb(p6, "t_dd", [P, 16 * P], I8)
        tern_d = sb(p6, "tern_d", [P, 32 * P], BF16)
        t_x1c = sb(p6, "t_x1c", [P, T], F32)
        t_oc = sb(p6, "t_oc", [P, T], F32)
        q2tags = [f"q2_{tc}_{jj}" for tc in range(NTC) for jj in range(32)]

        for o2 in range(NCH):
            for kh in range(2):
                for q in range(2):
                    wi = (o2 * 4 + kh * 2 + q) % 2
                    dma("sp", wdf[wi][:].rearrange("p (c m) -> p c m", m=P),
                        wd_t[o2, kh, 16 * q:16 * (q + 1)].rearrange(
                            "c p m -> p c m"), ["p6fence"], [f"wdf{wi}"])
                    S.c("dve", lambda e, wi=wi: e.tensor_scalar(
                        out=t_dd[:], in0=wdf[wi][:], scalar1=t_scol[:, 2:3],
                        scalar2=None, op0=ALU.mult),
                        reads=[f"wdf{wi}", "scol"], writes=["tdd"])
                    S.c("dve", lambda e, q=q: e.tensor_scalar(
                        out=tern_d[:, q * 16 * P:(q + 1) * 16 * P],
                        in0=t_dd[:], scalar1=1.0, scalar2=-1.0,
                        op0=ALU.min, op1=ALU.max),
                        reads=["tdd"], writes=["ternd", "tdd"])

                def dmm(e, kh=kh):
                    for ck in range(32):
                        kc = kh * 32 + ck
                        for hh in range(2):
                            ins = e.matmul(ps_a[:, hh * 512:(hh + 1) * 512],
                                           tern_d[:, ck * P:(ck + 1) * P],
                                           q2v[:, kc, hh * 512:(hh + 1) * 512],
                                           start=(kh == 0 and ck == 0),
                                           stop=(kh == 1 and ck == 31))
                    return ins
                S.c("pe", dmm, reads=["ternd"] + q2tags, writes=["ps_a"])

            dma("sp", t_x1c[:], x1d[o2], [f"x1d{o2}", "x1used"], ["x1c"])

            def down_ev(e, o2=o2):
                e.scalar_tensor_tensor(out=t_oc[:], in0=ps_a[:],
                                       scalar=t_mod[:, 80 + o2:81 + o2], in1=t_wb[:],
                                       op0=ALU.mult, op1=ALU.mult)
                return e.tensor_tensor(out=t_oc[:], in0=t_oc[:], in1=t_x1c[:],
                                       op=ALU.add)
            S.c("dve", down_ev, reads=["ps_a", "mod", "wb", "x1c"],
                writes=["toc", "ps_a", "x1used"])
            dma("sp", outT[o2], t_oc[:], ["toc"], [f"out{o2}", "toc"])

        S.c("dve", lambda e: e.engine_nop(),
            reads=[f"out{o2}" for o2 in range(NCH)] + dbg_tags, writes=["fin"])

        S.emit(block, sems)
        p6.close()
        p5a.close()

    return nc


def _sylvester(n):
    h = np.array([[1.0]], dtype=np.float32)
    while h.shape[0] < n:
        h = np.block([[h, h], [h, -h]])
    return h.astype(np.float32)


def _prep_inputs(inputs):
    X = np.asarray(inputs["L0_attn_X"])
    W_in = np.asarray(inputs["L0_attn_W_in"])
    b_in = np.asarray(inputs["L0_attn_b_in"])
    W_out = np.asarray(inputs["L0_attn_W_out"])
    b_out = np.asarray(inputs["L0_attn_b_out"])
    gamma = np.asarray(inputs["L0_attn_gamma"])
    beta = np.asarray(inputs["L0_attn_beta"])
    w_gate = np.asarray(inputs["L0_ffn_w_gate"])
    w_up = np.asarray(inputs["L0_ffn_w_up"])
    w_down = np.asarray(inputs["L0_ffn_w_down"])
    c = np.asarray(inputs["L0_c"])
    wmod = np.asarray(inputs["L0_wmod"])
    f32 = np.float32

    w_in_t = np.ascontiguousarray(
        W_in.reshape(NCH, P, OQKV, P).transpose(2, 0, 1, 3)).astype(f32, copy=False)
    w_out_t = np.ascontiguousarray(
        W_out.reshape(NCH, P, NCH, P).transpose(2, 0, 1, 3)).astype(f32, copy=False)
    wgT = np.ascontiguousarray(w_gate.T)
    wuT = np.ascontiguousarray(w_up.T)
    wg_tile = np.ascontiguousarray(wgT.reshape(NCH, P, NSUP, OSUP).transpose(2, 0, 1, 3))
    wu_tile = np.ascontiguousarray(wuT.reshape(NCH, P, NSUP, OSUP).transpose(2, 0, 1, 3))
    wdT = np.ascontiguousarray(w_down.T)
    perm = np.zeros(DFF, dtype=np.int64)
    for k in range(FCH):
        jj, half = k // 2, k % 2
        for p in range(P):
            bprime = 4 * jj + 2 * half + (1 if p >= 64 else 0)
            aprime = p % 64
            perm[k * P + p] = aprime * P + bprime
    wdT_perm = wdT[perm]
    wd_tile = np.ascontiguousarray(
        wdT_perm.reshape(2, 32, P, NCH, P).transpose(3, 0, 1, 2, 4))

    b_in_c = np.ascontiguousarray(b_in.reshape(OQKV, P).T)
    b_out_cc = np.ascontiguousarray(b_out.reshape(NCH, P).T)
    gam_cc = np.ascontiguousarray(gamma.reshape(NCH, P).T)
    bet_cc = np.ascontiguousarray(beta.reshape(NCH, P).T)
    cTt = np.ascontiguousarray(c.T.reshape(NCH, P, 2))

    h128m = _sylvester(128).astype(ml_dtypes.bfloat16)
    h64m = _sylvester(64).astype(ml_dtypes.bfloat16)
    identm = np.eye(P, dtype=f32)
    onesc = np.ones((P, 1), dtype=ml_dtypes.bfloat16)
    constv = np.tile(np.array([[EPS, 1.0, 0.0, 0.5]], dtype=f32), (P, 1))
    bchot = np.zeros((8, 8, P), dtype=ml_dtypes.bfloat16)
    for j in range(8):
        bchot[j, j, :] = 1.0

    rot_dim = D // 2
    inv_freq = (1.0 / (10000.0 ** (np.arange(0, rot_dim, 2, dtype=f32) / f32(rot_dim))))
    in_maps = []
    for core in range(NCORES):
        b = core // 4
        s0 = (core % 4) * T
        pos = np.arange(s0, s0 + T, dtype=f32)
        fr = pos[None, :] * inv_freq[:, None]
        cosf = np.repeat(np.cos(fr), 2, axis=0)
        sinf = np.repeat(np.sin(fr), 2, axis=0)
        sgn = np.where(np.arange(rot_dim) % 2 == 0, -1.0, 1.0).astype(f32)
        sinf = sinf * sgn[:, None]
        costabc = np.ascontiguousarray(cosf.reshape(8, P, T)).astype(ml_dtypes.bfloat16)
        sintabc = np.ascontiguousarray(sinf.reshape(8, P, T)).astype(ml_dtypes.bfloat16)

        xTc = np.ascontiguousarray(X[b].T[:, s0:s0 + T]).reshape(NCH, P, T).astype(f32)
        wmod_s = np.ascontiguousarray(
            wmod[:, core * 1536:(core + 1) * 1536]
            .reshape(NCH, P, 12, P).transpose(2, 0, 1, 3))
        bm = np.tile(np.array([[1.0, 0.0]] if b == 0 else [[0.0, 1.0]], dtype=f32), (P, 1))
        am = np.zeros((P, 8), dtype=f32)
        am[:, 4 * b:4 * b + 4] = 1.0
        mg = np.ascontiguousarray(
            wgT[:, core * 1024:(core + 1) * 1024].reshape(NCH, P, T))
        mu = np.ascontiguousarray(
            wuT[:, core * 1024:(core + 1) * 1024].reshape(NCH, P, T))
        md = np.ascontiguousarray(
            wdT_perm[core * 1024:(core + 1) * 1024].reshape(NCH, P, T))
        mshard = np.stack([mg, mu, md], axis=0)

        in_maps.append({
            "xT": xTc, "w_in": w_in_t, "b_in_c": b_in_c,
            "w_out": w_out_t, "b_out_c": b_out_cc, "gam_c": gam_cc, "bet_c": bet_cc,
            "wg_t": wg_tile, "wu_t": wu_tile, "wd_t": wd_tile,
            "wmod_sh": wmod_s, "cT": cTt, "costab": costabc, "sintab": sintabc,
            "h128": h128m, "h64": h64m, "ident": identm, "onescol": onesc,
            "bchot": bchot, "constv": constv, "bmask": bm, "amask": am,
            "msh": mshard,
        })
    return in_maps


_prog_cache = {}


def kernel(**inputs):
    in_maps = _prep_inputs(inputs)
    if "main" not in _prog_cache:
        _prog_cache["main"] = build_program(dbg=False, serialize=False)
    nc = _prog_cache["main"]
    res = run_bass_kernel_spmd(nc, in_maps, core_ids=list(range(NCORES)))
    out = np.zeros((2, 4096, D), dtype=np.float32)
    for core in range(NCORES):
        b = core // 4
        s0 = (core % 4) * T
        o = np.asarray(res.results[core]["outT"])
        out[b, s0:s0 + T, :] = o.reshape(D, T).T
    return out
